# version 18
# baseline (speedup 1.0000x reference)
import numpy as np
import ml_dtypes
import concourse.bass as bass
import concourse.mybir as mybir
import bass_rust
from concourse import library_config
from concourse.bass_utils import run_bass_kernel_spmd

F32 = mybir.dt.float32
BF16 = mybir.dt.bfloat16
AF = mybir.ActivationFunctionType
ALU = mybir.AluOpType
AX = mybir.AxisListType

D, E, S, I, L, T, P = 1024, 2048, 128, 3072, 4096, 2048, 128
NT = T // P
NKT = L // P
NIC = I // P
LN_EPS = 1e-5
MAGIC = 12582912.0
C7 = float(np.float32(1.0 / 7.0))
C32767 = float(np.float32(1.0 / 32767.0))
I4096 = float(np.float32(1.0 / 4096.0))

DEBUG_TAPS = False
STAGE = "full"
_LVL = {"a": 0, "b1": 1, "bw": 1.3, "bg": 1.6, "bmm": 2, "bchain": 3, "be2": 4, "full": 9}


class TBuf:
    __slots__ = ("producers", "readers")

    def __init__(self):
        self.producers = []
        self.readers = []

    def rdeps(self):
        return list(self.producers)

    def wdeps(self):
        return self.producers + self.readers

    def wrote(self, tok, reset=True):
        if reset:
            self.producers = [tok]
            self.readers = []
        else:
            self.producers.append(tok)

    def read(self, tok):
        self.readers.append(tok)


def mk_bufs(n):
    return [TBuf() for _ in range(n)]


class Em:
    ENGS = ("sy", "sc", "ve", "te", "gp")
    NSLOT = 8

    def __init__(self):
        self.ops = {e: [] for e in self.ENGS}
        self.barriers = []

    def barrier(self):
        snap = {}
        for eng in self.ENGS:
            if self.ops[eng]:
                self.ops[eng][-1]["inc"] = True
                snap[eng] = len(self.ops[eng])
        self.barriers.append(snap)

    def emit(self, eng, fn, deps=(), dma=False, cc=False, tag=""):
        idx = len(self.ops[eng])
        self.ops[eng].append({"fn": fn, "deps": [d for d in deps if d is not None],
                              "dma": dma, "cc": cc, "inc": False, "sem": None,
                              "val": 0, "prewait": None, "tag": tag,
                              "bar": len(self.barriers)})
        return (eng, idx)

    def op(self, eng, fn, rd=(), wr=(), deps=(), dma=False, cc=False, tag="", accum=False):
        dd = list(deps)
        for b in rd:
            dd += b.rdeps()
        for b in wr:
            dd += (b.rdeps() if accum else b.wdeps())
        best = {}
        keep = set()
        for d in dd:
            if d is None:
                continue
            pe, pi = d
            o = self.ops[pe][pi]
            if o["dma"] or o["cc"]:
                keep.add(d)
            elif pe == eng == "te":
                continue
            elif pi > best.get(pe, -1):
                best[pe] = pi
        dd = list(keep) + [(pe, pi) for pe, pi in best.items()]
        tok = self.emit(eng, fn, dd, dma=dma, cc=cc, tag=tag)
        for b in rd:
            b.read(tok)
        for b in wr:
            b.wrote(tok, reset=not accum)
        return tok

    def force_inc(self, tok):
        self.ops[tok[0]][tok[1]]["inc"] = True

    def finalize(self):
        for eng in self.ENGS:
            for o in self.ops[eng]:
                if o["dma"] or o["cc"]:
                    o["inc"] = True
                for (pe, pi) in o["deps"]:
                    self.ops[pe][pi]["inc"] = True
        self.final_vals = {}
        cc_n = 0
        for eng in self.ENGS:
            cum = 0
            dk = 0
            for o in self.ops[eng]:
                if o["cc"]:
                    slot = cc_n % 4
                    o["sem"] = f"cc{slot}"
                    o["val"] = cc_n // 4 + 1
                    if cc_n >= 4:
                        o["prewait"] = (o["sem"], o["val"] - 1)
                    cc_n += 1
                elif o["dma"]:
                    slot = dk % self.NSLOT
                    o["sem"] = f"{eng}_d{slot}"
                    o["val"] = 16 * (dk // self.NSLOT + 1)
                    if dk >= self.NSLOT:
                        o["prewait"] = (o["sem"], o["val"] - 16)
                    dk += 1
                elif o["inc"]:
                    cum += 1
                    o["sem"] = eng
                    o["val"] = cum
                if o["sem"] is not None:
                    self.final_vals[o["sem"]] = o["val"]
        self.bar_waits = []
        for snap in self.barriers:
            w = {}
            for eng, bound in snap.items():
                for o in self.ops[eng][:bound]:
                    if o["sem"] is not None:
                        w[o["sem"]] = max(w.get(o["sem"], 0), o["val"])
            self.bar_waits.append(w)

    def sem_names(self):
        names = list(self.ENGS) + [f"cc{j}" for j in range(4)]
        for eng in self.ENGS:
            for s in range(self.NSLOT):
                names.append(f"{eng}_d{s}")
        return names

    def replay(self, eng_name, eng, sems):
        watermark = {}
        cur_bar = 0
        for o in self.ops[eng_name]:
            need = {}
            if o["bar"] > cur_bar:
                for b in range(cur_bar, o["bar"]):
                    for sname, v in self.bar_waits[b].items():
                        if v > need.get(sname, 0):
                            need[sname] = v
                cur_bar = o["bar"]
            for (pe, pi) in o["deps"]:
                p = self.ops[pe][pi]
                if p["sem"] is None:
                    continue
                if p["val"] > need.get(p["sem"], 0):
                    need[p["sem"]] = p["val"]
            if o["prewait"] is not None:
                sname, v = o["prewait"]
                if v > need.get(sname, 0):
                    need[sname] = v
            for sname, v in sorted(need.items()):
                if v > watermark.get(sname, 0):
                    eng.wait_ge(sems[sname], v)
                    watermark[sname] = v
            inst = o["fn"](eng)
            if o["inc"]:
                assert inst is not None, f"inc op returned None: {o['tag']}"
                inst.then_inc(sems[o["sem"]], 16 if o["dma"] else 1)


def build_program():
    nc = bass.Bass("TRN2", target_bir_lowering=False)
    em = Em()
    dkind = dict(kind="ExternalOutput") if DEBUG_TAPS else {}

    x_in = nc.dram_tensor("x", [L, D], F32, kind="ExternalInput").ap()
    ln_g = nc.dram_tensor("ln_g", [D], F32, kind="ExternalInput").ap()
    ln_b = nc.dram_tensor("ln_b", [D], F32, kind="ExternalInput").ap()
    W_in = nc.dram_tensor("W_in", [D, 2 * E + S], F32, kind="ExternalInput").ap()
    b_in_d = nc.dram_tensor("b_in", [2 * E + S], F32, kind="ExternalInput").ap()
    W_out = nc.dram_tensor("W_out", [E, D], F32, kind="ExternalInput").ap()
    b_out_d = nc.dram_tensor("b_out", [D], F32, kind="ExternalInput").ap()
    g_q = nc.dram_tensor("g_q", [S], F32, kind="ExternalInput").ap()
    be_q = nc.dram_tensor("be_q", [S], F32, kind="ExternalInput").ap()
    g_k = nc.dram_tensor("g_k", [S], F32, kind="ExternalInput").ap()
    be_k = nc.dram_tensor("be_k", [S], F32, kind="ExternalInput").ap()
    wg_sh = nc.dram_tensor("wg_sh", [384, D], F32, kind="ExternalInput").ap()
    wu_sh = nc.dram_tensor("wu_sh", [384, D], F32, kind="ExternalInput").ap()
    wd_sh = nc.dram_tensor("wd_sh", [D, 384], F32, kind="ExternalInput").ap()
    y_out = nc.dram_tensor("y", [T, D], F32, kind="ExternalOutput").ap()

    u_dram = nc.dram_tensor("u_sc", [T, E], BF16, **dkind).ap()
    v_dram = nc.dram_tensor("v_sc", [L, E], BF16, **dkind).ap()
    xin_dram = nc.dram_tensor("xin_sc", [T, D], F32, **dkind).ap()
    y1_dram = nc.dram_tensor("y1_sc", [T, D], F32, **dkind).ap()
    y1n_dram = nc.dram_tensor("y1n_sc", [T, D], F32, **dkind).ap()
    wout_dram = nc.dram_tensor("woutbf_sc", [E, D], BF16).ap()
    SHN = 3 * 131072 * 2 + 8 * 49152
    wsh_in = nc.dram_tensor("wsh_in", [1, SHN], BF16).ap()
    wsh_out = nc.dram_tensor("wsh_out", [8, SHN], BF16).ap()
    VKG = 4 * P * E
    VKT = VKG + P * 512
    vkg_in = [nc.dram_tensor(f"vkg_in{g}", [1, VKT], BF16).ap() for g in range(4)]
    vkg_out = [nc.dram_tensor(f"vkg_out{g}", [2, VKT], BF16).ap() for g in range(4)]
    graw_dram = nc.dram_tensor("graw_sc", [I, T], F32, **dkind).ap()
    uraw_dram = nc.dram_tensor("uraw_sc", [I, T], F32, **dkind).ap()
    gpint_dram = nc.dram_tensor("gpint_sc", [I, T], F32, **dkind).ap()
    oint_dram = nc.dram_tensor("oint_sc", [I, T], F32, **dkind).ap()
    cc_in = {}
    cc_out = {}

    d_u = mk_bufs(NT)
    d_v = mk_bufs(NKT)
    d_xin = mk_bufs(NT)
    d_y1 = mk_bufs(NT)
    d_y1n = mk_bufs(NT)
    d_wout = mk_bufs(16)
    d_wsh = TBuf()
    b_ag = TBuf()
    d_vk = [TBuf() for _ in range(4)]
    b_vkag = [TBuf() for _ in range(4)]
    d_graw = mk_bufs(NIC)
    d_uraw = mk_bufs(NIC)
    d_gpint = mk_bufs(NIC)
    d_oint = mk_bufs(NIC)
    d_cc = mk_bufs(4)

    from contextlib import ExitStack
    st = ExitStack()
    b_y1nmx = TBuf()

    def sb(name, shape, dt):
        return st.enter_context(nc.sbuf_tensor(name, shape, dt))

    consts = sb("consts", [P, 4], F32)
    ident = sb("ident", [P, P], BF16)
    ones1 = sb("ones1", [1, P], BF16)
    gb_cols = sb("gb_cols", [P, 5], F32)
    lnT = sb("lnT", [P, 16], F32)
    mxsl = sb("mxsl", [P, 320], F32)
    gsc = sb("gsc", [P, 16], F32)
    bstat = sb("bstat", [P, 16], F32)
    bout_hl = sb("bout_hl", [1, 2 * D], BF16)
    atn = ExitStack()
    k_fm = atn.enter_context(nc.sbuf_tensor("k_fm", [P, L], BF16))
    q_fm = atn.enter_context(nc.sbuf_tensor("q_fm", [P, T], BF16))

    CONST_EPS = consts[:, 0:1]

    nc.const_aps.register(F32, LN_EPS, consts[:, 0:1])
    nc.const_aps.register(F32, MAGIC, consts[:, 1:2])
    nc.const_aps.register(F32, 65536.0, consts[:, 2:3])
    nc.const_aps.register(F32, I4096, consts[:, 3:4])

    b_consts = TBuf()
    b_ident = TBuf()
    b_ones = TBuf()
    b_gb = TBuf()
    b_lnT = TBuf()
    b_bout = TBuf()
    b_bstage = TBuf()
    b_bstat = TBuf()
    b_kfm = mk_bufs(NKT)
    b_qfm = mk_bufs(NT)

    em.op("gp", lambda g: g.memset(consts[:, 0:1], LN_EPS), wr=[b_consts], accum=True, tag="c0")
    em.op("gp", lambda g: g.memset(consts[:, 1:2], MAGIC), wr=[b_consts], accum=True, tag="c1")
    em.op("gp", lambda g: g.memset(consts[:, 2:3], 65536.0), wr=[b_consts], accum=True, tag="c2")
    em.op("gp", lambda g: g.memset(consts[:, 3:4], I4096), wr=[b_consts], accum=True, tag="c3")
    em.op("gp", lambda g: g.memset(ident[:], 0.0), wr=[b_ident], tag="id0")
    em.op("gp", lambda g: g.affine_select(out=ident[:], in_=ident[:], compare_op=ALU.not_equal,
                                          fill=1.0, base=0, pattern=[[-1, P]], channel_multiplier=1),
          rd=[b_ident], wr=[b_ident], tag="id1")
    em.op("gp", lambda g: g.memset(ones1[:], 1.0), wr=[b_ones], tag="ones")

    for j, src in enumerate([g_q, be_q, g_k, be_k]):
        em.op("sy", lambda s, j=j, src=src: s.dma_start(out=gb_cols[:, j:j + 1], in_=src[:, None]),
              wr=[b_gb], dma=True, accum=True, tag=f"gb{j}")
    em.op("sy", lambda s: s.dma_start(out=gb_cols[:, 4:5], in_=b_in_d[2 * E:2 * E + S, None]),
          wr=[b_gb], dma=True, accum=True, tag="bz")
    em.op("sy", lambda s: s.dma_start(out=lnT[:, 0:8], in_=ln_g.rearrange("(c p) -> p c", p=P),
                                        allow_slow_non_contiguous=True),
          wr=[b_lnT], dma=True, accum=True, tag="lng")
    em.op("sy", lambda s: s.dma_start(out=lnT[:, 8:16], in_=ln_b.rearrange("(c p) -> p c", p=P),
                                        allow_slow_non_contiguous=True),
          wr=[b_lnT], dma=True, accum=True, tag="lnb")
    sc_slot = [0]
    stats_holder = [None]

    def layernorm(x_ap, b_x, out_ap, b_out_t, tag):
        stats = stats_holder[0]
        s0 = sc_slot[0]
        sc_slot[0] += 18
        assert sc_slot[0] <= stats.shape[1]
        bns = stats[:, s0:s0 + 12]
        mv = stats[:, s0 + 12:s0 + 14]
        sd = stats[:, s0 + 14:s0 + 15]
        rstd = stats[:, s0 + 15:s0 + 16]
        nmr = stats[:, s0 + 16:s0 + 17]
        b_bns, b_mv, b_sd, b_rstd, b_nmr = TBuf(), TBuf(), TBuf(), TBuf(), TBuf()
        em.op("ve", lambda v: v.bn_stats(bns[:, 0:6], x_ap[:, 0:512]),
              rd=[b_x], wr=[b_bns], accum=True, tag=tag + "b0")
        em.op("ve", lambda v: v.bn_stats(bns[:, 6:12], x_ap[:, 512:1024]),
              rd=[b_x], wr=[b_bns], accum=True, tag=tag + "b1")
        em.op("ve", lambda v: v.bn_aggr(mv, bns), rd=[b_bns], wr=[b_mv], tag=tag + "ag")
        em.op("sc", lambda s: s.activation(sd, mv[:, 1:2], AF.Sqrt, bias=LN_EPS),
              rd=[b_mv], wr=[b_sd], deps=b_consts.rdeps(), tag=tag + "sd")
        em.op("ve", lambda v: v.reciprocal(rstd, sd), rd=[b_sd], wr=[b_rstd], tag=tag + "rs")
        em.op("ve", lambda v: v.scalar_tensor_tensor(nmr, mv[:, 0:1], -1.0, rstd,
                                                     op0=ALU.mult, op1=ALU.mult),
              rd=[b_mv, b_rstd], wr=[b_nmr], tag=tag + "nm")
        em.op("sc", lambda s: s.activation(out_ap, x_ap, AF.Identity, bias=nmr, scale=rstd),
              rd=[b_x, b_nmr, b_rstd], wr=[b_out_t], tag=tag + "ap")

    def layernorm12(x_ap, b_x, xin_ap, b_xin, na_ap, b_na, own, tag):
        stats = stats_holder[0]
        s0 = sc_slot[0]
        sc_slot[0] += 28
        assert sc_slot[0] <= stats.shape[1]
        bns = stats[:, s0:s0 + 12]
        mv = stats[:, s0 + 12:s0 + 14]
        ap2 = stats[:, s0 + 14:s0 + 16]
        sd2 = stats[:, s0 + 16:s0 + 18]
        rr2 = stats[:, s0 + 18:s0 + 20]
        nbs = stats[:, s0 + 20:s0 + 21]
        nb1 = stats[:, s0 + 21:s0 + 22]
        rv = stats[:, s0 + 22:s0 + 23]
        v2 = stats[:, s0 + 23:s0 + 24]
        b_st, b_mv, b_ap2, b_sd2, b_rr2, b_nb = TBuf(), TBuf(), TBuf(), TBuf(), TBuf(), TBuf()
        b_nb1 = TBuf()
        em.op("ve", lambda v: v.bn_stats(bns[:, 0:6], x_ap[:, 0:512]),
              rd=[b_x], wr=[b_st], accum=True, tag=tag + "b0")
        em.op("ve", lambda v: v.bn_stats(bns[:, 6:12], x_ap[:, 512:1024]),
              rd=[b_x], wr=[b_st], accum=True, tag=tag + "b1")
        em.op("ve", lambda v: v.bn_aggr(mv, bns), rd=[b_st], wr=[b_mv], tag=tag + "ag")
        em.op("ve", lambda v: v.tensor_scalar(ap2[:, 0:1], mv[:, 1:2], 1.0, LN_EPS,
                                              op0=ALU.mult, op1=ALU.add),
              rd=[b_mv], wr=[b_ap2], accum=True, tag=tag + "a1")
        em.op("ve", lambda v: v.reciprocal(rv, ap2[:, 0:1]), rd=[b_ap2], wr=[b_nb],
              accum=True, tag=tag + "rv")
        em.op("ve", lambda v: v.tensor_scalar(v2, mv[:, 1:2], rv, None, op0=ALU.mult),
              rd=[b_mv, b_nb], wr=[b_nb], accum=True, tag=tag + "v2")
        em.op("ve", lambda v: v.scalar_tensor_tensor(ap2[:, 1:2], v2, LN_EPS, ap2[:, 0:1],
                                                     op0=ALU.add, op1=ALU.mult),
              rd=[b_nb, b_ap2], wr=[b_ap2], accum=True, tag=tag + "pr")
        em.op("sc", lambda s: s.activation(sd2, ap2, AF.Sqrt),
              rd=[b_ap2], wr=[b_sd2], tag=tag + "sd")
        em.op("ve", lambda v: v.reciprocal(rr2, sd2), rd=[b_sd2], wr=[b_rr2], tag=tag + "rr")
        em.op("ve", lambda v: v.scalar_tensor_tensor(nbs, mv[:, 0:1], -1.0, rr2[:, 1:2],
                                                     op0=ALU.mult, op1=ALU.mult),
              rd=[b_mv, b_rr2], wr=[b_nb1], accum=True, tag=tag + "nb")
        em.op("sc", lambda s: s.activation(na_ap, x_ap, AF.Identity, bias=nbs,
                                           scale=rr2[:, 1:2]),
              rd=[b_x, b_nb1, b_rr2], wr=[b_na], tag=tag + "na")
        if own:
            em.op("ve", lambda v: v.scalar_tensor_tensor(nb1, mv[:, 0:1], -1.0, rr2[:, 0:1],
                                                         op0=ALU.mult, op1=ALU.mult),
                  rd=[b_mv, b_rr2], wr=[b_nb1], accum=True, tag=tag + "n1")
            em.op("sc", lambda s: s.activation(xin_ap, x_ap, AF.Identity, bias=nb1,
                                               scale=rr2[:, 0:1]),
                  rd=[b_x, b_nb1, b_rr2], wr=[b_xin], tag=tag + "x1")

    def phase_a1():
        ph = ExitStack()
        def _A(n, s, d):
            return ph.enter_context(nc.sbuf_tensor(n, s, d))
        def _Q(n, s, d):
            return ph.enter_context(nc.psum_tensor(n, s, d))
        tr_ps = _Q("tr_ps", [P, 1024], BF16)
        mm_ps0 = _Q("mm_ps0", [P, 512], F32)
        mm_ps1 = _Q("mm_ps1", [P, 512], F32)
        mm_ps2 = _Q("mm_ps2", [P, 512], F32)
        mm_ps3 = _Q("mm_ps3", [P, 512], F32)
        mm_ps4 = _Q("mm_ps4", [P, 512], F32)
        Wi_bf = _A("Wi_bf", [P, 8 * 4224], BF16)
        Wi_st = _A("Wi_st", [P, 4224], F32)
        binrow = _A("binrow", [1, 2 * 4224], BF16)
        xtile = _A("xtile", [P, 3 * D], F32)
        xintile = _A("xintile", [P, 3 * D], F32)
        na2 = _A("na2", [P, 4 * D], BF16)
        xnT = _A("xnT", [P, 4 * D], BF16)
        ztile = _A("ztile", [P, 2 * S], F32)
        uvout = _A("uvout", [P, 2 * E], BF16)
        w_st = _A("w_st", [P, 2 * D], F32)
        w_bfs = _A("w_bfs", [P, D], BF16)
        lp_in = _A("lp_in", [P, 2 * D], F32)
        lp_t = _A("lp_t", [P, D], F32)
        lp_t2 = _A("lp_t2", [P, D], F32)
        lp_q = _A("lp_q", [P, 2 * D], BF16)
        lp_st = _A("lp_st", [P, 96], F32)
        lnTb = _A("lnTb", [P, 16], BF16)
        stats_a1 = _A("stats_a1", [P, 1184], F32)

        stats_holder[0] = stats_a1
        sc_slot[0] = 0
        mm_ps = [mm_ps0, mm_ps1, mm_ps2, mm_ps3, mm_ps4]
        b_trps = TBuf()
        b_mmps = mk_bufs(5)
        b_Wi = mk_bufs(8)
        b_Wist = TBuf()
        b_binrow = TBuf()
        b_xt = mk_bufs(4)
        b_xint = mk_bufs(4)
        b_na2 = mk_bufs(4)
        b_xnT = mk_bufs(4)
        b_z = TBuf()
        b_z2 = TBuf()
        b_uv = mk_bufs(4)
        b_wst = mk_bufs(2)
        b_wbf = TBuf()
        b_lpin = mk_bufs(2)
        b_lpt = TBuf()
        b_lpt2 = TBuf()
        b_lpq = mk_bufs(2)
        b_lpst = TBuf()

        for d in range(8):
            em.op("sy", lambda s, d=d: s.dma_start(out=Wi_st[:], in_=W_in[d * P:(d + 1) * P, :]),
                  wr=[b_Wist], dma=True, tag=f"wi_ld{d}")
            em.op("ve", lambda v, d=d: v.tensor_copy(Wi_bf[:, d * 4224:(d + 1) * 4224], Wi_st[:]),
                  rd=[b_Wist], wr=[b_Wi[d]], tag=f"wi_c{d}")
        b_lnTb = TBuf()
        for d in range(8):
            em.op("ve", lambda v, d=d: v.tensor_copy(lnTb[:, 2 * d:2 * d + 1],
                                                     lnT[:, 8 + d:9 + d]),
                  rd=[b_lnT], wr=[b_lnTb], accum=True, tag=f"bh{d}")
            em.op("ve", lambda v, d=d: v.tensor_tensor(lnTb[:, 2 * d + 1:2 * d + 2],
                                                       lnT[:, 8 + d:9 + d],
                                                       lnTb[:, 2 * d:2 * d + 1], ALU.subtract),
                  rd=[b_lnT, b_lnTb], wr=[b_lnTb], accum=True, tag=f"bl{d}")
        bstage = Wi_st[0:1, :]
        em.op("sy", lambda s: s.dma_start(out=bstage[0:1, 0:4224], in_=b_in_d[None, :]),
              wr=[b_Wist, b_bstage], dma=True, tag="bi_ld")
        b_bsum = b_lpt
        for j in range(8):
            bank, bbank = mm_ps[j % 2], b_mmps[j % 2]
            for d in range(8):
                for hl in range(2):
                    em.op("te", lambda t, d=d, j=j, hl=hl, bank=bank: t.matmul(
                        bank[0:1, 0:512], lnTb[:, 2 * d + hl:2 * d + hl + 1],
                        Wi_bf[:, d * 4224 + j * 512: d * 4224 + j * 512 + 512],
                        start=(d == 0 and hl == 0), stop=(d == 7 and hl == 1)),
                        rd=[b_lnTb, b_Wi[d]], wr=[bbank], accum=not (d == 0 and hl == 0),
                        tag=f"bw{j}_{d}_{hl}")
            em.op("ve", lambda v, j=j, bank=bank: v.tensor_tensor(
                lp_t[0:1, 0:512], bank[0:1, 0:512],
                bstage[0:1, j * 512:(j + 1) * 512], ALU.add),
                rd=[bbank, b_bstage], wr=[b_bsum], tag=f"bws{j}")
            em.op("ve", lambda v, j=j: v.tensor_copy(
                binrow[0:1, j * 512:(j + 1) * 512], lp_t[0:1, 0:512]),
                rd=[b_bsum], wr=[b_binrow], accum=True, tag=f"bwh{j}")
            em.op("ve", lambda v, j=j: v.tensor_tensor(
                binrow[0:1, 4224 + j * 512: 4224 + (j + 1) * 512], lp_t[0:1, 0:512],
                binrow[0:1, j * 512:(j + 1) * 512], ALU.subtract),
                rd=[b_bsum, b_binrow], wr=[b_binrow], accum=True, tag=f"bwl{j}")
        zbank, b_zbank = mm_ps[2], b_mmps[2]
        for d in range(8):
            for hl in range(2):
                em.op("te", lambda t, d=d, hl=hl: t.matmul(
                    zbank[:, 0:1], Wi_bf[:, d * 4224 + 2 * E: d * 4224 + 2 * E + S],
                    lnTb[:, 2 * d + hl:2 * d + hl + 1],
                    start=(d == 0 and hl == 0), stop=(d == 7 and hl == 1)),
                    rd=[b_lnTb, b_Wi[d]], wr=[b_zbank], accum=not (d == 0 and hl == 0),
                    tag=f"bwz{d}_{hl}")
        em.op("ve", lambda v: v.tensor_tensor(gb_cols[:, 4:5], gb_cols[:, 4:5],
                                              zbank[:, 0:1], ALU.add),
              rd=[b_gb, b_zbank], wr=[b_gb], accum=True, tag="bwzadd")
        for d in range(8):
            em.op("ve", lambda v, d=d: v.tensor_scalar(
                Wi_bf[:, d * 4224:(d + 1) * 4224], Wi_bf[:, d * 4224:(d + 1) * 4224],
                lnT[:, d:d + 1], None, op0=ALU.mult),
                rd=[b_lnT], wr=[b_Wi[d]], tag=f"wg{d}")

        em.op("sy", lambda s: s.dma_start(out=bstage[0:1, 0:D], in_=b_out_d[None, :]),
              wr=[b_Wist, b_bstage], dma=True, tag="bo_ld")
        em.op("ve", lambda v: v.tensor_copy(bout_hl[0:1, 0:D], bstage[0:1, 0:D]),
              rd=[b_bstage], wr=[b_bout], accum=True, tag="bo_hi")
        em.op("ve", lambda v: v.tensor_tensor(bout_hl[0:1, D:2 * D], bstage[0:1, 0:D],
                                              bout_hl[0:1, 0:D], ALU.subtract),
              rd=[b_bstage, b_bout], wr=[b_bout], accum=True, tag="bo_lo")

        lp_units = ([("g", k) for k in range(3)] + [("u", k) for k in range(3)]
                    + [("d", k) for k in range(8)])
        lp_pos = [0]

        def emit_lpbq_unit():
            if lp_pos[0] >= len(lp_units):
                return
            kind, key = lp_units[lp_pos[0]]
            pi = lp_pos[0] % 2
            lp_pos[0] += 1
            if kind == "g":
                src_ap = wg_sh[key * P:(key + 1) * P, :]
                off, W = key * 131072, D
            elif kind == "u":
                src_ap = wu_sh[key * P:(key + 1) * P, :]
                off, W = 3 * 131072 + key * 131072, D
            else:
                src_ap = wd_sh[key * P:(key + 1) * P, :]
                off, W = 6 * 131072 + key * 49152, 384
            nb = W // 32
            dst_ap = wsh_in[0, off:off + P * W].rearrange("(p c) -> p c", p=P)
            inb = lp_in[:, pi * D:pi * D + W]
            qb = lp_q[:, pi * D:pi * D + W]
            bin_, bq_ = b_lpin[pi], b_lpq[pi]
            amax, ssc, rsc = lp_st[:, 0:nb], lp_st[:, 32:32 + nb], lp_st[:, 64:64 + nb]
            em.op("sy", lambda s: s.dma_start(out=inb, in_=src_ap), wr=[bin_], dma=True, tag="lpl")
            inv = inb.rearrange("p (b k) -> p b k", k=32)
            em.op("ve", lambda v: v.tensor_reduce(amax, inv, axis=AX.X, op=ALU.max,
                                                  apply_absolute_value=True),
                  rd=[bin_], wr=[b_lpst], tag="lpm")
            em.op("ve", lambda v: v.tensor_scalar(ssc, amax, C7, 1e-12, op0=ALU.mult, op1=ALU.max),
                  rd=[b_lpst], wr=[b_lpst], accum=True, tag="lps")
            em.op("ve", lambda v: v.reciprocal(rsc, ssc), rd=[b_lpst], wr=[b_lpst],
                  accum=True, tag="lpr")
            sb3 = ssc.unsqueeze(2).broadcast_to([P, nb, 32])
            rb3 = rsc.unsqueeze(2).broadcast_to([P, nb, 32])
            tv = lp_t[:, 0:W].rearrange("p (b k) -> p b k", k=32)
            t2v = lp_t2[:, 0:W].rearrange("p (b k) -> p b k", k=32)
            qv = qb.rearrange("p (b k) -> p b k", k=32)
            em.op("ve", lambda v: v.tensor_tensor(tv, inv, rb3, ALU.mult),
                  rd=[bin_, b_lpst], wr=[b_lpt], tag="lpt")
            em.op("ve", lambda v: v.tensor_scalar(lp_t2[:, 0:W], lp_t[:, 0:W], MAGIC, MAGIC,
                                                  op0=ALU.add, op1=ALU.subtract),
                  rd=[b_lpt], wr=[b_lpt2], tag="lpn")
            em.op("ve", lambda v: v.tensor_tensor(qv, t2v, sb3, ALU.mult),
                  rd=[b_lpt2, b_lpst], wr=[bq_], tag="lpq")
            em.op("gp", lambda g: g.dma_start(out=dst_ap, in_=qb), rd=[bq_], wr=[d_wsh],
                  dma=True, accum=True, tag="lpw")


        wo_pos = [0]

        def emit_wout_unit():
            if wo_pos[0] >= 16:
                return
            e = wo_pos[0]
            pi = e % 2
            wo_pos[0] += 1
            stg = w_st[:, pi * D:(pi + 1) * D]
            em.op("sy", lambda s, e=e, stg=stg: s.dma_start(out=stg, in_=W_out[e * P:(e + 1) * P, :]),
                  wr=[b_wst[pi]], dma=True, tag="wol")
            em.op("ve", lambda v, stg=stg: v.tensor_copy(w_bfs[:], stg), rd=[b_wst[pi]],
                  wr=[b_wbf], tag="woc")
            em.op("gp", lambda g, e=e: g.dma_start(out=wout_dram[e * P:(e + 1) * P, :],
                                                   in_=w_bfs[:]),
                  rd=[b_wbf], wr=[d_wout[e]], dma=True, tag="wos")

        def a1_front(abs_t, own):
            p3 = abs_t % 3
            pi = abs_t % 4
            xa = xtile[:, p3 * D:(p3 + 1) * D]
            xia = xintile[:, p3 * D:(p3 + 1) * D]
            naa = na2[:, pi * D:(pi + 1) * D]
            xnTa = xnT[:, pi * D:(pi + 1) * D]
            bx, bxi, bna, bxnT = b_xt[p3], b_xint[p3], b_na2[pi], b_xnT[pi]

            em.op("sy", lambda s: s.dma_start(out=xa, in_=x_in[abs_t * P:(abs_t + 1) * P, :]),
                  wr=[bx], dma=True, tag=f"xl{abs_t}")
            layernorm12(xa, bx, xia, bxi, naa, bna, own, f"l12_{abs_t}")
            if own:
                em.op("gp", lambda g: g.dma_start(out=xin_dram[abs_t * P:(abs_t + 1) * P, :],
                                                  in_=xia),
                      rd=[bxi], wr=[d_xin[abs_t]], dma=True, tag=f"xis{abs_t}")
            for dd in range(8):
                em.op("te", lambda t, dd=dd, naa=naa: t.transpose(
                    tr_ps[:, dd * P:(dd + 1) * P], naa[:, dd * P:(dd + 1) * P], ident[:]),
                    rd=[bna, b_ident], wr=[b_trps], accum=(dd > 0), tag=f"tr{abs_t}_{dd}")
            em.op("ve", lambda v, xnTa=xnTa: v.tensor_copy(xnTa, tr_ps[:]),
                  rd=[b_trps], wr=[bxnT], tag=f"te{abs_t}")

        def a1_back(abs_t, own):
            pi = abs_t % 4
            pu = abs_t % 2
            xnTa = xnT[:, pi * D:(pi + 1) * D]
            bxnT = b_xnT[pi]

            zp, bzp = mm_ps[4], b_mmps[4]
            for dd in range(8):
                em.op("te", lambda t, dd=dd: t.matmul(
                    zp[:, 0:S], Wi_bf[:, dd * 4224 + 2 * E: dd * 4224 + 2 * E + S],
                    xnTa[:, dd * P:(dd + 1) * P], start=(dd == 0), stop=(dd == 7)),
                    rd=[b_Wi[dd], bxnT], wr=[bzp], accum=(dd > 0), tag=f"zm{abs_t}_{dd}")
            zt = ztile[:, (abs_t % 2) * S:(abs_t % 2 + 1) * S]
            bz2 = b_z if abs_t % 2 == 0 else b_z2
            em.op("sc", lambda s, zt=zt: s.activation(zt, zp[:, 0:S], AF.Silu,
                                                      bias=gb_cols[:, 4:5]),
                  rd=[bzp, b_gb], wr=[bz2], tag=f"ze{abs_t}")
            em.op("sc", lambda s, zt=zt: s.activation(k_fm[:, abs_t * P:(abs_t + 1) * P], zt,
                                                      AF.Identity, bias=gb_cols[:, 3:4],
                                                      scale=gb_cols[:, 2:3]),
                  rd=[bz2, b_gb], wr=[b_kfm[abs_t]], tag=f"kv{abs_t}")
            if own:
                em.op("sc", lambda s, zt=zt: s.activation(q_fm[:, abs_t * P:(abs_t + 1) * P], zt,
                                                          AF.Identity, bias=gb_cols[:, 1:2],
                                                          scale=gb_cols[:, 0:1]),
                      rd=[bz2, b_gb], wr=[b_qfm[abs_t]], tag=f"qv{abs_t}")

            vdst = vkg_in[abs_t // 4][0, (abs_t % 4) * P * E:(abs_t % 4 + 1) * P * E]\
                .rearrange("(p c) -> p c", p=P)
            specs = [("v", E, vdst, d_vk[abs_t // 4], abs_t * P)]
            if own:
                specs.append(("u", 0, u_dram, d_u[abs_t], abs_t * P))
            for which, col0, dstd, dbuf, row in specs:
                uvb = b_uv[pu]
                for dd in range(8):
                    for eb in range(4):
                        bank, bbank = mm_ps[eb], b_mmps[eb]
                        c0 = col0 + eb * 512
                        em.op("te", lambda t, dd=dd, c0=c0, bank=bank: t.matmul(
                            bank[:], xnTa[:, dd * P:(dd + 1) * P],
                            Wi_bf[:, dd * 4224 + c0: dd * 4224 + c0 + 512],
                            start=(dd == 0), stop=False),
                            rd=[b_Wi[dd], bxnT], wr=[bbank], accum=(dd > 0),
                            tag=f"{which}m{abs_t}_{eb}_{dd}")
                for eb in range(4):
                    bank, bbank = mm_ps[eb], b_mmps[eb]
                    c0 = col0 + eb * 512
                    em.op("te", lambda t, c0=c0, bank=bank: t.matmul(
                        bank[:], ones1[:], binrow[0:1, c0:c0 + 512], start=False, stop=False),
                        rd=[b_ones, b_binrow], wr=[bbank], accum=True, tag=f"{which}h{abs_t}_{eb}")
                    em.op("te", lambda t, c0=c0, bank=bank: t.matmul(
                        bank[:], ones1[:], binrow[0:1, 4224 + c0:4224 + c0 + 512],
                        start=False, stop=True),
                        rd=[b_ones, b_binrow], wr=[bbank], accum=True, tag=f"{which}l{abs_t}_{eb}")
                    em.op("sc", lambda s, eb=eb, bank=bank: s.activation(
                        uvout[:, pu * E + eb * 512: pu * E + eb * 512 + 512], bank[:], AF.Silu),
                        rd=[bbank], wr=[uvb], accum=(eb > 0), tag=f"{which}e{abs_t}_{eb}")
                em.op("gp", lambda g, dstd=dstd, row=row, which=which: g.dma_start(
                    out=(dstd[row:row + P, :] if which == "u" else dstd),
                    in_=uvout[:, pu * E:(pu + 1) * E]),
                    rd=[uvb], wr=[dbuf], dma=True, accum=(which == "v"),
                    tag=f"{which}s{abs_t}")

        a1_front(0, True)
        a1_front(1, True)
        for t_ in range(NT):
            if t_ + 2 < NT:
                a1_front(t_ + 2, True)
            a1_back(t_, True)
            emit_lpbq_unit()
            emit_wout_unit()
            if t_ % 4 == 3:
                gidx = t_ // 4
                em.op("gp", lambda g, gidx=gidx: g.dma_start(
                    out=vkg_in[gidx][0, VKG:VKT].rearrange("(p c) -> p c", p=P),
                    in_=k_fm[:, gidx * 512:(gidx + 1) * 512]),
                    rd=b_kfm[4 * gidx:4 * gidx + 4], wr=[d_vk[gidx]], dma=True,
                    accum=True, tag=f"kship{gidx}")
                em.op("gp", lambda g, gidx=gidx: g.collective_compute(
                    "AllGather", ALU.bypass,
                    replica_groups=[[0, 1], [2, 3], [4, 5], [6, 7]],
                    ins=[vkg_in[gidx].opt()], outs=[vkg_out[gidx].opt()]),
                    rd=[d_vk[gidx]], wr=[b_vkag[gidx]], cc=True, tag=f"vk_ag{gidx}")
        while lp_pos[0] < len(lp_units):
            emit_lpbq_unit()
        while wo_pos[0] < 16:
            emit_wout_unit()
        em.op("gp", lambda g: g.collective_compute(
            "AllGather", ALU.bypass, replica_groups=[list(range(8))],
            ins=[wsh_in.opt()], outs=[wsh_out.opt()]),
            rd=[d_wsh], wr=[b_ag], cc=True, tag="wsh_ag")
        ph.close()

    y1n_mx = mxsl[:, 0:16]
    phase_a1()
    em.barrier()

    def phase_a2():
        ph = ExitStack()
        def _A(n, s, d):
            return ph.enter_context(nc.sbuf_tensor(n, s, d))
        def _Q(n, s, d):
            return ph.enter_context(nc.psum_tensor(n, s, d))
        sc_ps0 = _Q("sc_ps0", [P, 512], F32)
        sc_ps1 = _Q("sc_ps1", [P, 512], F32)
        av_ps0 = _Q("av_ps0", [P, 512], F32)
        av_ps1 = _Q("av_ps1", [P, 512], F32)
        av_ps2 = _Q("av_ps2", [P, 512], F32)
        av_ps3 = _Q("av_ps3", [P, 512], F32)
        pj_ps0 = _Q("pj_ps0", [P, 512], F32)
        pj_ps1 = _Q("pj_ps1", [P, 512], F32)
        v_sb = _A("v_sb", [P, NKT * E], BF16)
        scT = _A("scT", [P, L], BF16)
        scf = _A("scf", [P, 1024], F32)
        u_sb = _A("u_sb", [P, 2 * E], BF16)
        m_sb = _A("m_sb", [P, E], BF16)
        mT_sb = _A("mT_sb", [P, E], BF16)
        wo_sb = _A("wo_sb", [P, 8 * D], BF16)
        ep1 = _A("ep1", [P, D], F32)
        ep2 = _A("ep2", [P, D], F32)
        ep3 = _A("ep3", [P, D], F32)
        stats_a2 = _A("stats_a2", [P, 600], F32)

        stats_holder[0] = stats_a2
        sc_slot[0] = 0
        av_ps = [av_ps0, av_ps1, av_ps2, av_ps3]
        sc_ps = [sc_ps0, sc_ps1]
        pj_ps = [pj_ps0, pj_ps1]
        mt_ps = [av_ps0[:].bitcast(BF16), av_ps2[:].bitcast(BF16)]
        b_scps = mk_bufs(2)
        b_scf2 = mk_bufs(2)
        b_avps = mk_bufs(4)
        b_pjps = mk_bufs(2)
        b_vsb = mk_bufs(NKT)
        b_scT = mk_bufs(8)
        b_scf = TBuf()
        b_usb = mk_bufs(2)
        b_msb = mk_bufs(4)
        b_mTsb = mk_bufs(2)
        b_wosb = mk_bufs(2)
        b_ep1, b_ep2, b_ep3 = TBuf(), TBuf(), TBuf()

        for half in range(2):
            for g_ in range(4):
                em.op("sy", lambda s, half=half, g_=g_: s.dma_start(
                    out=k_fm[:, half * T + g_ * 512: half * T + (g_ + 1) * 512],
                    in_=vkg_out[g_][half, VKG:VKT].rearrange("(p c) -> p c", p=P)),
                    rd=[b_vkag[g_]], wr=b_kfm[half * NT + 4 * g_: half * NT + 4 * g_ + 4],
                    dma=True, tag=f"krl{half}_{g_}")
        for g_ in range(4):
            for half in range(2):
                for i_ in range(4):
                    kt = half * NT + g_ * 4 + i_
                    em.op("sy", lambda s, kt=kt, g_=g_, i_=i_, half=half: s.dma_start(
                        out=v_sb[:, kt * E:(kt + 1) * E],
                        in_=vkg_out[g_][half, i_ * P * E:(i_ + 1) * P * E]
                        .rearrange("(p c) -> p c", p=P)),
                        rd=[b_vkag[g_]], wr=[b_vsb[kt]], dma=True, tag=f"vr{kt}")

        for q in range(NT):
            for g in range(4):
                em.op("sy", lambda s, g=g: s.dma_start(
                    out=wo_sb[:, (g % 2) * 4 * D:((g % 2) + 1) * 4 * D].rearrange(
                        "p (c h) -> p c h", c=4),
                    in_=wout_dram[g * 512:(g + 1) * 512, :].rearrange("(c p) h -> p c h", p=P)),
                    rd=[d_wout[4 * g], d_wout[4 * g + 1], d_wout[4 * g + 2], d_wout[4 * g + 3]],
                    wr=[b_wosb[g % 2]], dma=True, tag=f"wr{q}_{g}")
            upi = q % 2
            em.op("sy", lambda s, q=q, upi=upi: s.dma_start(
                out=u_sb[:, upi * E:(upi + 1) * E], in_=u_dram[q * P:(q + 1) * P, :]),
                rd=[d_u[q]], wr=[b_usb[upi]], dma=True, tag=f"ur{q}")
            for g4 in range(8):
                bank, bbank = sc_ps[g4 % 2], b_scps[g4 % 2]
                for j in range(4):
                    kt = g4 * 4 + j
                    em.op("te", lambda t, kt=kt, j=j, q=q, bank=bank: t.matmul(
                        bank[:, j * P:(j + 1) * P], k_fm[:, kt * P:(kt + 1) * P],
                        q_fm[:, q * P:(q + 1) * P], start=True, stop=True),
                        rd=[b_kfm[kt], b_qfm[q]], wr=[bbank], accum=(j > 0), tag=f"sc{q}_{kt}")
                em.op("sc", lambda s, bank=bank, g4=g4: s.activation(
                    scf[:, (g4 % 2) * 512:(g4 % 2 + 1) * 512], bank[:], AF.Relu, scale=I4096),
                    rd=[bbank], wr=[b_scf2[g4 % 2]], deps=b_consts.rdeps(),
                    tag=f"se{q}_{g4}")
                em.op("ve", lambda v, g4=g4: v.tensor_tensor(
                    scT[:, g4 * 512:(g4 + 1) * 512], scf[:, (g4 % 2) * 512:(g4 % 2 + 1) * 512],
                    scf[:, (g4 % 2) * 512:(g4 % 2 + 1) * 512], ALU.mult),
                    rd=[b_scf2[g4 % 2]], wr=[b_scT[g4]], tag=f"sq{q}_{g4}")
            for kt in range(NKT):
                for eb in range(4):
                    em.op("te", lambda t, kt=kt, eb=eb: t.matmul(
                        av_ps[eb][:], scT[:, kt * P:(kt + 1) * P],
                        v_sb[:, kt * E + eb * 512: kt * E + eb * 512 + 512],
                        start=(kt == 0), stop=(kt == NKT - 1)),
                        rd=[b_scT[kt // 4], b_vsb[kt]], wr=[b_avps[eb]], accum=(kt > 0),
                        tag=f"av{q}_{eb}_{kt}")
            for eb in range(4):
                em.op("ve", lambda v, eb=eb, upi=upi: v.tensor_tensor(
                    m_sb[:, eb * 512:(eb + 1) * 512], av_ps[eb][:],
                    u_sb[:, upi * E + eb * 512: upi * E + eb * 512 + 512], ALU.mult),
                    rd=[b_avps[eb], b_usb[upi]], wr=[b_msb[eb]], tag=f"m{q}_{eb}")
            for g in range(2):
                mtb = mt_ps[g]
                wrb = [b_avps[2 * g], b_avps[2 * g + 1]]
                for j in range(8):
                    e = g * 8 + j
                    em.op("te", lambda t, e=e, j=j, mtb=mtb: t.transpose(
                        mtb[:, j * P:(j + 1) * P], m_sb[:, e * P:(e + 1) * P], ident[:]),
                        rd=[b_msb[e // 4], b_ident], wr=wrb, accum=(j > 0), tag=f"mt{q}_{e}")
                em.op("sc", lambda s, g=g, mtb=mtb: s.copy(mT_sb[:, g * 1024:(g + 1) * 1024],
                                                           mtb[:]),
                      rd=wrb, wr=[b_mTsb[g]], tag=f"me{q}_{g}")
            for half in range(2):
                em.op("te", lambda t, half=half: t.matmul(
                    pj_ps[half][:], ones1[:], bout_hl[0:1, half * 512:(half + 1) * 512],
                    start=True, stop=False),
                    rd=[b_ones, b_bout], wr=[b_pjps[half]], tag=f"ph{q}_{half}")
                em.op("te", lambda t, half=half: t.matmul(
                    pj_ps[half][:], ones1[:], bout_hl[0:1, D + half * 512: D + half * 512 + 512],
                    start=False, stop=False),
                    rd=[b_ones, b_bout], wr=[b_pjps[half]], accum=True, tag=f"pl{q}_{half}")
            for e in range(16):
                for half in range(2):
                    em.op("te", lambda t, e=e, half=half: t.matmul(
                        pj_ps[half][:], mT_sb[:, e * P:(e + 1) * P],
                        wo_sb[:, (e % 8) * D + half * 512: (e % 8) * D + half * 512 + 512],
                        start=False, stop=(e == 15)),
                        rd=[b_mTsb[e // 8], b_wosb[(e // 4) % 2]], wr=[b_pjps[half]], accum=True,
                        tag=f"pj{q}_{e}_{half}")
            em.op("sy", lambda s, q=q: s.dma_start(out=ep1[:], in_=xin_dram[q * P:(q + 1) * P, :]),
                  rd=[d_xin[q]], wr=[b_ep1], dma=True, tag=f"xir{q}")
            em.op("sy", lambda s, q=q: s.dma_start(out=ep2[:], in_=x_in[q * P:(q + 1) * P, :]),
                  wr=[b_ep2], dma=True, tag=f"xr{q}")
            for half in range(2):
                em.op("ve", lambda v, half=half: v.tensor_tensor(
                    ep1[:, half * 512:(half + 1) * 512], pj_ps[half][:],
                    ep1[:, half * 512:(half + 1) * 512], ALU.add),
                    rd=[b_pjps[half], b_ep1], wr=[b_ep1], accum=True, tag=f"ao{q}_{half}")
            layernorm(ep1[:], b_ep1, ep3[:], b_ep3, f"l3_{q}")
            em.op("ve", lambda v: v.tensor_tensor(ep2[:], ep2[:], ep3[:], ALU.add),
                  rd=[b_ep2, b_ep3], wr=[b_ep2], tag=f"y1_{q}")
            em.op("gp", lambda g, q=q: g.dma_start(out=y1_dram[q * P:(q + 1) * P, :], in_=ep2[:]),
                  rd=[b_ep2], wr=[d_y1[q]], dma=True, tag=f"y1s{q}")
            layernorm(ep2[:], b_ep2, ep3[:], b_ep3, f"l4_{q}")
            em.op("gp", lambda g, q=q: g.dma_start(out=y1n_dram[q * P:(q + 1) * P, :], in_=ep3[:]),
                  rd=[b_ep3], wr=[d_y1n[q]], dma=True, tag=f"y1ns{q}")
            em.op("ve", lambda v, q=q: v.tensor_reduce(y1n_mx[:, q:q + 1], ep3[:], axis=AX.X,
                                                       op=ALU.max, apply_absolute_value=True),
                  rd=[b_ep3], wr=[b_y1nmx], accum=True, tag=f"ymx{q}")
        ph.close()

    phase_a2()
    atn.close()
    em.barrier()

    def dump_y1():
        for t_ in range(NT):
            tok = em.op("gp", lambda g, t_=t_: g.dma_start(
                out=y_out[t_ * P:(t_ + 1) * P, :], in_=y1_dram[t_ * P:(t_ + 1) * P, :]),
                rd=[d_y1[t_]], dma=True, tag=f"dump_y{t_}")
            em.force_inc(tok)

    if STAGE == "a":
        dump_y1()

    barrier_n = [0]
    b_pmst = TBuf()

    def all_reduce_max(col_ap, b_col, nvals, tag):
        bi = barrier_n[0]
        barrier_n[0] += 1
        cc_in[bi] = nc.dram_tensor(f"cc_in{bi}", [1, nvals], F32)
        cc_out[bi] = nc.dram_tensor(f"cc_out{bi}", [1, nvals], F32)
        pm = mxsl[:, 300 + 2 * bi:302 + 2 * bi]
        em.op("gp", lambda g: g.tensor_reduce(pm[0:1, 0:nvals], col_ap, axis=AX.C, op=ALU.max),
              rd=[b_col], wr=[b_pmst], accum=True, tag=tag + "p")
        em.op("gp", lambda g: g.dma_start(out=cc_in[bi].ap()[0:1, 0:nvals], in_=pm[0:1, 0:nvals]),
              rd=[b_pmst], wr=[d_cc[bi]], dma=True, tag=tag + "w")
        em.op("gp", lambda g: g.collective_compute(
            "AllReduce", ALU.max, replica_groups=[list(range(8))],
            ins=[cc_in[bi].ap().opt()], outs=[cc_out[bi].ap().opt()]),
            rd=[d_cc[bi]], wr=[d_cc[bi]], accum=True, cc=True, tag=tag + "c")
        gcol = gsc[:, 4 * bi:4 * bi + nvals]
        b_g = TBuf()
        em.op("sy", lambda s: s.dma_start(out=gcol,
                                          in_=cc_out[bi].ap()[0, 0:nvals].partition_broadcast(P)),
              rd=[d_cc[bi]], wr=[b_g], dma=True, tag=tag + "b")
        return gcol, b_g

    def phase_b():
        ph = ExitStack()
        def _A(n, s, d):
            return ph.enter_context(nc.sbuf_tensor(n, s, d))
        def _Q(n, s, d):
            return ph.enter_context(nc.psum_tensor(n, s, d))
        xq_ps = _Q("xq_ps", [P, 1024], BF16)
        fm_ps = [_Q(f"fm_ps{i}", [P, 512], F32) for i in range(6)]
        b_xqps = TBuf()
        b_fmps = mk_bufs(6)

        ph_x = ExitStack()
        if True:
            xqT = ph_x.enter_context(nc.sbuf_tensor("xqT", [P, 8 * T], BF16))
            b_xqT = mk_bufs(8)

            wstg = ph_x.enter_context(nc.sbuf_tensor("wstg", [P, 2 * D], BF16))
            mrow = ph_x.enter_context(nc.sbuf_tensor("mrow", [P, 2 * E], F32))
            ytile = ph_x.enter_context(nc.sbuf_tensor("ytile", [P, 2 * D], F32))
            yq = ph_x.enter_context(nc.sbuf_tensor("yq", [P, D], BF16))
            ph2 = ExitStack()
            if True:
                wgT = ph2.enter_context(nc.sbuf_tensor("wgT", [P, 8 * I], BF16))
                b_wgT = mk_bufs(8)
                b_wstg2 = mk_bufs(2)
                b_mrow = mk_bufs(2)

                def wprep_chunk(base, wT_sb, b_wT, ci, name):
                    wp = ci % 2
                    wsa = wstg[:, wp * D:(wp + 1) * D]
                    off = base + (ci // 3) * 0
                    j, k = ci // 3, ci % 3
                    src_ap = wsh_out[j, base + k * 131072: base + (k + 1) * 131072]\
                        .rearrange("(p c) -> p c", p=P)
                    em.op("sy", lambda s, wsa=wsa, src_ap=src_ap: s.dma_start(
                        out=wsa, in_=src_ap),
                        rd=[b_ag], wr=[b_wstg2[wp]], dma=True, tag=f"{name}wl{ci}")
                    for dd in range(8):
                        em.op("te", lambda t, dd=dd, wsa=wsa: t.transpose(
                            xq_ps[:, dd * P:(dd + 1) * P], wsa[:, dd * P:(dd + 1) * P], ident[:]),
                            rd=[b_wstg2[wp], b_ident], wr=[b_xqps], accum=(dd > 0),
                            tag=f"{name}wt{ci}_{dd}")
                    wv3 = wT_sb.rearrange("p (d i) -> p d i", d=8)[:, :, ci * P:(ci + 1) * P]
                    em.op("ve", lambda v, wv3=wv3: v.tensor_copy(
                        wv3, xq_ps[:].rearrange("p (d i) -> p d i", d=8)),
                        rd=[b_xqps], wr=b_wT, accum=True, tag=f"{name}we{ci}")

                def mm1_ic(wT_sb, b_wT, dst_dram, d_dst, mx0, b_mx, bank_off, name, ic):
                    pi = ic % 2
                    rowa = mrow[:, pi * E:(pi + 1) * E]
                    bk0 = (ic % 3) * 2
                    for dd in range(8):
                        for tb in range(4):
                            bank = fm_ps[(bk0 + tb) % 6]
                            bbank = b_fmps[(bk0 + tb) % 6]
                            em.op("te", lambda t, dd=dd, ic=ic, tb=tb, bank=bank: t.matmul(
                                bank[:], wT_sb[:, dd * I + ic * P: dd * I + (ic + 1) * P],
                                xqT[:, dd * T + tb * 512: dd * T + tb * 512 + 512],
                                start=(dd == 0), stop=(dd == 7)),
                                rd=[b_wT[dd], b_xqT[dd]], wr=[bbank], accum=(dd > 0),
                                tag=f"{name}m{ic}_{tb}_{dd}")
                    for tb in range(4):
                        bank = fm_ps[(bk0 + tb) % 6]
                        bbank = b_fmps[(bk0 + tb) % 6]
                        em.op("sc", lambda s, tb=tb, bank=bank, rowa=rowa: s.copy(
                            rowa[:, tb * 512:(tb + 1) * 512], bank[:]),
                            rd=[bbank], wr=[b_mrow[pi]], accum=(tb > 0), tag=f"{name}e{ic}_{tb}")
                        em.op("ve", lambda v, tb=tb, ic=ic, rowa=rowa: v.tensor_reduce(
                            mx0[:, ic * 4 + tb:ic * 4 + tb + 1],
                            rowa[:, tb * 512:(tb + 1) * 512], axis=AX.X,
                            op=ALU.max, apply_absolute_value=True),
                            rd=[b_mrow[pi]], wr=[b_mx], accum=True, tag=f"{name}x{ic}_{tb}")
                    em.op("gp", lambda g, ic=ic, rowa=rowa: g.dma_start(
                        out=dst_dram[ic * P:(ic + 1) * P, :], in_=rowa),
                        rd=[b_mrow[pi]], wr=[d_dst[ic]], dma=True, tag=f"{name}s{ic}")

                for ci in range(8):
                    wprep_chunk(0, wgT, b_wgT, ci, "g")
                ymx_col = mxsl[:, 290:291]
                b_ymxc = TBuf()
                em.op("ve", lambda v: v.tensor_reduce(ymx_col, y1n_mx, axis=AX.X, op=ALU.max),
                      rd=[b_y1nmx], wr=[b_ymxc], tag="ymxr")
                g1col, b_g1 = all_reduce_max(ymx_col, b_ymxc, 1, "b1")
                s1, r1 = bstat[:, 0:1], bstat[:, 1:2]
                em.op("ve", lambda v: v.tensor_scalar(s1, g1col, C32767, 1e-12,
                                                      op0=ALU.mult, op1=ALU.max),
                      rd=[b_g1], wr=[b_bstat], accum=True, tag="s1")
                em.op("ve", lambda v: v.reciprocal(r1, s1), rd=[b_bstat], wr=[b_bstat],
                      accum=True, tag="r1")

                if True:
                    b_yt = mk_bufs(2)
                    b_yq = TBuf()
                    for t_ in range(NT):
                        if t_ + 8 < NIC:
                            wprep_chunk(0, wgT, b_wgT, t_ + 8, "g")
                        pi = t_ % 2
                        ya = ytile[:, pi * D:(pi + 1) * D]
                        em.op("sy", lambda s, t_=t_, ya=ya: s.dma_start(
                            out=ya, in_=y1n_dram[t_ * P:(t_ + 1) * P, :]),
                            rd=[d_y1n[t_]], wr=[b_yt[pi]], dma=True, tag=f"ynr{t_}")
                        em.op("ve", lambda v, ya=ya: v.tensor_scalar(ya, ya, r1, MAGIC,
                                                                     op0=ALU.mult, op1=ALU.add),
                              rd=[b_yt[pi], b_bstat], wr=[b_yt[pi]], tag=f"xq1_{t_}")
                        em.op("ve", lambda v, ya=ya: v.tensor_scalar(yq[:], ya, MAGIC, s1,
                                                                     op0=ALU.subtract, op1=ALU.mult),
                              rd=[b_yt[pi], b_bstat], wr=[b_yq], tag=f"xq2_{t_}")
                        for dd in range(8):
                            em.op("te", lambda t, dd=dd: t.transpose(
                                xq_ps[:, dd * P:(dd + 1) * P], yq[:, dd * P:(dd + 1) * P], ident[:]),
                                rd=[b_yq, b_ident], wr=[b_xqps], accum=(dd > 0), tag=f"xt{t_}_{dd}")
                        for dd in range(8):
                            em.op("sc", lambda s, dd=dd, t_=t_: s.copy(
                                xqT[:, dd * T + t_ * P: dd * T + (t_ + 1) * P],
                                xq_ps[:, dd * P:(dd + 1) * P]),
                                rd=[b_xqps], wr=[b_xqT[dd]], accum=True, tag=f"xe{t_}_{dd}")

                if _LVL[STAGE] <= 1:
                    dump_y1()
                    ph_x.close()
                    ph.close()
                    return
                if _LVL[STAGE] <= 1.3:
                    dump_y1()
                    ph2.close()
                    ph_x.close()
                    ph.close()
                    return
                gate_mx = mxsl[:, 16:112]
                b_gmx = TBuf()
                for ic in range(NIC):
                    mm1_ic(wgT, b_wgT, graw_dram, d_graw, gate_mx, b_gmx, 0, "g", ic)
                if _LVL[STAGE] <= 1.6:
                    dump_y1()
                    ph2.close()
                    ph_x.close()
                    ph.close()
                    return
                gmax_l = mxsl[:, 291:292]
                b_gmaxl = TBuf()
                em.op("ve", lambda v: v.tensor_reduce(gmax_l, gate_mx, axis=AX.X, op=ALU.max),
                      rd=[b_gmx], wr=[b_gmaxl], tag="gmr")
                ggcol, b_gg = all_reduce_max(gmax_l, b_gmaxl, 1, "b2")
                s_g, r_g = bstat[:, 2:3], bstat[:, 3:4]
                em.op("ve", lambda v: v.tensor_scalar(s_g, ggcol, C32767, 1e-12,
                                                      op0=ALU.mult, op1=ALU.max),
                      rd=[b_gg], wr=[b_bstat], accum=True, tag="sg")
                em.op("ve", lambda v: v.reciprocal(r_g, s_g), rd=[b_bstat], wr=[b_bstat],
                      accum=True, tag="rg")
                nmsg = bstat[:, 11:12]
                em.op("ve", lambda v: v.tensor_scalar(nmsg, s_g, -MAGIC, None, op0=ALU.mult),
                      rd=[b_bstat], wr=[b_bstat], accum=True, tag="nmsg")
                ph2.close()
                em.barrier()
                if _LVL[STAGE] <= 2:
                    dump_y1()

                if _LVL[STAGE] <= 2:
                    ph_x.close()
                    ph.close()
                    return
                ph2b = ExitStack()
                wuT = ph2b.enter_context(nc.sbuf_tensor("wuT", [P, 8 * I], BF16))
                cg_in = ph2b.enter_context(nc.sbuf_tensor("cg_in", [P, 2 * E], F32))
                cg_t = ph2b.enter_context(nc.sbuf_tensor("cg_t", [P, E], F32))
                cg_g1 = ph2b.enter_context(nc.sbuf_tensor("cg_g1", [P, E], F32))
                cg_sq = ph2b.enter_context(nc.sbuf_tensor("cg_sq", [P, E], F32))
                cg_out = ph2b.enter_context(nc.sbuf_tensor("cg_out", [P, 2 * E], F32))
                cg_sig = cg_t
                b_wuT = mk_bufs(8)
                b_cgin = mk_bufs(2)
                b_cgt = TBuf()
                b_cgg1, b_cgsq = TBuf(), TBuf()
                b_cgsig = b_cgt
                b_cgout = mk_bufs(2)

                for ci in range(NIC):
                    wprep_chunk(3 * 131072, wuT, b_wuT, ci, "u")

                up_mx = mxsl[:, 112:208]
                b_umx = TBuf()
                gpmx_slots = mxsl[:, 208:232]
                b_gpsl = TBuf()

                def chain_ic(ic):
                    pi = ic % 2
                    ina = cg_in[:, pi * E:(pi + 1) * E]
                    outa = cg_out[:, pi * E:(pi + 1) * E]
                    em.op("sy", lambda s, ic=ic, ina=ina: s.dma_start(
                        out=ina, in_=graw_dram[ic * P:(ic + 1) * P, :]),
                        rd=[d_graw[ic]], wr=[b_cgin[pi]], dma=True, tag=f"cl{ic}")
                    em.op("sc", lambda s, ina=ina: s.activation(cg_t[:], ina, AF.Identity,
                                                                scale=r_g, bias=MAGIC),
                          rd=[b_cgin[pi], b_bstat], wr=[b_cgt],
                          deps=b_consts.rdeps(), tag=f"ct{ic}")
                    em.op("sc", lambda s: s.activation(cg_g1[:], cg_t[:], AF.Identity,
                                                       scale=s_g, bias=nmsg),
                          rd=[b_cgt, b_bstat], wr=[b_cgg1], tag=f"cg{ic}")
                    em.op("sc", lambda s: s.activation(cg_sig[:], cg_g1[:], AF.Sigmoid),
                          rd=[b_cgg1], wr=[b_cgsig], tag=f"cs{ic}")
                    em.op("sc", lambda s: s.activation(cg_sq[:], cg_sig[:], AF.Identity,
                                                       bias=MAGIC, scale=65536.0),
                          rd=[b_cgsig], wr=[b_cgsq], deps=b_consts.rdeps(), tag=f"c2{ic}")
                    em.op("ve", lambda v: v.tensor_scalar(cg_sq[:], cg_sq[:], MAGIC, 65535.0,
                                                          op0=ALU.subtract, op1=ALU.min),
                          rd=[b_cgsq], wr=[b_cgsq], tag=f"cm{ic}")
                    em.op("ve", lambda v, outa=outa: v.tensor_tensor(outa, cg_g1[:], cg_sq[:],
                                                                     ALU.mult),
                          rd=[b_cgg1, b_cgsq], wr=[b_cgout[pi]], tag=f"cp{ic}")
                    em.op("ve", lambda v, ic=ic, outa=outa: v.tensor_reduce(
                        gpmx_slots[:, ic:ic + 1], outa, axis=AX.X, op=ALU.max,
                        apply_absolute_value=True),
                        rd=[b_cgout[pi]], wr=[b_gpsl], accum=True, tag=f"cx{ic}")
                    em.op("gp", lambda g, ic=ic, outa=outa: g.dma_start(
                        out=gpint_dram[ic * P:(ic + 1) * P, :], in_=outa),
                        rd=[b_cgout[pi]], wr=[d_gpint[ic]], dma=True, tag=f"cw{ic}")

                for ic in range(NIC):
                    mm1_ic(wuT, b_wuT, uraw_dram, d_uraw, up_mx, b_umx, 2, "u", ic)
                    chain_ic(ic)

                gp_mx = mxsl[:, 292:293]
                b_gpmx = TBuf()
                em.op("ve", lambda v: v.tensor_reduce(gp_mx, gpmx_slots, axis=AX.X, op=ALU.max),
                      rd=[b_gpsl], wr=[b_gpmx], tag="gpr")
                umax_l = mxsl[:, 293:294]
                b_umaxl = TBuf()
                em.op("ve", lambda v: v.tensor_reduce(umax_l, up_mx, axis=AX.X, op=ALU.max),
                      rd=[b_umx], wr=[b_umaxl], tag="umr")
                both = mxsl[:, 294:296]
                b_both = TBuf()
                em.op("ve", lambda v: v.tensor_copy(both[:, 0:1], gp_mx), rd=[b_gpmx],
                      wr=[b_both], accum=True, tag="bb0")
                em.op("ve", lambda v: v.tensor_copy(both[:, 1:2], umax_l), rd=[b_umaxl],
                      wr=[b_both], accum=True, tag="bb1")
                g2col, b_g2 = all_reduce_max(both, b_both, 2, "b3")
                s_gp, r_gp, rg2 = bstat[:, 4:5], bstat[:, 5:6], bstat[:, 6:7]
                s_up, r_up = bstat[:, 7:8], bstat[:, 8:9]
                em.op("ve", lambda v: v.tensor_scalar(s_gp, g2col[:, 0:1], C32767 * 1.0, 0.0,
                                                      op0=ALU.mult, op1=ALU.max),
                      rd=[b_g2], wr=[b_bstat], accum=True, tag="sgp0")
                em.op("ve", lambda v: v.tensor_scalar(s_gp, s_gp, 2.0 ** -16, 1e-12,
                                                      op0=ALU.mult, op1=ALU.max),
                      rd=[b_bstat], wr=[b_bstat], accum=True, tag="sgp1")
                em.op("ve", lambda v: v.reciprocal(r_gp, s_gp), rd=[b_bstat], wr=[b_bstat],
                      accum=True, tag="rgp")
                em.op("ve", lambda v: v.tensor_scalar(rg2, r_gp, 2.0 ** -16, None, op0=ALU.mult),
                      rd=[b_bstat], wr=[b_bstat], accum=True, tag="rg2")
                em.op("ve", lambda v: v.tensor_scalar(s_up, g2col[:, 1:2], C32767, 1e-12,
                                                      op0=ALU.mult, op1=ALU.max),
                      rd=[b_g2], wr=[b_bstat], accum=True, tag="su")
                em.op("ve", lambda v: v.reciprocal(r_up, s_up), rd=[b_bstat], wr=[b_bstat],
                      accum=True, tag="ru")
                nm_gp = bstat[:, 12:13]
                nm_up = bstat[:, 13:14]
                em.op("ve", lambda v: v.tensor_scalar(nm_gp, s_gp, -MAGIC, None, op0=ALU.mult),
                      rd=[b_bstat], wr=[b_bstat], accum=True, tag="nmgp")
                em.op("ve", lambda v: v.tensor_scalar(nm_up, s_up, -MAGIC, None, op0=ALU.mult),
                      rd=[b_bstat], wr=[b_bstat], accum=True, tag="nmup")
            ph2b.close()
            ph_x.close()
            em.barrier()
            if _LVL[STAGE] <= 3:
                dump_y1()
                ph.close()
                return

            wdT = _A("wdT", [P, 24 * D], BF16)
            wdstg = _A("wdstg", [P, I], BF16)
            b_wdT = mk_bufs(24)
            b_wdstg = TBuf()
            for hc in range(8):
                src_ap = wsh_out[:, 6 * 131072 + hc * 49152:
                                 6 * 131072 + (hc + 1) * 49152]\
                    .rearrange("r (p c) -> p r c", p=P)
                em.op("sy", lambda s, hc=hc, src_ap=src_ap: s.dma_start(
                    out=wdstg[:].rearrange("p (r c) -> p r c", r=8), in_=src_ap),
                    rd=[b_ag], wr=[b_wdstg], dma=True, tag=f"wdl{hc}")
                for g in range(3):
                    for j in range(8):
                        ic = g * 8 + j
                        em.op("te", lambda t, ic=ic, j=j: t.transpose(
                            xq_ps[:, j * P:(j + 1) * P], wdstg[:, ic * P:(ic + 1) * P],
                            ident[:]),
                            rd=[b_wdstg, b_ident], wr=[b_xqps], accum=(j > 0),
                            tag=f"wdt{hc}_{ic}")
                    for j in range(8):
                        ic = g * 8 + j
                        em.op("sc", lambda s, ic=ic, j=j, hc=hc: s.copy(
                            wdT[:, ic * D + hc * P: ic * D + (hc + 1) * P],
                            xq_ps[:, j * P:(j + 1) * P]),
                            rd=[b_xqps], wr=[b_wdT[ic]], accum=True, tag=f"wde{hc}_{ic}")


            ph3 = ExitStack()
            if True:
                e2_g = ph3.enter_context(nc.sbuf_tensor("e2_g", [P, 2 * E], F32))
                e2_u = ph3.enter_context(nc.sbuf_tensor("e2_u", [P, 2 * E], F32))
                e2_tg = ph3.enter_context(nc.sbuf_tensor("e2_tg", [P, E], F32))
                e2_tu = ph3.enter_context(nc.sbuf_tensor("e2_tu", [P, E], F32))
                e2_o = ph3.enter_context(nc.sbuf_tensor("e2_o", [P, 2 * E], F32))
                b_e2g = mk_bufs(2)
                b_e2u = mk_bufs(2)
                b_e2tg, b_e2tu = TBuf(), TBuf()
                b_e2o = mk_bufs(2)
                oint_slots = mxsl[:, 232:256]
                b_osl = TBuf()
                s_gp, rg2 = bstat[:, 4:5], bstat[:, 6:7]
                s_up, r_up = bstat[:, 7:8], bstat[:, 8:9]
                nm_gp, nm_up = bstat[:, 12:13], bstat[:, 13:14]
                for ic in range(NIC):
                    pi = ic % 2
                    ga = e2_g[:, pi * E:(pi + 1) * E]
                    ua = e2_u[:, pi * E:(pi + 1) * E]
                    oa = e2_o[:, pi * E:(pi + 1) * E]
                    em.op("sy", lambda s, ic=ic, ga=ga: s.dma_start(
                        out=ga, in_=gpint_dram[ic * P:(ic + 1) * P, :]),
                        rd=[d_gpint[ic]], wr=[b_e2g[pi]], dma=True, tag=f"e2g{ic}")
                    em.op("sy", lambda s, ic=ic, ua=ua: s.dma_start(
                        out=ua, in_=uraw_dram[ic * P:(ic + 1) * P, :]),
                        rd=[d_uraw[ic]], wr=[b_e2u[pi]], dma=True, tag=f"e2u{ic}")
                    em.op("sc", lambda s, ga=ga: s.activation(e2_tg[:], ga, AF.Identity,
                                                              scale=rg2, bias=MAGIC),
                          rd=[b_e2g[pi], b_bstat], wr=[b_e2tg],
                          deps=b_consts.rdeps(), tag=f"etg{ic}")
                    em.op("sc", lambda s: s.activation(e2_tg[:], e2_tg[:], AF.Identity,
                                                       scale=s_gp, bias=nm_gp),
                          rd=[b_e2tg, b_bstat], wr=[b_e2tg], tag=f"eg2{ic}")
                    em.op("sc", lambda s, ua=ua: s.activation(e2_tu[:], ua, AF.Identity,
                                                              scale=r_up, bias=MAGIC),
                          rd=[b_e2u[pi], b_bstat], wr=[b_e2tu], tag=f"etu{ic}")
                    em.op("ve", lambda v: v.tensor_scalar(e2_tu[:], e2_tu[:], MAGIC, s_up,
                                                          op0=ALU.subtract, op1=ALU.mult),
                          rd=[b_e2tu, b_bstat], wr=[b_e2tu], tag=f"eu2{ic}")
                    em.op("ve", lambda v, oa=oa: v.tensor_tensor(oa, e2_tg[:], e2_tu[:], ALU.mult),
                          rd=[b_e2tg, b_e2tu], wr=[b_e2o[pi]], tag=f"eo{ic}")
                    em.op("ve", lambda v, ic=ic, oa=oa: v.tensor_reduce(
                        oint_slots[:, ic:ic + 1], oa, axis=AX.X, op=ALU.max,
                        apply_absolute_value=True),
                        rd=[b_e2o[pi]], wr=[b_osl], accum=True, tag=f"ex{ic}")
                    em.op("gp", lambda g, ic=ic, oa=oa: g.dma_start(
                        out=oint_dram[ic * P:(ic + 1) * P, :], in_=oa),
                        rd=[b_e2o[pi]], wr=[d_oint[ic]], dma=True, tag=f"es{ic}")
                omax_l = mxsl[:, 296:297]
                b_omaxl = TBuf()
                em.op("ve", lambda v: v.tensor_reduce(omax_l, oint_slots, axis=AX.X, op=ALU.max),
                      rd=[b_osl], wr=[b_omaxl], tag="omr")
                gocol, b_go = all_reduce_max(omax_l, b_omaxl, 1, "b4")
                s_o, r_o = bstat[:, 9:10], bstat[:, 10:11]
                em.op("ve", lambda v: v.tensor_scalar(s_o, gocol, C32767, 1e-12,
                                                      op0=ALU.mult, op1=ALU.max),
                      rd=[b_go], wr=[b_bstat], accum=True, tag="so")
                em.op("ve", lambda v: v.reciprocal(r_o, s_o), rd=[b_bstat], wr=[b_bstat],
                      accum=True, tag="ro")
            ph3.close()
            em.barrier()
            if _LVL[STAGE] <= 4:
                dump_y1()
                ph.close()
                return

            if True:
                dn_ps0 = fm_ps[4]
                dn_ps1 = fm_ps[5]
                obf = _A("obf", [P, 24 * T], BF16)
                e3_in = _A("e3_in", [P, 2 * E], F32)
                e3_t = _A("e3_t", [P, E], F32)
                f_out = _A("f_out", [P, D], F32)
                f_t = _A("f_t", [P, D], F32)
                f_y1 = _A("f_y1", [P, D], F32)
                stats_b = _A("stats_b", [P, 300], F32)
                stats_holder[0] = stats_b
                sc_slot[0] = 0
                dn_ps = [dn_ps0, dn_ps1]
                b_dnps = mk_bufs(2)
                b_obf = mk_bufs(24)
                b_e3in = mk_bufs(2)
                b_e3t = TBuf()
                b_fout, b_ft, b_fy1 = TBuf(), TBuf(), TBuf()
                s_o, r_o = bstat[:, 9:10], bstat[:, 10:11]

                for ic in range(NIC):
                    pi = ic % 2
                    ina = e3_in[:, pi * E:(pi + 1) * E]
                    em.op("sy", lambda s, ic=ic, ina=ina: s.dma_start(
                        out=ina, in_=oint_dram[ic * P:(ic + 1) * P, :]),
                        rd=[d_oint[ic]], wr=[b_e3in[pi]], dma=True, tag=f"e3l{ic}")
                    em.op("ve", lambda v, ina=ina: v.tensor_scalar(e3_t[:], ina, r_o, MAGIC,
                                                                   op0=ALU.mult, op1=ALU.add),
                          rd=[b_e3in[pi], b_bstat], wr=[b_e3t], tag=f"e3t{ic}")
                    em.op("ve", lambda v, ic=ic: v.tensor_scalar(
                        obf[:, ic * T:(ic + 1) * T], e3_t[:], MAGIC, None, op0=ALU.subtract),
                        rd=[b_e3t], wr=[b_obf[ic]], tag=f"e3q{ic}")

                for t_ in range(NT):
                    for ic in range(24):
                        for half in range(2):
                            em.op("te", lambda t, ic=ic, t_=t_, half=half: t.matmul(
                                dn_ps[half][:], obf[:, ic * T + t_ * P: ic * T + (t_ + 1) * P],
                                wdT[:, ic * D + half * 512: ic * D + half * 512 + 512],
                                start=(ic == 0), stop=(ic == 23)),
                                rd=[b_obf[ic], b_wdT[ic]], wr=[b_dnps[half]], accum=(ic > 0),
                                tag=f"dn{t_}_{half}_{ic}")
                    for half in range(2):
                        em.op("ve", lambda v, half=half: v.tensor_scalar(
                            f_out[:, half * 512:(half + 1) * 512], dn_ps[half][:], s_o, None,
                            op0=ALU.mult),
                            rd=[b_dnps[half], b_bstat], wr=[b_fout], accum=(half > 0),
                            tag=f"fo{t_}_{half}")
                    layernorm(f_out[:], b_fout, f_t[:], b_ft, f"l5_{t_}")
                    em.op("sy", lambda s, t_=t_: s.dma_start(
                        out=f_y1[:], in_=y1_dram[t_ * P:(t_ + 1) * P, :]),
                        rd=[d_y1[t_]], wr=[b_fy1], dma=True, tag=f"fyl{t_}")
                    em.op("ve", lambda v: v.tensor_tensor(f_t[:], f_t[:], f_y1[:], ALU.add),
                          rd=[b_ft, b_fy1], wr=[b_ft], tag=f"y2_{t_}")
                    tok = em.op("gp", lambda g, t_=t_: g.dma_start(
                        out=y_out[t_ * P:(t_ + 1) * P, :], in_=f_t[:]),
                        rd=[b_ft], dma=True, tag=f"ys{t_}")
                    em.force_inc(tok)
        ph.close()

    if STAGE != "a":
        phase_b()

    st.close()

    em.finalize()
    from contextlib import ExitStack as _ES
    semstack = _ES()
    sems = {}
    for name in em.sem_names():
        sems[name] = semstack.enter_context(nc.semaphore("s_" + name))
    with nc.Block() as block:
        @block.sync
        def _(sync):
            em.replay("sy", sync, sems)

        @block.scalar
        def _(scalar):
            em.replay("sc", scalar, sems)

        @block.vector
        def _(vector):
            em.replay("ve", vector, sems)

        @block.tensor
        def _(tensor):
            em.replay("te", tensor, sems)

        @block.gpsimd
        def _(gpsimd):
            em.replay("gp", gpsimd, sems)
            for s in range(Em.NSLOT):
                name = f"gp_d{s}"
                if em.final_vals.get(name):
                    gpsimd.wait_ge(sems[name], em.final_vals[name])
    semstack.close()

    return nc


_NC_CACHE = {}


def kernel(**inputs):
    if "prog" not in _NC_CACHE:
        _NC_CACHE["prog"] = build_program()
    nc = _NC_CACHE["prog"]

    xf = np.ascontiguousarray(np.asarray(inputs["L0_attn_x"], dtype=np.float32))
    shared = {
        "ln_g": np.ascontiguousarray(np.asarray(inputs["L0_attn_ln_g"], np.float32)),
        "ln_b": np.ascontiguousarray(np.asarray(inputs["L0_attn_ln_b"], np.float32)),
        "W_in": np.ascontiguousarray(np.asarray(inputs["L0_attn_W_in"], np.float32)),
        "b_in": np.ascontiguousarray(np.asarray(inputs["L0_attn_b_in"], np.float32)),
        "W_out": np.ascontiguousarray(np.asarray(inputs["L0_attn_W_out"], np.float32)),
        "b_out": np.ascontiguousarray(np.asarray(inputs["L0_attn_b_out"], np.float32)),
        "g_q": np.ascontiguousarray(np.asarray(inputs["L0_attn_gamma_q"], np.float32)),
        "be_q": np.ascontiguousarray(np.asarray(inputs["L0_attn_beta_q"], np.float32)),
        "g_k": np.ascontiguousarray(np.asarray(inputs["L0_attn_gamma_k"], np.float32)),
        "be_k": np.ascontiguousarray(np.asarray(inputs["L0_attn_beta_k"], np.float32)),
    }
    wg = np.asarray(inputs["L0_ffn_w_gate"], np.float32)
    wu = np.asarray(inputs["L0_ffn_w_up"], np.float32)
    wd = np.asarray(inputs["L0_ffn_w_down"], np.float32)
    in_maps = []
    for c in range(8):
        b, h = c // 2, c % 2
        xb = xf[b]
        xr = np.concatenate([xb[h * T:(h + 1) * T], xb[(1 - h) * T:(2 - h) * T]], axis=0)
        m = {"x": np.ascontiguousarray(xr),
             "wg_sh": np.ascontiguousarray(wg[c * 384:(c + 1) * 384]),
             "wu_sh": np.ascontiguousarray(wu[c * 384:(c + 1) * 384]),
             "wd_sh": np.ascontiguousarray(wd[:, c * 384:(c + 1) * 384])}
        m.update(shared)
        in_maps.append(m)

    res = run_bass_kernel_spmd(nc, in_maps, core_ids=list(range(8)))
    out = np.empty((4, L, D), dtype=np.float32)
    for c in range(8):
        b, h = c // 2, c % 2
        out[b, h * T:(h + 1) * T] = res.results[c]["y"]
    return out


# version 19
# speedup vs baseline: 1.0158x; 1.0158x over previous
import numpy as np
import ml_dtypes
import concourse.bass as bass
import concourse.mybir as mybir
import bass_rust
from concourse import library_config
from concourse.bass_utils import run_bass_kernel_spmd

F32 = mybir.dt.float32
BF16 = mybir.dt.bfloat16
AF = mybir.ActivationFunctionType
ALU = mybir.AluOpType
AX = mybir.AxisListType

D, E, S, I, L, T, P = 1024, 2048, 128, 3072, 4096, 2048, 128
NT = T // P
NKT = L // P
NIC = I // P
LN_EPS = 1e-5
MAGIC = 12582912.0
C7 = float(np.float32(1.0 / 7.0))
C32767 = float(np.float32(1.0 / 32767.0))
I4096 = float(np.float32(1.0 / 4096.0))

DEBUG_TAPS = False
STAGE = "full"
_LVL = {"a": 0, "b1": 1, "bw": 1.3, "bg": 1.6, "bmm": 2, "bchain": 3, "be2": 4, "full": 9}


class TBuf:
    __slots__ = ("producers", "readers")

    def __init__(self):
        self.producers = []
        self.readers = []

    def rdeps(self):
        return list(self.producers)

    def wdeps(self):
        return self.producers + self.readers

    def wrote(self, tok, reset=True):
        if reset:
            self.producers = [tok]
            self.readers = []
        else:
            self.producers.append(tok)

    def read(self, tok):
        self.readers.append(tok)


def mk_bufs(n):
    return [TBuf() for _ in range(n)]


class Em:
    ENGS = ("sy", "sc", "ve", "te", "gp")
    NSLOT = 8

    def __init__(self):
        self.ops = {e: [] for e in self.ENGS}
        self.barriers = []

    def barrier(self):
        snap = {}
        for eng in self.ENGS:
            if self.ops[eng]:
                self.ops[eng][-1]["inc"] = True
                snap[eng] = len(self.ops[eng])
        self.barriers.append(snap)

    def emit(self, eng, fn, deps=(), dma=False, cc=False, tag=""):
        idx = len(self.ops[eng])
        self.ops[eng].append({"fn": fn, "deps": [d for d in deps if d is not None],
                              "dma": dma, "cc": cc, "inc": False, "sem": None,
                              "val": 0, "prewait": None, "tag": tag,
                              "bar": len(self.barriers)})
        return (eng, idx)

    def op(self, eng, fn, rd=(), wr=(), deps=(), dma=False, cc=False, tag="", accum=False):
        dd = list(deps)
        for b in rd:
            dd += b.rdeps()
        for b in wr:
            dd += (b.rdeps() if accum else b.wdeps())
        best = {}
        keep = set()
        for d in dd:
            if d is None:
                continue
            pe, pi = d
            o = self.ops[pe][pi]
            if o["dma"] or o["cc"]:
                keep.add(d)
            elif pe == eng == "te":
                continue
            elif pi > best.get(pe, -1):
                best[pe] = pi
        dd = list(keep) + [(pe, pi) for pe, pi in best.items()]
        tok = self.emit(eng, fn, dd, dma=dma, cc=cc, tag=tag)
        for b in rd:
            b.read(tok)
        for b in wr:
            b.wrote(tok, reset=not accum)
        return tok

    def force_inc(self, tok):
        self.ops[tok[0]][tok[1]]["inc"] = True

    def finalize(self):
        for eng in self.ENGS:
            for o in self.ops[eng]:
                if o["dma"] or o["cc"]:
                    o["inc"] = True
                for (pe, pi) in o["deps"]:
                    self.ops[pe][pi]["inc"] = True
        self.final_vals = {}
        cc_n = 0
        for eng in self.ENGS:
            cum = 0
            dk = 0
            for o in self.ops[eng]:
                if o["cc"]:
                    slot = cc_n % 4
                    o["sem"] = f"cc{slot}"
                    o["val"] = cc_n // 4 + 1
                    if cc_n >= 4:
                        o["prewait"] = (o["sem"], o["val"] - 1)
                    cc_n += 1
                elif o["dma"]:
                    slot = dk % self.NSLOT
                    o["sem"] = f"{eng}_d{slot}"
                    o["val"] = 16 * (dk // self.NSLOT + 1)
                    if dk >= self.NSLOT:
                        o["prewait"] = (o["sem"], o["val"] - 16)
                    dk += 1
                elif o["inc"]:
                    cum += 1
                    o["sem"] = eng
                    o["val"] = cum
                if o["sem"] is not None:
                    self.final_vals[o["sem"]] = o["val"]
        self.bar_waits = []
        for snap in self.barriers:
            w = {}
            for eng, bound in snap.items():
                for o in self.ops[eng][:bound]:
                    if o["sem"] is not None:
                        w[o["sem"]] = max(w.get(o["sem"], 0), o["val"])
            self.bar_waits.append(w)

    def sem_names(self):
        names = list(self.ENGS) + [f"cc{j}" for j in range(4)]
        for eng in self.ENGS:
            for s in range(self.NSLOT):
                names.append(f"{eng}_d{s}")
        return names

    def replay(self, eng_name, eng, sems):
        watermark = {}
        cur_bar = 0
        for o in self.ops[eng_name]:
            need = {}
            if o["bar"] > cur_bar:
                for b in range(cur_bar, o["bar"]):
                    for sname, v in self.bar_waits[b].items():
                        if v > need.get(sname, 0):
                            need[sname] = v
                cur_bar = o["bar"]
            for (pe, pi) in o["deps"]:
                p = self.ops[pe][pi]
                if p["sem"] is None:
                    continue
                if p["val"] > need.get(p["sem"], 0):
                    need[p["sem"]] = p["val"]
            if o["prewait"] is not None:
                sname, v = o["prewait"]
                if v > need.get(sname, 0):
                    need[sname] = v
            for sname, v in sorted(need.items()):
                if v > watermark.get(sname, 0):
                    eng.wait_ge(sems[sname], v)
                    watermark[sname] = v
            inst = o["fn"](eng)
            if o["inc"]:
                assert inst is not None, f"inc op returned None: {o['tag']}"
                inst.then_inc(sems[o["sem"]], 16 if o["dma"] else 1)


def build_program():
    nc = bass.Bass("TRN2", target_bir_lowering=False)
    em = Em()
    dkind = dict(kind="ExternalOutput") if DEBUG_TAPS else {}

    x_in = nc.dram_tensor("x", [L, D], F32, kind="ExternalInput").ap()
    ln_g = nc.dram_tensor("ln_g", [D], F32, kind="ExternalInput").ap()
    ln_b = nc.dram_tensor("ln_b", [D], F32, kind="ExternalInput").ap()
    W_in = nc.dram_tensor("W_in", [D, 2 * E + S], F32, kind="ExternalInput").ap()
    b_in_d = nc.dram_tensor("b_in", [2 * E + S], F32, kind="ExternalInput").ap()
    W_out = nc.dram_tensor("W_out", [E, D], F32, kind="ExternalInput").ap()
    b_out_d = nc.dram_tensor("b_out", [D], F32, kind="ExternalInput").ap()
    g_q = nc.dram_tensor("g_q", [S], F32, kind="ExternalInput").ap()
    be_q = nc.dram_tensor("be_q", [S], F32, kind="ExternalInput").ap()
    g_k = nc.dram_tensor("g_k", [S], F32, kind="ExternalInput").ap()
    be_k = nc.dram_tensor("be_k", [S], F32, kind="ExternalInput").ap()
    wg_sh = nc.dram_tensor("wg_sh", [384, D], F32, kind="ExternalInput").ap()
    wu_sh = nc.dram_tensor("wu_sh", [384, D], F32, kind="ExternalInput").ap()
    wd_sh = nc.dram_tensor("wd_sh", [D, 384], F32, kind="ExternalInput").ap()
    y_out = nc.dram_tensor("y", [T, D], F32, kind="ExternalOutput").ap()

    u_dram = nc.dram_tensor("u_sc", [T, E], BF16, **dkind).ap()
    v_dram = nc.dram_tensor("v_sc", [L, E], BF16, **dkind).ap()
    xin_dram = nc.dram_tensor("xin_sc", [T, D], F32, **dkind).ap()
    y1_dram = nc.dram_tensor("y1_sc", [T, D], F32, **dkind).ap()
    y1n_dram = nc.dram_tensor("y1n_sc", [T, D], F32, **dkind).ap()
    wout_dram = nc.dram_tensor("woutbf_sc", [E, D], BF16).ap()
    SHN = 3 * 131072 * 2 + 8 * 49152
    wsh_in = nc.dram_tensor("wsh_in", [1, SHN], BF16).ap()
    wsh_out = nc.dram_tensor("wsh_out", [8, SHN], BF16).ap()
    VKG = 4 * P * E
    VKT = VKG + P * 512
    vkg_in = [nc.dram_tensor(f"vkg_in{g}", [1, VKT], BF16).ap() for g in range(4)]
    vkg_out = [nc.dram_tensor(f"vkg_out{g}", [2, VKT], BF16).ap() for g in range(4)]
    graw_dram = nc.dram_tensor("graw_sc", [I, T], F32, **dkind).ap()
    uraw_dram = nc.dram_tensor("uraw_sc", [I, T], F32, **dkind).ap()
    gpint_dram = nc.dram_tensor("gpint_sc", [I, T], F32, **dkind).ap()
    oint_dram = nc.dram_tensor("oint_sc", [I, T], F32, **dkind).ap()
    cc_in = {}
    cc_out = {}

    d_u = mk_bufs(NT)
    d_v = mk_bufs(NKT)
    d_xin = mk_bufs(NT)
    d_y1 = mk_bufs(NT)
    d_y1n = mk_bufs(NT)
    d_wout = mk_bufs(16)
    d_wsh = TBuf()
    b_ag = TBuf()
    d_vk = [TBuf() for _ in range(4)]
    b_vkag = [TBuf() for _ in range(4)]
    d_graw = mk_bufs(NIC)
    d_uraw = mk_bufs(NIC)
    d_gpint = mk_bufs(NIC)
    d_oint = mk_bufs(NIC)
    d_cc = mk_bufs(4)

    from contextlib import ExitStack
    st = ExitStack()
    b_y1nmx = TBuf()

    def sb(name, shape, dt):
        return st.enter_context(nc.sbuf_tensor(name, shape, dt))

    consts = sb("consts", [P, 4], F32)
    ident = sb("ident", [P, P], BF16)
    ones1 = sb("ones1", [1, P], BF16)
    gb_cols = sb("gb_cols", [P, 5], F32)
    lnT = sb("lnT", [P, 16], F32)
    mxsl = sb("mxsl", [P, 320], F32)
    gsc = sb("gsc", [P, 16], F32)
    bstat = sb("bstat", [P, 16], F32)
    bout_hl = sb("bout_hl", [1, 2 * D], BF16)
    atn = ExitStack()
    k_fm = atn.enter_context(nc.sbuf_tensor("k_fm", [P, L], BF16))
    q_fm = atn.enter_context(nc.sbuf_tensor("q_fm", [P, T], BF16))

    CONST_EPS = consts[:, 0:1]

    nc.const_aps.register(F32, LN_EPS, consts[:, 0:1])
    nc.const_aps.register(F32, MAGIC, consts[:, 1:2])
    nc.const_aps.register(F32, 65536.0, consts[:, 2:3])
    nc.const_aps.register(F32, I4096, consts[:, 3:4])

    b_consts = TBuf()
    b_ident = TBuf()
    b_ones = TBuf()
    b_gb = TBuf()
    b_lnT = TBuf()
    b_bout = TBuf()
    b_bstage = TBuf()
    b_bstat = TBuf()
    b_kfm = mk_bufs(NKT)
    b_qfm = mk_bufs(NT)

    em.op("gp", lambda g: g.memset(consts[:, 0:1], LN_EPS), wr=[b_consts], accum=True, tag="c0")
    em.op("gp", lambda g: g.memset(consts[:, 1:2], MAGIC), wr=[b_consts], accum=True, tag="c1")
    em.op("gp", lambda g: g.memset(consts[:, 2:3], 65536.0), wr=[b_consts], accum=True, tag="c2")
    em.op("gp", lambda g: g.memset(consts[:, 3:4], I4096), wr=[b_consts], accum=True, tag="c3")
    em.op("gp", lambda g: g.memset(ident[:], 0.0), wr=[b_ident], tag="id0")
    em.op("gp", lambda g: g.affine_select(out=ident[:], in_=ident[:], compare_op=ALU.not_equal,
                                          fill=1.0, base=0, pattern=[[-1, P]], channel_multiplier=1),
          rd=[b_ident], wr=[b_ident], tag="id1")
    em.op("gp", lambda g: g.memset(ones1[:], 1.0), wr=[b_ones], tag="ones")

    for j, src in enumerate([g_q, be_q, g_k, be_k]):
        em.op("sy", lambda s, j=j, src=src: s.dma_start(out=gb_cols[:, j:j + 1], in_=src[:, None]),
              wr=[b_gb], dma=True, accum=True, tag=f"gb{j}")
    em.op("sy", lambda s: s.dma_start(out=gb_cols[:, 4:5], in_=b_in_d[2 * E:2 * E + S, None]),
          wr=[b_gb], dma=True, accum=True, tag="bz")
    em.op("sy", lambda s: s.dma_start(out=lnT[:, 0:8], in_=ln_g.rearrange("(c p) -> p c", p=P),
                                        allow_slow_non_contiguous=True),
          wr=[b_lnT], dma=True, accum=True, tag="lng")
    em.op("sy", lambda s: s.dma_start(out=lnT[:, 8:16], in_=ln_b.rearrange("(c p) -> p c", p=P),
                                        allow_slow_non_contiguous=True),
          wr=[b_lnT], dma=True, accum=True, tag="lnb")
    sc_slot = [0]
    stats_holder = [None]

    def layernorm(x_ap, b_x, out_ap, b_out_t, tag):
        stats = stats_holder[0]
        s0 = sc_slot[0]
        sc_slot[0] += 18
        assert sc_slot[0] <= stats.shape[1]
        bns = stats[:, s0:s0 + 12]
        mv = stats[:, s0 + 12:s0 + 14]
        sd = stats[:, s0 + 14:s0 + 15]
        rstd = stats[:, s0 + 15:s0 + 16]
        nmr = stats[:, s0 + 16:s0 + 17]
        b_bns, b_mv, b_sd, b_rstd, b_nmr = TBuf(), TBuf(), TBuf(), TBuf(), TBuf()
        em.op("ve", lambda v: v.bn_stats(bns[:, 0:6], x_ap[:, 0:512]),
              rd=[b_x], wr=[b_bns], accum=True, tag=tag + "b0")
        em.op("ve", lambda v: v.bn_stats(bns[:, 6:12], x_ap[:, 512:1024]),
              rd=[b_x], wr=[b_bns], accum=True, tag=tag + "b1")
        em.op("ve", lambda v: v.bn_aggr(mv, bns), rd=[b_bns], wr=[b_mv], tag=tag + "ag")
        em.op("sc", lambda s: s.activation(sd, mv[:, 1:2], AF.Sqrt, bias=LN_EPS),
              rd=[b_mv], wr=[b_sd], deps=b_consts.rdeps(), tag=tag + "sd")
        em.op("ve", lambda v: v.reciprocal(rstd, sd), rd=[b_sd], wr=[b_rstd], tag=tag + "rs")
        em.op("ve", lambda v: v.scalar_tensor_tensor(nmr, mv[:, 0:1], -1.0, rstd,
                                                     op0=ALU.mult, op1=ALU.mult),
              rd=[b_mv, b_rstd], wr=[b_nmr], tag=tag + "nm")
        em.op("sc", lambda s: s.activation(out_ap, x_ap, AF.Identity, bias=nmr, scale=rstd),
              rd=[b_x, b_nmr, b_rstd], wr=[b_out_t], tag=tag + "ap")

    def layernorm12(x_ap, b_x, xin_ap, b_xin, na_ap, b_na, own, tag):
        stats = stats_holder[0]
        s0 = sc_slot[0]
        sc_slot[0] += 28
        assert sc_slot[0] <= stats.shape[1]
        bns = stats[:, s0:s0 + 12]
        mv = stats[:, s0 + 12:s0 + 14]
        ap2 = stats[:, s0 + 14:s0 + 16]
        sd2 = stats[:, s0 + 16:s0 + 18]
        rr2 = stats[:, s0 + 18:s0 + 20]
        nbs = stats[:, s0 + 20:s0 + 21]
        nb1 = stats[:, s0 + 21:s0 + 22]
        rv = stats[:, s0 + 22:s0 + 23]
        v2 = stats[:, s0 + 23:s0 + 24]
        b_st, b_mv, b_ap2, b_sd2, b_rr2, b_nb = TBuf(), TBuf(), TBuf(), TBuf(), TBuf(), TBuf()
        b_nb1 = TBuf()
        em.op("ve", lambda v: v.bn_stats(bns[:, 0:6], x_ap[:, 0:512]),
              rd=[b_x], wr=[b_st], accum=True, tag=tag + "b0")
        em.op("ve", lambda v: v.bn_stats(bns[:, 6:12], x_ap[:, 512:1024]),
              rd=[b_x], wr=[b_st], accum=True, tag=tag + "b1")
        em.op("ve", lambda v: v.bn_aggr(mv, bns), rd=[b_st], wr=[b_mv], tag=tag + "ag")
        em.op("ve", lambda v: v.tensor_scalar(ap2[:, 0:1], mv[:, 1:2], 1.0, LN_EPS,
                                              op0=ALU.mult, op1=ALU.add),
              rd=[b_mv], wr=[b_ap2], accum=True, tag=tag + "a1")
        em.op("ve", lambda v: v.reciprocal(rv, ap2[:, 0:1]), rd=[b_ap2], wr=[b_nb],
              accum=True, tag=tag + "rv")
        em.op("ve", lambda v: v.tensor_scalar(v2, mv[:, 1:2], rv, None, op0=ALU.mult),
              rd=[b_mv, b_nb], wr=[b_nb], accum=True, tag=tag + "v2")
        em.op("ve", lambda v: v.scalar_tensor_tensor(ap2[:, 1:2], v2, LN_EPS, ap2[:, 0:1],
                                                     op0=ALU.add, op1=ALU.mult),
              rd=[b_nb, b_ap2], wr=[b_ap2], accum=True, tag=tag + "pr")
        em.op("sc", lambda s: s.activation(sd2, ap2, AF.Sqrt),
              rd=[b_ap2], wr=[b_sd2], tag=tag + "sd")
        em.op("ve", lambda v: v.reciprocal(rr2, sd2), rd=[b_sd2], wr=[b_rr2], tag=tag + "rr")
        em.op("ve", lambda v: v.scalar_tensor_tensor(nbs, mv[:, 0:1], -1.0, rr2[:, 1:2],
                                                     op0=ALU.mult, op1=ALU.mult),
              rd=[b_mv, b_rr2], wr=[b_nb1], accum=True, tag=tag + "nb")
        em.op("sc", lambda s: s.activation(na_ap, x_ap, AF.Identity, bias=nbs,
                                           scale=rr2[:, 1:2]),
              rd=[b_x, b_nb1, b_rr2], wr=[b_na], tag=tag + "na")
        if own:
            em.op("ve", lambda v: v.scalar_tensor_tensor(nb1, mv[:, 0:1], -1.0, rr2[:, 0:1],
                                                         op0=ALU.mult, op1=ALU.mult),
                  rd=[b_mv, b_rr2], wr=[b_nb1], accum=True, tag=tag + "n1")
            em.op("sc", lambda s: s.activation(xin_ap, x_ap, AF.Identity, bias=nb1,
                                               scale=rr2[:, 0:1]),
                  rd=[b_x, b_nb1, b_rr2], wr=[b_xin], tag=tag + "x1")

    def phase_a1():
        ph = ExitStack()
        def _A(n, s, d):
            return ph.enter_context(nc.sbuf_tensor(n, s, d))
        def _Q(n, s, d):
            return ph.enter_context(nc.psum_tensor(n, s, d))
        tr_ps = _Q("tr_ps", [P, 1024], BF16)
        mm_ps0 = _Q("mm_ps0", [P, 512], F32)
        mm_ps1 = _Q("mm_ps1", [P, 512], F32)
        mm_ps2 = _Q("mm_ps2", [P, 512], F32)
        mm_ps3 = _Q("mm_ps3", [P, 512], F32)
        mm_ps4 = _Q("mm_ps4", [P, 512], F32)
        Wi_bf = _A("Wi_bf", [P, 8 * 4224], BF16)
        Wi_st = _A("Wi_st", [P, 2 * 4224], F32)
        binrow = _A("binrow", [1, 2 * 4224], BF16)
        xtile = _A("xtile", [P, 3 * D], F32)
        xintile = _A("xintile", [P, 3 * D], F32)
        na2 = _A("na2", [P, 4 * D], BF16)
        xnT = _A("xnT", [P, 4 * D], BF16)
        ztile = _A("ztile", [P, 2 * S], F32)
        uvout = _A("uvout", [P, 2 * E], BF16)
        w_st = _A("w_st", [P, D], F32)
        w_bfs = _A("w_bfs", [P, D], BF16)
        lp_in = _A("lp_in", [P, D], F32)
        lp_t = _A("lp_t", [P, D], F32)
        lp_t2 = _A("lp_t2", [P, D], F32)
        lp_q = _A("lp_q", [P, D], BF16)
        lp_st = _A("lp_st", [P, 96], F32)
        lnTb = _A("lnTb", [P, 16], BF16)
        stats_a1 = _A("stats_a1", [P, 600], F32)

        stats_holder[0] = stats_a1
        sc_slot[0] = 0
        mm_ps = [mm_ps0, mm_ps1, mm_ps2, mm_ps3, mm_ps4]
        b_trps = TBuf()
        b_mmps = mk_bufs(5)
        b_Wi = mk_bufs(8)
        b_Wist = TBuf()
        b_binrow = TBuf()
        b_xt = mk_bufs(4)
        b_xint = mk_bufs(4)
        b_na2 = mk_bufs(4)
        b_xnT = mk_bufs(4)
        b_z = TBuf()
        b_z2 = TBuf()
        b_uv = mk_bufs(4)
        b_wst = mk_bufs(2)
        b_wbf = TBuf()
        b_lpin = mk_bufs(2)
        b_lpt = TBuf()
        b_lpt2 = TBuf()
        b_lpq = mk_bufs(2)
        b_lpst = TBuf()

        b_Wist2 = [b_Wist, TBuf()]
        for d in range(8):
            wsl = Wi_st[:, (d % 2) * 4224:(d % 2 + 1) * 4224]
            em.op("sy", lambda s, d=d, wsl=wsl: s.dma_start(
                out=wsl, in_=W_in[d * P:(d + 1) * P, :]),
                wr=[b_Wist2[d % 2]], dma=True, tag=f"wi_ld{d}")
            em.op("ve", lambda v, d=d, wsl=wsl: v.tensor_copy(
                Wi_bf[:, d * 4224:(d + 1) * 4224], wsl),
                rd=[b_Wist2[d % 2]], wr=[b_Wi[d]], tag=f"wi_c{d}")
        b_lnTb = TBuf()
        for d in range(8):
            em.op("ve", lambda v, d=d: v.tensor_copy(lnTb[:, 2 * d:2 * d + 1],
                                                     lnT[:, 8 + d:9 + d]),
                  rd=[b_lnT], wr=[b_lnTb], accum=True, tag=f"bh{d}")
            em.op("ve", lambda v, d=d: v.tensor_tensor(lnTb[:, 2 * d + 1:2 * d + 2],
                                                       lnT[:, 8 + d:9 + d],
                                                       lnTb[:, 2 * d:2 * d + 1], ALU.subtract),
                  rd=[b_lnT, b_lnTb], wr=[b_lnTb], accum=True, tag=f"bl{d}")
        bstage = Wi_st[0:1, :]
        em.op("sy", lambda s: s.dma_start(out=bstage[0:1, 0:4224], in_=b_in_d[None, :]),
              wr=[b_Wist2[0], b_Wist2[1], b_bstage], dma=True, tag="bi_ld")
        b_bsum = b_lpt
        for j in range(8):
            bank, bbank = mm_ps[j % 2], b_mmps[j % 2]
            for d in range(8):
                for hl in range(2):
                    em.op("te", lambda t, d=d, j=j, hl=hl, bank=bank: t.matmul(
                        bank[0:1, 0:512], lnTb[:, 2 * d + hl:2 * d + hl + 1],
                        Wi_bf[:, d * 4224 + j * 512: d * 4224 + j * 512 + 512],
                        start=(d == 0 and hl == 0), stop=(d == 7 and hl == 1)),
                        rd=[b_lnTb, b_Wi[d]], wr=[bbank], accum=not (d == 0 and hl == 0),
                        tag=f"bw{j}_{d}_{hl}")
            em.op("ve", lambda v, j=j, bank=bank: v.tensor_tensor(
                lp_t[0:1, 0:512], bank[0:1, 0:512],
                bstage[0:1, j * 512:(j + 1) * 512], ALU.add),
                rd=[bbank, b_bstage], wr=[b_bsum], tag=f"bws{j}")
            em.op("ve", lambda v, j=j: v.tensor_copy(
                binrow[0:1, j * 512:(j + 1) * 512], lp_t[0:1, 0:512]),
                rd=[b_bsum], wr=[b_binrow], accum=True, tag=f"bwh{j}")
            em.op("ve", lambda v, j=j: v.tensor_tensor(
                binrow[0:1, 4224 + j * 512: 4224 + (j + 1) * 512], lp_t[0:1, 0:512],
                binrow[0:1, j * 512:(j + 1) * 512], ALU.subtract),
                rd=[b_bsum, b_binrow], wr=[b_binrow], accum=True, tag=f"bwl{j}")
        zbank, b_zbank = mm_ps[2], b_mmps[2]
        for d in range(8):
            for hl in range(2):
                em.op("te", lambda t, d=d, hl=hl: t.matmul(
                    zbank[:, 0:1], Wi_bf[:, d * 4224 + 2 * E: d * 4224 + 2 * E + S],
                    lnTb[:, 2 * d + hl:2 * d + hl + 1],
                    start=(d == 0 and hl == 0), stop=(d == 7 and hl == 1)),
                    rd=[b_lnTb, b_Wi[d]], wr=[b_zbank], accum=not (d == 0 and hl == 0),
                    tag=f"bwz{d}_{hl}")
        em.op("ve", lambda v: v.tensor_tensor(gb_cols[:, 4:5], gb_cols[:, 4:5],
                                              zbank[:, 0:1], ALU.add),
              rd=[b_gb, b_zbank], wr=[b_gb], accum=True, tag="bwzadd")
        for d in range(8):
            em.op("ve", lambda v, d=d: v.tensor_scalar(
                Wi_bf[:, d * 4224:(d + 1) * 4224], Wi_bf[:, d * 4224:(d + 1) * 4224],
                lnT[:, d:d + 1], None, op0=ALU.mult),
                rd=[b_lnT], wr=[b_Wi[d]], tag=f"wg{d}")

        em.op("sy", lambda s: s.dma_start(out=bstage[0:1, 0:D], in_=b_out_d[None, :]),
              wr=[b_Wist, b_bstage], dma=True, tag="bo_ld")
        em.op("ve", lambda v: v.tensor_copy(bout_hl[0:1, 0:D], bstage[0:1, 0:D]),
              rd=[b_bstage], wr=[b_bout], accum=True, tag="bo_hi")
        em.op("ve", lambda v: v.tensor_tensor(bout_hl[0:1, D:2 * D], bstage[0:1, 0:D],
                                              bout_hl[0:1, 0:D], ALU.subtract),
              rd=[b_bstage, b_bout], wr=[b_bout], accum=True, tag="bo_lo")

        lp_units = ([("g", k) for k in range(3)] + [("u", k) for k in range(3)]
                    + [("d", k) for k in range(8)])
        lp_pos = [0]

        def emit_lpbq_unit():
            if lp_pos[0] >= len(lp_units):
                return
            kind, key = lp_units[lp_pos[0]]
            pi = 0
            lp_pos[0] += 1
            if kind == "g":
                src_ap = wg_sh[key * P:(key + 1) * P, :]
                off, W = key * 131072, D
            elif kind == "u":
                src_ap = wu_sh[key * P:(key + 1) * P, :]
                off, W = 3 * 131072 + key * 131072, D
            else:
                src_ap = wd_sh[key * P:(key + 1) * P, :]
                off, W = 6 * 131072 + key * 49152, 384
            nb = W // 32
            dst_ap = wsh_in[0, off:off + P * W].rearrange("(p c) -> p c", p=P)
            inb = lp_in[:, pi * D:pi * D + W]
            qb = lp_q[:, pi * D:pi * D + W]
            bin_, bq_ = b_lpin[pi], b_lpq[pi]
            amax, ssc, rsc = lp_st[:, 0:nb], lp_st[:, 32:32 + nb], lp_st[:, 64:64 + nb]
            em.op("sy", lambda s: s.dma_start(out=inb, in_=src_ap), wr=[bin_], dma=True, tag="lpl")
            inv = inb.rearrange("p (b k) -> p b k", k=32)
            em.op("ve", lambda v: v.tensor_reduce(amax, inv, axis=AX.X, op=ALU.max,
                                                  apply_absolute_value=True),
                  rd=[bin_], wr=[b_lpst], tag="lpm")
            em.op("ve", lambda v: v.tensor_scalar(ssc, amax, C7, 1e-12, op0=ALU.mult, op1=ALU.max),
                  rd=[b_lpst], wr=[b_lpst], accum=True, tag="lps")
            em.op("ve", lambda v: v.reciprocal(rsc, ssc), rd=[b_lpst], wr=[b_lpst],
                  accum=True, tag="lpr")
            sb3 = ssc.unsqueeze(2).broadcast_to([P, nb, 32])
            rb3 = rsc.unsqueeze(2).broadcast_to([P, nb, 32])
            tv = lp_t[:, 0:W].rearrange("p (b k) -> p b k", k=32)
            t2v = lp_t2[:, 0:W].rearrange("p (b k) -> p b k", k=32)
            qv = qb.rearrange("p (b k) -> p b k", k=32)
            em.op("ve", lambda v: v.tensor_tensor(tv, inv, rb3, ALU.mult),
                  rd=[bin_, b_lpst], wr=[b_lpt], tag="lpt")
            em.op("ve", lambda v: v.tensor_scalar(lp_t2[:, 0:W], lp_t[:, 0:W], MAGIC, MAGIC,
                                                  op0=ALU.add, op1=ALU.subtract),
                  rd=[b_lpt], wr=[b_lpt2], tag="lpn")
            em.op("ve", lambda v: v.tensor_tensor(qv, t2v, sb3, ALU.mult),
                  rd=[b_lpt2, b_lpst], wr=[bq_], tag="lpq")
            em.op("gp", lambda g: g.dma_start(out=dst_ap, in_=qb), rd=[bq_], wr=[d_wsh],
                  dma=True, accum=True, tag="lpw")


        wo_pos = [0]

        def emit_wout_unit():
            if wo_pos[0] >= 16:
                return
            e = wo_pos[0]
            pi = 0
            wo_pos[0] += 1
            stg = w_st[:, 0:D]
            em.op("sy", lambda s, e=e, stg=stg: s.dma_start(out=stg, in_=W_out[e * P:(e + 1) * P, :]),
                  wr=[b_wst[pi], b_wbf], dma=True, tag="wol")
            em.op("ve", lambda v, stg=stg: v.tensor_copy(w_bfs[:], stg), rd=[b_wst[pi]],
                  wr=[b_wbf], tag="woc")
            em.op("gp", lambda g, e=e: g.dma_start(out=wout_dram[e * P:(e + 1) * P, :],
                                                   in_=w_bfs[:]),
                  rd=[b_wbf], wr=[d_wout[e]], dma=True, tag="wos")

        def a1_front(abs_t, own):
            p3 = abs_t % 3
            pi = abs_t % 4
            xa = xtile[:, p3 * D:(p3 + 1) * D]
            xia = xintile[:, p3 * D:(p3 + 1) * D]
            naa = na2[:, pi * D:(pi + 1) * D]
            xnTa = xnT[:, pi * D:(pi + 1) * D]
            bx, bxi, bna, bxnT = b_xt[p3], b_xint[p3], b_na2[pi], b_xnT[pi]

            em.op("sy", lambda s: s.dma_start(out=xa, in_=x_in[abs_t * P:(abs_t + 1) * P, :]),
                  wr=[bx], dma=True, tag=f"xl{abs_t}")
            layernorm12(xa, bx, xia, bxi, naa, bna, own, f"l12_{abs_t}")
            if own:
                em.op("gp", lambda g: g.dma_start(out=xin_dram[abs_t * P:(abs_t + 1) * P, :],
                                                  in_=xia),
                      rd=[bxi], wr=[d_xin[abs_t]], dma=True, tag=f"xis{abs_t}")
            for dd in range(8):
                em.op("te", lambda t, dd=dd, naa=naa: t.transpose(
                    tr_ps[:, dd * P:(dd + 1) * P], naa[:, dd * P:(dd + 1) * P], ident[:]),
                    rd=[bna, b_ident], wr=[b_trps], accum=(dd > 0), tag=f"tr{abs_t}_{dd}")
            em.op("ve", lambda v, xnTa=xnTa: v.tensor_copy(xnTa, tr_ps[:]),
                  rd=[b_trps], wr=[bxnT], tag=f"te{abs_t}")

        def a1_back(abs_t, own):
            pi = abs_t % 4
            pu = abs_t % 2
            xnTa = xnT[:, pi * D:(pi + 1) * D]
            bxnT = b_xnT[pi]

            zp, bzp = mm_ps[4], b_mmps[4]
            for dd in range(8):
                em.op("te", lambda t, dd=dd: t.matmul(
                    zp[:, 0:S], Wi_bf[:, dd * 4224 + 2 * E: dd * 4224 + 2 * E + S],
                    xnTa[:, dd * P:(dd + 1) * P], start=(dd == 0), stop=(dd == 7)),
                    rd=[b_Wi[dd], bxnT], wr=[bzp], accum=(dd > 0), tag=f"zm{abs_t}_{dd}")
            zt = ztile[:, (abs_t % 2) * S:(abs_t % 2 + 1) * S]
            bz2 = b_z if abs_t % 2 == 0 else b_z2
            em.op("sc", lambda s, zt=zt: s.activation(zt, zp[:, 0:S], AF.Silu,
                                                      bias=gb_cols[:, 4:5]),
                  rd=[bzp, b_gb], wr=[bz2], tag=f"ze{abs_t}")
            em.op("sc", lambda s, zt=zt: s.activation(k_fm[:, abs_t * P:(abs_t + 1) * P], zt,
                                                      AF.Identity, bias=gb_cols[:, 3:4],
                                                      scale=gb_cols[:, 2:3]),
                  rd=[bz2, b_gb], wr=[b_kfm[abs_t]], tag=f"kv{abs_t}")
            if own:
                em.op("sc", lambda s, zt=zt: s.activation(q_fm[:, abs_t * P:(abs_t + 1) * P], zt,
                                                          AF.Identity, bias=gb_cols[:, 1:2],
                                                          scale=gb_cols[:, 0:1]),
                      rd=[bz2, b_gb], wr=[b_qfm[abs_t]], tag=f"qv{abs_t}")

            vdst = vkg_in[abs_t // 4][0, (abs_t % 4) * P * E:(abs_t % 4 + 1) * P * E]\
                .rearrange("(p c) -> p c", p=P)
            specs = [("v", E, vdst, d_vk[abs_t // 4], abs_t * P)]
            if own:
                specs.append(("u", 0, u_dram, d_u[abs_t], abs_t * P))
            for which, col0, dstd, dbuf, row in specs:
                uvb = b_uv[pu]
                for dd in range(8):
                    for eb in range(4):
                        bank, bbank = mm_ps[eb], b_mmps[eb]
                        c0 = col0 + eb * 512
                        em.op("te", lambda t, dd=dd, c0=c0, bank=bank: t.matmul(
                            bank[:], xnTa[:, dd * P:(dd + 1) * P],
                            Wi_bf[:, dd * 4224 + c0: dd * 4224 + c0 + 512],
                            start=(dd == 0), stop=False),
                            rd=[b_Wi[dd], bxnT], wr=[bbank], accum=(dd > 0),
                            tag=f"{which}m{abs_t}_{eb}_{dd}")
                for eb in range(4):
                    bank, bbank = mm_ps[eb], b_mmps[eb]
                    c0 = col0 + eb * 512
                    em.op("te", lambda t, c0=c0, bank=bank: t.matmul(
                        bank[:], ones1[:], binrow[0:1, c0:c0 + 512], start=False, stop=False),
                        rd=[b_ones, b_binrow], wr=[bbank], accum=True, tag=f"{which}h{abs_t}_{eb}")
                    em.op("te", lambda t, c0=c0, bank=bank: t.matmul(
                        bank[:], ones1[:], binrow[0:1, 4224 + c0:4224 + c0 + 512],
                        start=False, stop=True),
                        rd=[b_ones, b_binrow], wr=[bbank], accum=True, tag=f"{which}l{abs_t}_{eb}")
                    em.op("sc", lambda s, eb=eb, bank=bank: s.activation(
                        uvout[:, pu * E + eb * 512: pu * E + eb * 512 + 512], bank[:], AF.Silu),
                        rd=[bbank], wr=[uvb], accum=(eb > 0), tag=f"{which}e{abs_t}_{eb}")
                em.op("gp", lambda g, dstd=dstd, row=row, which=which: g.dma_start(
                    out=(dstd[row:row + P, :] if which == "u" else dstd),
                    in_=uvout[:, pu * E:(pu + 1) * E]),
                    rd=[uvb], wr=[dbuf], dma=True, accum=(which == "v"),
                    tag=f"{which}s{abs_t}")

        a1_front(0, True)
        a1_front(1, True)
        for t_ in range(NT):
            if t_ + 2 < NT:
                a1_front(t_ + 2, True)
            a1_back(t_, True)
            emit_lpbq_unit()
            emit_wout_unit()
            if t_ % 4 == 3:
                gidx = t_ // 4
                em.op("gp", lambda g, gidx=gidx: g.dma_start(
                    out=vkg_in[gidx][0, VKG:VKT].rearrange("(p c) -> p c", p=P),
                    in_=k_fm[:, gidx * 512:(gidx + 1) * 512]),
                    rd=b_kfm[4 * gidx:4 * gidx + 4], wr=[d_vk[gidx]], dma=True,
                    accum=True, tag=f"kship{gidx}")
                em.op("gp", lambda g, gidx=gidx: g.collective_compute(
                    "AllGather", ALU.bypass,
                    replica_groups=[[0, 1], [2, 3], [4, 5], [6, 7]],
                    ins=[vkg_in[gidx].opt()], outs=[vkg_out[gidx].opt()]),
                    rd=[d_vk[gidx]], wr=[b_vkag[gidx]], cc=True, tag=f"vk_ag{gidx}")
        while lp_pos[0] < len(lp_units):
            emit_lpbq_unit()
        while wo_pos[0] < 16:
            emit_wout_unit()
        em.op("gp", lambda g: g.collective_compute(
            "AllGather", ALU.bypass, replica_groups=[list(range(8))],
            ins=[wsh_in.opt()], outs=[wsh_out.opt()]),
            rd=[d_wsh], wr=[b_ag], cc=True, tag="wsh_ag")
        ph.close()

    y1n_mx = mxsl[:, 0:16]
    phase_a1()
    em.barrier()

    def phase_a2():
        ph = ExitStack()
        def _A(n, s, d):
            return ph.enter_context(nc.sbuf_tensor(n, s, d))
        def _Q(n, s, d):
            return ph.enter_context(nc.psum_tensor(n, s, d))
        sc_ps0 = _Q("sc_ps0", [P, 512], F32)
        sc_ps1 = _Q("sc_ps1", [P, 512], F32)
        av_ps0 = _Q("av_ps0", [P, 512], F32)
        av_ps1 = _Q("av_ps1", [P, 512], F32)
        av_ps2 = _Q("av_ps2", [P, 512], F32)
        av_ps3 = _Q("av_ps3", [P, 512], F32)
        pj_ps0 = _Q("pj_ps0", [P, 512], F32)
        pj_ps1 = _Q("pj_ps1", [P, 512], F32)
        v_sb = _A("v_sb", [P, NKT * E], BF16)
        scT = _A("scT", [P, L], BF16)
        scf = _A("scf", [P, 1024], F32)
        u_sb = _A("u_sb", [P, 2 * E], BF16)
        m_sb = _A("m_sb", [P, E], BF16)
        mT_sb = _A("mT_sb", [P, E], BF16)
        wo_sb = _A("wo_sb", [P, 8 * D], BF16)
        ep1 = _A("ep1", [P, D], F32)
        ep2 = _A("ep2", [P, D], F32)
        ep3 = _A("ep3", [P, D], F32)
        stats_a2 = _A("stats_a2", [P, 600], F32)

        stats_holder[0] = stats_a2
        sc_slot[0] = 0
        av_ps = [av_ps0, av_ps1, av_ps2, av_ps3]
        sc_ps = [sc_ps0, sc_ps1]
        pj_ps = [pj_ps0, pj_ps1]
        mt_ps = [av_ps0[:].bitcast(BF16), av_ps2[:].bitcast(BF16)]
        b_scps = mk_bufs(2)
        b_scf2 = mk_bufs(2)
        b_avps = mk_bufs(4)
        b_pjps = mk_bufs(2)
        b_vsb = mk_bufs(NKT)
        b_scT = mk_bufs(8)
        b_scf = TBuf()
        b_usb = mk_bufs(2)
        b_msb = mk_bufs(4)
        b_mTsb = mk_bufs(2)
        b_wosb = mk_bufs(2)
        b_ep1, b_ep2, b_ep3 = TBuf(), TBuf(), TBuf()

        for half in range(2):
            for g_ in range(4):
                em.op("sy", lambda s, half=half, g_=g_: s.dma_start(
                    out=k_fm[:, half * T + g_ * 512: half * T + (g_ + 1) * 512],
                    in_=vkg_out[g_][half, VKG:VKT].rearrange("(p c) -> p c", p=P)),
                    rd=[b_vkag[g_]], wr=b_kfm[half * NT + 4 * g_: half * NT + 4 * g_ + 4],
                    dma=True, tag=f"krl{half}_{g_}")
        for g_ in range(4):
            for half in range(2):
                for i_ in range(4):
                    kt = half * NT + g_ * 4 + i_
                    em.op("sy", lambda s, kt=kt, g_=g_, i_=i_, half=half: s.dma_start(
                        out=v_sb[:, kt * E:(kt + 1) * E],
                        in_=vkg_out[g_][half, i_ * P * E:(i_ + 1) * P * E]
                        .rearrange("(p c) -> p c", p=P)),
                        rd=[b_vkag[g_]], wr=[b_vsb[kt]], dma=True, tag=f"vr{kt}")

        for q in range(NT):
            for g in range(4):
                em.op("sy", lambda s, g=g: s.dma_start(
                    out=wo_sb[:, (g % 2) * 4 * D:((g % 2) + 1) * 4 * D].rearrange(
                        "p (c h) -> p c h", c=4),
                    in_=wout_dram[g * 512:(g + 1) * 512, :].rearrange("(c p) h -> p c h", p=P)),
                    rd=[d_wout[4 * g], d_wout[4 * g + 1], d_wout[4 * g + 2], d_wout[4 * g + 3]],
                    wr=[b_wosb[g % 2]], dma=True, tag=f"wr{q}_{g}")
            upi = q % 2
            em.op("sy", lambda s, q=q, upi=upi: s.dma_start(
                out=u_sb[:, upi * E:(upi + 1) * E], in_=u_dram[q * P:(q + 1) * P, :]),
                rd=[d_u[q]], wr=[b_usb[upi]], dma=True, tag=f"ur{q}")
            for g4 in range(8):
                bank, bbank = sc_ps[g4 % 2], b_scps[g4 % 2]
                for j in range(4):
                    kt = g4 * 4 + j
                    em.op("te", lambda t, kt=kt, j=j, q=q, bank=bank: t.matmul(
                        bank[:, j * P:(j + 1) * P], k_fm[:, kt * P:(kt + 1) * P],
                        q_fm[:, q * P:(q + 1) * P], start=True, stop=True),
                        rd=[b_kfm[kt], b_qfm[q]], wr=[bbank], accum=(j > 0), tag=f"sc{q}_{kt}")
                em.op("sc", lambda s, bank=bank, g4=g4: s.activation(
                    scf[:, (g4 % 2) * 512:(g4 % 2 + 1) * 512], bank[:], AF.Relu, scale=I4096),
                    rd=[bbank], wr=[b_scf2[g4 % 2]], deps=b_consts.rdeps(),
                    tag=f"se{q}_{g4}")
                em.op("ve", lambda v, g4=g4: v.tensor_tensor(
                    scT[:, g4 * 512:(g4 + 1) * 512], scf[:, (g4 % 2) * 512:(g4 % 2 + 1) * 512],
                    scf[:, (g4 % 2) * 512:(g4 % 2 + 1) * 512], ALU.mult),
                    rd=[b_scf2[g4 % 2]], wr=[b_scT[g4]], tag=f"sq{q}_{g4}")
            for kt in range(NKT):
                for eb in range(4):
                    em.op("te", lambda t, kt=kt, eb=eb: t.matmul(
                        av_ps[eb][:], scT[:, kt * P:(kt + 1) * P],
                        v_sb[:, kt * E + eb * 512: kt * E + eb * 512 + 512],
                        start=(kt == 0), stop=(kt == NKT - 1)),
                        rd=[b_scT[kt // 4], b_vsb[kt]], wr=[b_avps[eb]], accum=(kt > 0),
                        tag=f"av{q}_{eb}_{kt}")
            for eb in range(4):
                em.op("ve", lambda v, eb=eb, upi=upi: v.tensor_tensor(
                    m_sb[:, eb * 512:(eb + 1) * 512], av_ps[eb][:],
                    u_sb[:, upi * E + eb * 512: upi * E + eb * 512 + 512], ALU.mult),
                    rd=[b_avps[eb], b_usb[upi]], wr=[b_msb[eb]], tag=f"m{q}_{eb}")
            for g in range(2):
                mtb = mt_ps[g]
                wrb = [b_avps[2 * g], b_avps[2 * g + 1]]
                for j in range(8):
                    e = g * 8 + j
                    em.op("te", lambda t, e=e, j=j, mtb=mtb: t.transpose(
                        mtb[:, j * P:(j + 1) * P], m_sb[:, e * P:(e + 1) * P], ident[:]),
                        rd=[b_msb[e // 4], b_ident], wr=wrb, accum=(j > 0), tag=f"mt{q}_{e}")
                em.op("sc", lambda s, g=g, mtb=mtb: s.copy(mT_sb[:, g * 1024:(g + 1) * 1024],
                                                           mtb[:]),
                      rd=wrb, wr=[b_mTsb[g]], tag=f"me{q}_{g}")
            for half in range(2):
                em.op("te", lambda t, half=half: t.matmul(
                    pj_ps[half][:], ones1[:], bout_hl[0:1, half * 512:(half + 1) * 512],
                    start=True, stop=False),
                    rd=[b_ones, b_bout], wr=[b_pjps[half]], tag=f"ph{q}_{half}")
                em.op("te", lambda t, half=half: t.matmul(
                    pj_ps[half][:], ones1[:], bout_hl[0:1, D + half * 512: D + half * 512 + 512],
                    start=False, stop=False),
                    rd=[b_ones, b_bout], wr=[b_pjps[half]], accum=True, tag=f"pl{q}_{half}")
            for e in range(16):
                for half in range(2):
                    em.op("te", lambda t, e=e, half=half: t.matmul(
                        pj_ps[half][:], mT_sb[:, e * P:(e + 1) * P],
                        wo_sb[:, (e % 8) * D + half * 512: (e % 8) * D + half * 512 + 512],
                        start=False, stop=(e == 15)),
                        rd=[b_mTsb[e // 8], b_wosb[(e // 4) % 2]], wr=[b_pjps[half]], accum=True,
                        tag=f"pj{q}_{e}_{half}")
            em.op("sy", lambda s, q=q: s.dma_start(out=ep1[:], in_=xin_dram[q * P:(q + 1) * P, :]),
                  rd=[d_xin[q]], wr=[b_ep1], dma=True, tag=f"xir{q}")
            em.op("sy", lambda s, q=q: s.dma_start(out=ep2[:], in_=x_in[q * P:(q + 1) * P, :]),
                  wr=[b_ep2], dma=True, tag=f"xr{q}")
            for half in range(2):
                em.op("ve", lambda v, half=half: v.tensor_tensor(
                    ep1[:, half * 512:(half + 1) * 512], pj_ps[half][:],
                    ep1[:, half * 512:(half + 1) * 512], ALU.add),
                    rd=[b_pjps[half], b_ep1], wr=[b_ep1], accum=True, tag=f"ao{q}_{half}")
            layernorm(ep1[:], b_ep1, ep3[:], b_ep3, f"l3_{q}")
            em.op("ve", lambda v: v.tensor_tensor(ep2[:], ep2[:], ep3[:], ALU.add),
                  rd=[b_ep2, b_ep3], wr=[b_ep2], tag=f"y1_{q}")
            em.op("gp", lambda g, q=q: g.dma_start(out=y1_dram[q * P:(q + 1) * P, :], in_=ep2[:]),
                  rd=[b_ep2], wr=[d_y1[q]], dma=True, tag=f"y1s{q}")
            layernorm(ep2[:], b_ep2, ep3[:], b_ep3, f"l4_{q}")
            em.op("gp", lambda g, q=q: g.dma_start(out=y1n_dram[q * P:(q + 1) * P, :], in_=ep3[:]),
                  rd=[b_ep3], wr=[d_y1n[q]], dma=True, tag=f"y1ns{q}")
            em.op("ve", lambda v, q=q: v.tensor_reduce(y1n_mx[:, q:q + 1], ep3[:], axis=AX.X,
                                                       op=ALU.max, apply_absolute_value=True),
                  rd=[b_ep3], wr=[b_y1nmx], accum=True, tag=f"ymx{q}")
        ph.close()

    phase_a2()
    atn.close()
    em.barrier()

    def dump_y1():
        for t_ in range(NT):
            tok = em.op("gp", lambda g, t_=t_: g.dma_start(
                out=y_out[t_ * P:(t_ + 1) * P, :], in_=y1_dram[t_ * P:(t_ + 1) * P, :]),
                rd=[d_y1[t_]], dma=True, tag=f"dump_y{t_}")
            em.force_inc(tok)

    if STAGE == "a":
        dump_y1()

    barrier_n = [0]
    b_pmst = TBuf()

    def all_reduce_max(col_ap, b_col, nvals, tag):
        bi = barrier_n[0]
        barrier_n[0] += 1
        cc_in[bi] = nc.dram_tensor(f"cc_in{bi}", [1, nvals], F32)
        cc_out[bi] = nc.dram_tensor(f"cc_out{bi}", [1, nvals], F32)
        pm = mxsl[:, 300 + 2 * bi:302 + 2 * bi]
        em.op("gp", lambda g: g.tensor_reduce(pm[0:1, 0:nvals], col_ap, axis=AX.C, op=ALU.max),
              rd=[b_col], wr=[b_pmst], accum=True, tag=tag + "p")
        em.op("gp", lambda g: g.dma_start(out=cc_in[bi].ap()[0:1, 0:nvals], in_=pm[0:1, 0:nvals]),
              rd=[b_pmst], wr=[d_cc[bi]], dma=True, tag=tag + "w")
        em.op("gp", lambda g: g.collective_compute(
            "AllReduce", ALU.max, replica_groups=[list(range(8))],
            ins=[cc_in[bi].ap().opt()], outs=[cc_out[bi].ap().opt()]),
            rd=[d_cc[bi]], wr=[d_cc[bi]], accum=True, cc=True, tag=tag + "c")
        gcol = gsc[:, 4 * bi:4 * bi + nvals]
        b_g = TBuf()
        em.op("sy", lambda s: s.dma_start(out=gcol,
                                          in_=cc_out[bi].ap()[0, 0:nvals].partition_broadcast(P)),
              rd=[d_cc[bi]], wr=[b_g], dma=True, tag=tag + "b")
        return gcol, b_g

    def phase_b():
        ph = ExitStack()
        def _A(n, s, d):
            return ph.enter_context(nc.sbuf_tensor(n, s, d))
        def _Q(n, s, d):
            return ph.enter_context(nc.psum_tensor(n, s, d))
        xq_ps = _Q("xq_ps", [P, 1024], BF16)
        fm_ps = [_Q(f"fm_ps{i}", [P, 512], F32) for i in range(6)]
        b_xqps = TBuf()
        b_fmps = mk_bufs(6)

        ph_x = ExitStack()
        if True:
            xqT = ph_x.enter_context(nc.sbuf_tensor("xqT", [P, 8 * T], BF16))
            b_xqT = mk_bufs(8)

            wstg = ph_x.enter_context(nc.sbuf_tensor("wstg", [P, 2 * D], BF16))
            mrow = ph_x.enter_context(nc.sbuf_tensor("mrow", [P, 2 * E], F32))
            ytile = ph_x.enter_context(nc.sbuf_tensor("ytile", [P, 2 * D], F32))
            yq = ph_x.enter_context(nc.sbuf_tensor("yq", [P, D], BF16))
            ph2 = ExitStack()
            if True:
                wgT = ph2.enter_context(nc.sbuf_tensor("wgT", [P, 8 * I], BF16))
                b_wgT = mk_bufs(8)
                b_wstg2 = mk_bufs(2)
                b_mrow = mk_bufs(2)

                def wprep_chunk(base, wT_sb, b_wT, ci, name):
                    wp = ci % 2
                    wsa = wstg[:, wp * D:(wp + 1) * D]
                    off = base + (ci // 3) * 0
                    j, k = ci // 3, ci % 3
                    src_ap = wsh_out[j, base + k * 131072: base + (k + 1) * 131072]\
                        .rearrange("(p c) -> p c", p=P)
                    em.op("sy", lambda s, wsa=wsa, src_ap=src_ap: s.dma_start(
                        out=wsa, in_=src_ap),
                        rd=[b_ag], wr=[b_wstg2[wp]], dma=True, tag=f"{name}wl{ci}")
                    for dd in range(8):
                        em.op("te", lambda t, dd=dd, wsa=wsa: t.transpose(
                            xq_ps[:, dd * P:(dd + 1) * P], wsa[:, dd * P:(dd + 1) * P], ident[:]),
                            rd=[b_wstg2[wp], b_ident], wr=[b_xqps], accum=(dd > 0),
                            tag=f"{name}wt{ci}_{dd}")
                    wv3 = wT_sb.rearrange("p (d i) -> p d i", d=8)[:, :, ci * P:(ci + 1) * P]
                    em.op("ve", lambda v, wv3=wv3: v.tensor_copy(
                        wv3, xq_ps[:].rearrange("p (d i) -> p d i", d=8)),
                        rd=[b_xqps], wr=b_wT, accum=True, tag=f"{name}we{ci}")

                def mm1_ic(wT_sb, b_wT, dst_dram, d_dst, mx0, b_mx, bank_off, name, ic):
                    pi = ic % 2
                    rowa = mrow[:, pi * E:(pi + 1) * E]
                    bk0 = (ic % 3) * 2
                    for dd in range(8):
                        for tb in range(4):
                            bank = fm_ps[(bk0 + tb) % 6]
                            bbank = b_fmps[(bk0 + tb) % 6]
                            em.op("te", lambda t, dd=dd, ic=ic, tb=tb, bank=bank: t.matmul(
                                bank[:], wT_sb[:, dd * I + ic * P: dd * I + (ic + 1) * P],
                                xqT[:, dd * T + tb * 512: dd * T + tb * 512 + 512],
                                start=(dd == 0), stop=(dd == 7)),
                                rd=[b_wT[dd], b_xqT[dd]], wr=[bbank], accum=(dd > 0),
                                tag=f"{name}m{ic}_{tb}_{dd}")
                    for tb in range(4):
                        bank = fm_ps[(bk0 + tb) % 6]
                        bbank = b_fmps[(bk0 + tb) % 6]
                        em.op("sc", lambda s, tb=tb, bank=bank, rowa=rowa: s.copy(
                            rowa[:, tb * 512:(tb + 1) * 512], bank[:]),
                            rd=[bbank], wr=[b_mrow[pi]], accum=(tb > 0), tag=f"{name}e{ic}_{tb}")
                        em.op("ve", lambda v, tb=tb, ic=ic, rowa=rowa: v.tensor_reduce(
                            mx0[:, ic * 4 + tb:ic * 4 + tb + 1],
                            rowa[:, tb * 512:(tb + 1) * 512], axis=AX.X,
                            op=ALU.max, apply_absolute_value=True),
                            rd=[b_mrow[pi]], wr=[b_mx], accum=True, tag=f"{name}x{ic}_{tb}")
                    em.op("gp", lambda g, ic=ic, rowa=rowa: g.dma_start(
                        out=dst_dram[ic * P:(ic + 1) * P, :], in_=rowa),
                        rd=[b_mrow[pi]], wr=[d_dst[ic]], dma=True, tag=f"{name}s{ic}")

                for ci in range(8):
                    wprep_chunk(0, wgT, b_wgT, ci, "g")
                ymx_col = mxsl[:, 290:291]
                b_ymxc = TBuf()
                em.op("ve", lambda v: v.tensor_reduce(ymx_col, y1n_mx, axis=AX.X, op=ALU.max),
                      rd=[b_y1nmx], wr=[b_ymxc], tag="ymxr")
                g1col, b_g1 = all_reduce_max(ymx_col, b_ymxc, 1, "b1")
                s1, r1 = bstat[:, 0:1], bstat[:, 1:2]
                em.op("ve", lambda v: v.tensor_scalar(s1, g1col, C32767, 1e-12,
                                                      op0=ALU.mult, op1=ALU.max),
                      rd=[b_g1], wr=[b_bstat], accum=True, tag="s1")
                em.op("ve", lambda v: v.reciprocal(r1, s1), rd=[b_bstat], wr=[b_bstat],
                      accum=True, tag="r1")

                if True:
                    b_yt = mk_bufs(2)
                    b_yq = TBuf()
                    for t_ in range(NT):
                        if t_ + 8 < NIC:
                            wprep_chunk(0, wgT, b_wgT, t_ + 8, "g")
                        pi = t_ % 2
                        ya = ytile[:, pi * D:(pi + 1) * D]
                        em.op("sy", lambda s, t_=t_, ya=ya: s.dma_start(
                            out=ya, in_=y1n_dram[t_ * P:(t_ + 1) * P, :]),
                            rd=[d_y1n[t_]], wr=[b_yt[pi]], dma=True, tag=f"ynr{t_}")
                        em.op("ve", lambda v, ya=ya: v.tensor_scalar(ya, ya, r1, MAGIC,
                                                                     op0=ALU.mult, op1=ALU.add),
                              rd=[b_yt[pi], b_bstat], wr=[b_yt[pi]], tag=f"xq1_{t_}")
                        em.op("ve", lambda v, ya=ya: v.tensor_scalar(yq[:], ya, MAGIC, s1,
                                                                     op0=ALU.subtract, op1=ALU.mult),
                              rd=[b_yt[pi], b_bstat], wr=[b_yq], tag=f"xq2_{t_}")
                        for dd in range(8):
                            em.op("te", lambda t, dd=dd: t.transpose(
                                xq_ps[:, dd * P:(dd + 1) * P], yq[:, dd * P:(dd + 1) * P], ident[:]),
                                rd=[b_yq, b_ident], wr=[b_xqps], accum=(dd > 0), tag=f"xt{t_}_{dd}")
                        for dd in range(8):
                            em.op("sc", lambda s, dd=dd, t_=t_: s.copy(
                                xqT[:, dd * T + t_ * P: dd * T + (t_ + 1) * P],
                                xq_ps[:, dd * P:(dd + 1) * P]),
                                rd=[b_xqps], wr=[b_xqT[dd]], accum=True, tag=f"xe{t_}_{dd}")

                if _LVL[STAGE] <= 1:
                    dump_y1()
                    ph_x.close()
                    ph.close()
                    return
                if _LVL[STAGE] <= 1.3:
                    dump_y1()
                    ph2.close()
                    ph_x.close()
                    ph.close()
                    return
                gate_mx = mxsl[:, 16:112]
                b_gmx = TBuf()
                for ic in range(NIC):
                    mm1_ic(wgT, b_wgT, graw_dram, d_graw, gate_mx, b_gmx, 0, "g", ic)
                if _LVL[STAGE] <= 1.6:
                    dump_y1()
                    ph2.close()
                    ph_x.close()
                    ph.close()
                    return
                gmax_l = mxsl[:, 291:292]
                b_gmaxl = TBuf()
                em.op("ve", lambda v: v.tensor_reduce(gmax_l, gate_mx, axis=AX.X, op=ALU.max),
                      rd=[b_gmx], wr=[b_gmaxl], tag="gmr")
                ggcol, b_gg = all_reduce_max(gmax_l, b_gmaxl, 1, "b2")
                s_g, r_g = bstat[:, 2:3], bstat[:, 3:4]
                em.op("ve", lambda v: v.tensor_scalar(s_g, ggcol, C32767, 1e-12,
                                                      op0=ALU.mult, op1=ALU.max),
                      rd=[b_gg], wr=[b_bstat], accum=True, tag="sg")
                em.op("ve", lambda v: v.reciprocal(r_g, s_g), rd=[b_bstat], wr=[b_bstat],
                      accum=True, tag="rg")
                nmsg = bstat[:, 11:12]
                em.op("ve", lambda v: v.tensor_scalar(nmsg, s_g, -MAGIC, None, op0=ALU.mult),
                      rd=[b_bstat], wr=[b_bstat], accum=True, tag="nmsg")
                ph2.close()
                em.barrier()
                if _LVL[STAGE] <= 2:
                    dump_y1()

                if _LVL[STAGE] <= 2:
                    ph_x.close()
                    ph.close()
                    return
                ph2b = ExitStack()
                wuT = ph2b.enter_context(nc.sbuf_tensor("wuT", [P, 8 * I], BF16))
                cg_in = ph2b.enter_context(nc.sbuf_tensor("cg_in", [P, 2 * E], F32))
                cg_t = ph2b.enter_context(nc.sbuf_tensor("cg_t", [P, E], F32))
                cg_g1 = ph2b.enter_context(nc.sbuf_tensor("cg_g1", [P, E], F32))
                cg_sq = ph2b.enter_context(nc.sbuf_tensor("cg_sq", [P, E], F32))
                cg_out = ph2b.enter_context(nc.sbuf_tensor("cg_out", [P, 2 * E], F32))
                cg_sig = cg_t
                b_wuT = mk_bufs(8)
                b_cgin = mk_bufs(2)
                b_cgt = TBuf()
                b_cgg1, b_cgsq = TBuf(), TBuf()
                b_cgsig = b_cgt
                b_cgout = mk_bufs(2)

                for ci in range(NIC):
                    wprep_chunk(3 * 131072, wuT, b_wuT, ci, "u")

                up_mx = mxsl[:, 112:208]
                b_umx = TBuf()
                gpmx_slots = mxsl[:, 208:232]
                b_gpsl = TBuf()

                def chain_ic(ic):
                    pi = ic % 2
                    ina = cg_in[:, pi * E:(pi + 1) * E]
                    outa = cg_out[:, pi * E:(pi + 1) * E]
                    em.op("sy", lambda s, ic=ic, ina=ina: s.dma_start(
                        out=ina, in_=graw_dram[ic * P:(ic + 1) * P, :]),
                        rd=[d_graw[ic]], wr=[b_cgin[pi]], dma=True, tag=f"cl{ic}")
                    em.op("sc", lambda s, ina=ina: s.activation(cg_t[:], ina, AF.Identity,
                                                                scale=r_g, bias=MAGIC),
                          rd=[b_cgin[pi], b_bstat], wr=[b_cgt],
                          deps=b_consts.rdeps(), tag=f"ct{ic}")
                    em.op("sc", lambda s: s.activation(cg_g1[:], cg_t[:], AF.Identity,
                                                       scale=s_g, bias=nmsg),
                          rd=[b_cgt, b_bstat], wr=[b_cgg1], tag=f"cg{ic}")
                    em.op("sc", lambda s: s.activation(cg_sig[:], cg_g1[:], AF.Sigmoid),
                          rd=[b_cgg1], wr=[b_cgsig], tag=f"cs{ic}")
                    em.op("sc", lambda s: s.activation(cg_sq[:], cg_sig[:], AF.Identity,
                                                       bias=MAGIC, scale=65536.0),
                          rd=[b_cgsig], wr=[b_cgsq], deps=b_consts.rdeps(), tag=f"c2{ic}")
                    em.op("ve", lambda v: v.tensor_scalar(cg_sq[:], cg_sq[:], MAGIC, 65535.0,
                                                          op0=ALU.subtract, op1=ALU.min),
                          rd=[b_cgsq], wr=[b_cgsq], tag=f"cm{ic}")
                    em.op("ve", lambda v, outa=outa: v.tensor_tensor(outa, cg_g1[:], cg_sq[:],
                                                                     ALU.mult),
                          rd=[b_cgg1, b_cgsq], wr=[b_cgout[pi]], tag=f"cp{ic}")
                    em.op("ve", lambda v, ic=ic, outa=outa: v.tensor_reduce(
                        gpmx_slots[:, ic:ic + 1], outa, axis=AX.X, op=ALU.max,
                        apply_absolute_value=True),
                        rd=[b_cgout[pi]], wr=[b_gpsl], accum=True, tag=f"cx{ic}")
                    em.op("gp", lambda g, ic=ic, outa=outa: g.dma_start(
                        out=gpint_dram[ic * P:(ic + 1) * P, :], in_=outa),
                        rd=[b_cgout[pi]], wr=[d_gpint[ic]], dma=True, tag=f"cw{ic}")

                for ic in range(NIC):
                    mm1_ic(wuT, b_wuT, uraw_dram, d_uraw, up_mx, b_umx, 2, "u", ic)
                    chain_ic(ic)

                gp_mx = mxsl[:, 292:293]
                b_gpmx = TBuf()
                em.op("ve", lambda v: v.tensor_reduce(gp_mx, gpmx_slots, axis=AX.X, op=ALU.max),
                      rd=[b_gpsl], wr=[b_gpmx], tag="gpr")
                umax_l = mxsl[:, 293:294]
                b_umaxl = TBuf()
                em.op("ve", lambda v: v.tensor_reduce(umax_l, up_mx, axis=AX.X, op=ALU.max),
                      rd=[b_umx], wr=[b_umaxl], tag="umr")
                both = mxsl[:, 294:296]
                b_both = TBuf()
                em.op("ve", lambda v: v.tensor_copy(both[:, 0:1], gp_mx), rd=[b_gpmx],
                      wr=[b_both], accum=True, tag="bb0")
                em.op("ve", lambda v: v.tensor_copy(both[:, 1:2], umax_l), rd=[b_umaxl],
                      wr=[b_both], accum=True, tag="bb1")
                g2col, b_g2 = all_reduce_max(both, b_both, 2, "b3")
                s_gp, r_gp, rg2 = bstat[:, 4:5], bstat[:, 5:6], bstat[:, 6:7]
                s_up, r_up = bstat[:, 7:8], bstat[:, 8:9]
                em.op("ve", lambda v: v.tensor_scalar(s_gp, g2col[:, 0:1], C32767 * 1.0, 0.0,
                                                      op0=ALU.mult, op1=ALU.max),
                      rd=[b_g2], wr=[b_bstat], accum=True, tag="sgp0")
                em.op("ve", lambda v: v.tensor_scalar(s_gp, s_gp, 2.0 ** -16, 1e-12,
                                                      op0=ALU.mult, op1=ALU.max),
                      rd=[b_bstat], wr=[b_bstat], accum=True, tag="sgp1")
                em.op("ve", lambda v: v.reciprocal(r_gp, s_gp), rd=[b_bstat], wr=[b_bstat],
                      accum=True, tag="rgp")
                em.op("ve", lambda v: v.tensor_scalar(rg2, r_gp, 2.0 ** -16, None, op0=ALU.mult),
                      rd=[b_bstat], wr=[b_bstat], accum=True, tag="rg2")
                em.op("ve", lambda v: v.tensor_scalar(s_up, g2col[:, 1:2], C32767, 1e-12,
                                                      op0=ALU.mult, op1=ALU.max),
                      rd=[b_g2], wr=[b_bstat], accum=True, tag="su")
                em.op("ve", lambda v: v.reciprocal(r_up, s_up), rd=[b_bstat], wr=[b_bstat],
                      accum=True, tag="ru")
                nm_gp = bstat[:, 12:13]
                nm_up = bstat[:, 13:14]
                em.op("ve", lambda v: v.tensor_scalar(nm_gp, s_gp, -MAGIC, None, op0=ALU.mult),
                      rd=[b_bstat], wr=[b_bstat], accum=True, tag="nmgp")
                em.op("ve", lambda v: v.tensor_scalar(nm_up, s_up, -MAGIC, None, op0=ALU.mult),
                      rd=[b_bstat], wr=[b_bstat], accum=True, tag="nmup")
            ph2b.close()
            ph_x.close()
            em.barrier()
            if _LVL[STAGE] <= 3:
                dump_y1()
                ph.close()
                return

            wdT = _A("wdT", [P, 24 * D], BF16)
            wdstg = _A("wdstg", [P, I], BF16)
            b_wdT = mk_bufs(24)
            b_wdstg = TBuf()
            for hc in range(8):
                src_ap = wsh_out[:, 6 * 131072 + hc * 49152:
                                 6 * 131072 + (hc + 1) * 49152]\
                    .rearrange("r (p c) -> p r c", p=P)
                em.op("sy", lambda s, hc=hc, src_ap=src_ap: s.dma_start(
                    out=wdstg[:].rearrange("p (r c) -> p r c", r=8), in_=src_ap),
                    rd=[b_ag], wr=[b_wdstg], dma=True, tag=f"wdl{hc}")
                for g in range(3):
                    for j in range(8):
                        ic = g * 8 + j
                        em.op("te", lambda t, ic=ic, j=j: t.transpose(
                            xq_ps[:, j * P:(j + 1) * P], wdstg[:, ic * P:(ic + 1) * P],
                            ident[:]),
                            rd=[b_wdstg, b_ident], wr=[b_xqps], accum=(j > 0),
                            tag=f"wdt{hc}_{ic}")
                    for j in range(8):
                        ic = g * 8 + j
                        em.op("sc", lambda s, ic=ic, j=j, hc=hc: s.copy(
                            wdT[:, ic * D + hc * P: ic * D + (hc + 1) * P],
                            xq_ps[:, j * P:(j + 1) * P]),
                            rd=[b_xqps], wr=[b_wdT[ic]], accum=True, tag=f"wde{hc}_{ic}")


            ph3 = ExitStack()
            if True:
                e2_g = ph3.enter_context(nc.sbuf_tensor("e2_g", [P, 2 * E], F32))
                e2_u = ph3.enter_context(nc.sbuf_tensor("e2_u", [P, 2 * E], F32))
                e2_tg = ph3.enter_context(nc.sbuf_tensor("e2_tg", [P, E], F32))
                e2_tu = ph3.enter_context(nc.sbuf_tensor("e2_tu", [P, E], F32))
                e2_o = ph3.enter_context(nc.sbuf_tensor("e2_o", [P, 2 * E], F32))
                b_e2g = mk_bufs(2)
                b_e2u = mk_bufs(2)
                b_e2tg, b_e2tu = TBuf(), TBuf()
                b_e2o = mk_bufs(2)
                oint_slots = mxsl[:, 232:256]
                b_osl = TBuf()
                s_gp, rg2 = bstat[:, 4:5], bstat[:, 6:7]
                s_up, r_up = bstat[:, 7:8], bstat[:, 8:9]
                nm_gp, nm_up = bstat[:, 12:13], bstat[:, 13:14]
                for ic in range(NIC):
                    pi = ic % 2
                    ga = e2_g[:, pi * E:(pi + 1) * E]
                    ua = e2_u[:, pi * E:(pi + 1) * E]
                    oa = e2_o[:, pi * E:(pi + 1) * E]
                    em.op("sy", lambda s, ic=ic, ga=ga: s.dma_start(
                        out=ga, in_=gpint_dram[ic * P:(ic + 1) * P, :]),
                        rd=[d_gpint[ic]], wr=[b_e2g[pi]], dma=True, tag=f"e2g{ic}")
                    em.op("sy", lambda s, ic=ic, ua=ua: s.dma_start(
                        out=ua, in_=uraw_dram[ic * P:(ic + 1) * P, :]),
                        rd=[d_uraw[ic]], wr=[b_e2u[pi]], dma=True, tag=f"e2u{ic}")
                    em.op("sc", lambda s, ga=ga: s.activation(e2_tg[:], ga, AF.Identity,
                                                              scale=rg2, bias=MAGIC),
                          rd=[b_e2g[pi], b_bstat], wr=[b_e2tg],
                          deps=b_consts.rdeps(), tag=f"etg{ic}")
                    em.op("sc", lambda s: s.activation(e2_tg[:], e2_tg[:], AF.Identity,
                                                       scale=s_gp, bias=nm_gp),
                          rd=[b_e2tg, b_bstat], wr=[b_e2tg], tag=f"eg2{ic}")
                    em.op("sc", lambda s, ua=ua: s.activation(e2_tu[:], ua, AF.Identity,
                                                              scale=r_up, bias=MAGIC),
                          rd=[b_e2u[pi], b_bstat], wr=[b_e2tu], tag=f"etu{ic}")
                    em.op("ve", lambda v: v.tensor_scalar(e2_tu[:], e2_tu[:], MAGIC, s_up,
                                                          op0=ALU.subtract, op1=ALU.mult),
                          rd=[b_e2tu, b_bstat], wr=[b_e2tu], tag=f"eu2{ic}")
                    em.op("ve", lambda v, oa=oa: v.tensor_tensor(oa, e2_tg[:], e2_tu[:], ALU.mult),
                          rd=[b_e2tg, b_e2tu], wr=[b_e2o[pi]], tag=f"eo{ic}")
                    em.op("ve", lambda v, ic=ic, oa=oa: v.tensor_reduce(
                        oint_slots[:, ic:ic + 1], oa, axis=AX.X, op=ALU.max,
                        apply_absolute_value=True),
                        rd=[b_e2o[pi]], wr=[b_osl], accum=True, tag=f"ex{ic}")
                    em.op("gp", lambda g, ic=ic, oa=oa: g.dma_start(
                        out=oint_dram[ic * P:(ic + 1) * P, :], in_=oa),
                        rd=[b_e2o[pi]], wr=[d_oint[ic]], dma=True, tag=f"es{ic}")
                omax_l = mxsl[:, 296:297]
                b_omaxl = TBuf()
                em.op("ve", lambda v: v.tensor_reduce(omax_l, oint_slots, axis=AX.X, op=ALU.max),
                      rd=[b_osl], wr=[b_omaxl], tag="omr")
                gocol, b_go = all_reduce_max(omax_l, b_omaxl, 1, "b4")
                s_o, r_o = bstat[:, 9:10], bstat[:, 10:11]
                em.op("ve", lambda v: v.tensor_scalar(s_o, gocol, C32767, 1e-12,
                                                      op0=ALU.mult, op1=ALU.max),
                      rd=[b_go], wr=[b_bstat], accum=True, tag="so")
                em.op("ve", lambda v: v.reciprocal(r_o, s_o), rd=[b_bstat], wr=[b_bstat],
                      accum=True, tag="ro")
            ph3.close()
            em.barrier()
            if _LVL[STAGE] <= 4:
                dump_y1()
                ph.close()
                return

            if True:
                dn_ps0 = fm_ps[4]
                dn_ps1 = fm_ps[5]
                obf = _A("obf", [P, 24 * T], BF16)
                e3_in = _A("e3_in", [P, 2 * E], F32)
                e3_t = _A("e3_t", [P, E], F32)
                f_out = _A("f_out", [P, D], F32)
                f_t = _A("f_t", [P, D], F32)
                f_y1 = _A("f_y1", [P, D], F32)
                stats_b = _A("stats_b", [P, 300], F32)
                stats_holder[0] = stats_b
                sc_slot[0] = 0
                dn_ps = [dn_ps0, dn_ps1]
                b_dnps = mk_bufs(2)
                b_obf = mk_bufs(24)
                b_e3in = mk_bufs(2)
                b_e3t = TBuf()
                b_fout, b_ft, b_fy1 = TBuf(), TBuf(), TBuf()
                s_o, r_o = bstat[:, 9:10], bstat[:, 10:11]

                for ic in range(NIC):
                    pi = ic % 2
                    ina = e3_in[:, pi * E:(pi + 1) * E]
                    em.op("sy", lambda s, ic=ic, ina=ina: s.dma_start(
                        out=ina, in_=oint_dram[ic * P:(ic + 1) * P, :]),
                        rd=[d_oint[ic]], wr=[b_e3in[pi]], dma=True, tag=f"e3l{ic}")
                    em.op("ve", lambda v, ina=ina: v.tensor_scalar(e3_t[:], ina, r_o, MAGIC,
                                                                   op0=ALU.mult, op1=ALU.add),
                          rd=[b_e3in[pi], b_bstat], wr=[b_e3t], tag=f"e3t{ic}")
                    em.op("ve", lambda v, ic=ic: v.tensor_scalar(
                        obf[:, ic * T:(ic + 1) * T], e3_t[:], MAGIC, None, op0=ALU.subtract),
                        rd=[b_e3t], wr=[b_obf[ic]], tag=f"e3q{ic}")

                for t_ in range(NT):
                    for ic in range(24):
                        for half in range(2):
                            em.op("te", lambda t, ic=ic, t_=t_, half=half: t.matmul(
                                dn_ps[half][:], obf[:, ic * T + t_ * P: ic * T + (t_ + 1) * P],
                                wdT[:, ic * D + half * 512: ic * D + half * 512 + 512],
                                start=(ic == 0), stop=(ic == 23)),
                                rd=[b_obf[ic], b_wdT[ic]], wr=[b_dnps[half]], accum=(ic > 0),
                                tag=f"dn{t_}_{half}_{ic}")
                    for half in range(2):
                        em.op("ve", lambda v, half=half: v.tensor_scalar(
                            f_out[:, half * 512:(half + 1) * 512], dn_ps[half][:], s_o, None,
                            op0=ALU.mult),
                            rd=[b_dnps[half], b_bstat], wr=[b_fout], accum=(half > 0),
                            tag=f"fo{t_}_{half}")
                    layernorm(f_out[:], b_fout, f_t[:], b_ft, f"l5_{t_}")
                    em.op("sy", lambda s, t_=t_: s.dma_start(
                        out=f_y1[:], in_=y1_dram[t_ * P:(t_ + 1) * P, :]),
                        rd=[d_y1[t_]], wr=[b_fy1], dma=True, tag=f"fyl{t_}")
                    em.op("ve", lambda v: v.tensor_tensor(f_t[:], f_t[:], f_y1[:], ALU.add),
                          rd=[b_ft, b_fy1], wr=[b_ft], tag=f"y2_{t_}")
                    tok = em.op("gp", lambda g, t_=t_: g.dma_start(
                        out=y_out[t_ * P:(t_ + 1) * P, :], in_=f_t[:]),
                        rd=[b_ft], dma=True, tag=f"ys{t_}")
                    em.force_inc(tok)
        ph.close()

    if STAGE != "a":
        phase_b()

    st.close()

    em.finalize()
    from contextlib import ExitStack as _ES
    semstack = _ES()
    sems = {}
    for name in em.sem_names():
        sems[name] = semstack.enter_context(nc.semaphore("s_" + name))
    with nc.Block() as block:
        @block.sync
        def _(sync):
            em.replay("sy", sync, sems)

        @block.scalar
        def _(scalar):
            em.replay("sc", scalar, sems)

        @block.vector
        def _(vector):
            em.replay("ve", vector, sems)

        @block.tensor
        def _(tensor):
            em.replay("te", tensor, sems)

        @block.gpsimd
        def _(gpsimd):
            em.replay("gp", gpsimd, sems)
            for s in range(Em.NSLOT):
                name = f"gp_d{s}"
                if em.final_vals.get(name):
                    gpsimd.wait_ge(sems[name], em.final_vals[name])
    semstack.close()

    return nc


_NC_CACHE = {}


def kernel(**inputs):
    if "prog" not in _NC_CACHE:
        _NC_CACHE["prog"] = build_program()
    nc = _NC_CACHE["prog"]

    xf = np.ascontiguousarray(np.asarray(inputs["L0_attn_x"], dtype=np.float32))
    shared = {
        "ln_g": np.ascontiguousarray(np.asarray(inputs["L0_attn_ln_g"], np.float32)),
        "ln_b": np.ascontiguousarray(np.asarray(inputs["L0_attn_ln_b"], np.float32)),
        "W_in": np.ascontiguousarray(np.asarray(inputs["L0_attn_W_in"], np.float32)),
        "b_in": np.ascontiguousarray(np.asarray(inputs["L0_attn_b_in"], np.float32)),
        "W_out": np.ascontiguousarray(np.asarray(inputs["L0_attn_W_out"], np.float32)),
        "b_out": np.ascontiguousarray(np.asarray(inputs["L0_attn_b_out"], np.float32)),
        "g_q": np.ascontiguousarray(np.asarray(inputs["L0_attn_gamma_q"], np.float32)),
        "be_q": np.ascontiguousarray(np.asarray(inputs["L0_attn_beta_q"], np.float32)),
        "g_k": np.ascontiguousarray(np.asarray(inputs["L0_attn_gamma_k"], np.float32)),
        "be_k": np.ascontiguousarray(np.asarray(inputs["L0_attn_beta_k"], np.float32)),
    }
    wg = np.asarray(inputs["L0_ffn_w_gate"], np.float32)
    wu = np.asarray(inputs["L0_ffn_w_up"], np.float32)
    wd = np.asarray(inputs["L0_ffn_w_down"], np.float32)
    in_maps = []
    for c in range(8):
        b, h = c // 2, c % 2
        xb = xf[b]
        xr = np.concatenate([xb[h * T:(h + 1) * T], xb[(1 - h) * T:(2 - h) * T]], axis=0)
        m = {"x": np.ascontiguousarray(xr),
             "wg_sh": np.ascontiguousarray(wg[c * 384:(c + 1) * 384]),
             "wu_sh": np.ascontiguousarray(wu[c * 384:(c + 1) * 384]),
             "wd_sh": np.ascontiguousarray(wd[:, c * 384:(c + 1) * 384])}
        m.update(shared)
        in_maps.append(m)

    res = run_bass_kernel_spmd(nc, in_maps, core_ids=list(range(8)))
    out = np.empty((4, L, D), dtype=np.float32)
    for c in range(8):
        b, h = c // 2, c % 2
        out[b, h * T:(h + 1) * T] = res.results[c]["y"]
    return out


# version 20
# speedup vs baseline: 1.0241x; 1.0082x over previous
import numpy as np
import ml_dtypes
import concourse.bass as bass
import concourse.mybir as mybir
import bass_rust
from concourse import library_config
from concourse.bass_utils import run_bass_kernel_spmd

F32 = mybir.dt.float32
BF16 = mybir.dt.bfloat16
AF = mybir.ActivationFunctionType
ALU = mybir.AluOpType
AX = mybir.AxisListType

D, E, S, I, L, T, P = 1024, 2048, 128, 3072, 4096, 2048, 128
NT = T // P
NKT = L // P
NIC = I // P
LN_EPS = 1e-5
MAGIC = 12582912.0
C7 = float(np.float32(1.0 / 7.0))
C32767 = float(np.float32(1.0 / 32767.0))
I4096 = float(np.float32(1.0 / 4096.0))

DEBUG_TAPS = False
STAGE = "full"
_LVL = {"a": 0, "b1": 1, "bw": 1.3, "bg": 1.6, "bmm": 2, "bchain": 3, "be2": 4, "full": 9}


class TBuf:
    __slots__ = ("producers", "readers")

    def __init__(self):
        self.producers = []
        self.readers = []

    def rdeps(self):
        return list(self.producers)

    def wdeps(self):
        return self.producers + self.readers

    def wrote(self, tok, reset=True):
        if reset:
            self.producers = [tok]
            self.readers = []
        else:
            self.producers.append(tok)

    def read(self, tok):
        self.readers.append(tok)


def mk_bufs(n):
    return [TBuf() for _ in range(n)]


class Em:
    ENGS = ("sy", "sc", "ve", "te", "gp")
    NSLOT = 8

    def __init__(self):
        self.ops = {e: [] for e in self.ENGS}
        self.barriers = []

    def barrier(self):
        snap = {}
        for eng in self.ENGS:
            if self.ops[eng]:
                self.ops[eng][-1]["inc"] = True
                snap[eng] = len(self.ops[eng])
        self.barriers.append(snap)

    def emit(self, eng, fn, deps=(), dma=False, cc=False, tag=""):
        idx = len(self.ops[eng])
        self.ops[eng].append({"fn": fn, "deps": [d for d in deps if d is not None],
                              "dma": dma, "cc": cc, "inc": False, "sem": None,
                              "val": 0, "prewait": None, "tag": tag,
                              "bar": len(self.barriers)})
        return (eng, idx)

    def op(self, eng, fn, rd=(), wr=(), deps=(), dma=False, cc=False, tag="", accum=False):
        dd = list(deps)
        for b in rd:
            dd += b.rdeps()
        for b in wr:
            dd += (b.rdeps() if accum else b.wdeps())
        best = {}
        keep = set()
        for d in dd:
            if d is None:
                continue
            pe, pi = d
            o = self.ops[pe][pi]
            if o["dma"] or o["cc"]:
                keep.add(d)
            elif pe == eng == "te":
                continue
            elif pi > best.get(pe, -1):
                best[pe] = pi
        dd = list(keep) + [(pe, pi) for pe, pi in best.items()]
        tok = self.emit(eng, fn, dd, dma=dma, cc=cc, tag=tag)
        for b in rd:
            b.read(tok)
        for b in wr:
            b.wrote(tok, reset=not accum)
        return tok

    def force_inc(self, tok):
        self.ops[tok[0]][tok[1]]["inc"] = True

    def finalize(self):
        for eng in self.ENGS:
            for o in self.ops[eng]:
                if o["dma"] or o["cc"]:
                    o["inc"] = True
                for (pe, pi) in o["deps"]:
                    self.ops[pe][pi]["inc"] = True
        self.final_vals = {}
        cc_n = 0
        for eng in self.ENGS:
            cum = 0
            dk = 0
            for o in self.ops[eng]:
                if o["cc"]:
                    slot = cc_n % 4
                    o["sem"] = f"cc{slot}"
                    o["val"] = cc_n // 4 + 1
                    if cc_n >= 4:
                        o["prewait"] = (o["sem"], o["val"] - 1)
                    cc_n += 1
                elif o["dma"]:
                    slot = dk % self.NSLOT
                    o["sem"] = f"{eng}_d{slot}"
                    o["val"] = 16 * (dk // self.NSLOT + 1)
                    if dk >= self.NSLOT:
                        o["prewait"] = (o["sem"], o["val"] - 16)
                    dk += 1
                elif o["inc"]:
                    cum += 1
                    o["sem"] = eng
                    o["val"] = cum
                if o["sem"] is not None:
                    self.final_vals[o["sem"]] = o["val"]
        self.bar_waits = []
        for snap in self.barriers:
            w = {}
            for eng, bound in snap.items():
                for o in self.ops[eng][:bound]:
                    if o["sem"] is not None:
                        w[o["sem"]] = max(w.get(o["sem"], 0), o["val"])
            self.bar_waits.append(w)

    def sem_names(self):
        names = list(self.ENGS) + [f"cc{j}" for j in range(4)]
        for eng in self.ENGS:
            for s in range(self.NSLOT):
                names.append(f"{eng}_d{s}")
        return names

    def replay(self, eng_name, eng, sems):
        watermark = {}
        cur_bar = 0
        for o in self.ops[eng_name]:
            need = {}
            if o["bar"] > cur_bar:
                for b in range(cur_bar, o["bar"]):
                    for sname, v in self.bar_waits[b].items():
                        if v > need.get(sname, 0):
                            need[sname] = v
                cur_bar = o["bar"]
            for (pe, pi) in o["deps"]:
                p = self.ops[pe][pi]
                if p["sem"] is None:
                    continue
                if p["val"] > need.get(p["sem"], 0):
                    need[p["sem"]] = p["val"]
            if o["prewait"] is not None:
                sname, v = o["prewait"]
                if v > need.get(sname, 0):
                    need[sname] = v
            for sname, v in sorted(need.items()):
                if v > watermark.get(sname, 0):
                    eng.wait_ge(sems[sname], v)
                    watermark[sname] = v
            inst = o["fn"](eng)
            if o["inc"]:
                assert inst is not None, f"inc op returned None: {o['tag']}"
                inst.then_inc(sems[o["sem"]], 16 if o["dma"] else 1)


def build_program():
    nc = bass.Bass("TRN2", target_bir_lowering=False)
    em = Em()
    dkind = dict(kind="ExternalOutput") if DEBUG_TAPS else {}

    x_in = nc.dram_tensor("x", [L, D], F32, kind="ExternalInput").ap()
    ln_g = nc.dram_tensor("ln_g", [D], F32, kind="ExternalInput").ap()
    ln_b = nc.dram_tensor("ln_b", [D], F32, kind="ExternalInput").ap()
    W_in = nc.dram_tensor("W_in", [D, 2 * E + S], F32, kind="ExternalInput").ap()
    b_in_d = nc.dram_tensor("b_in", [2 * E + S], F32, kind="ExternalInput").ap()
    W_out = nc.dram_tensor("W_out", [E, D], F32, kind="ExternalInput").ap()
    b_out_d = nc.dram_tensor("b_out", [D], F32, kind="ExternalInput").ap()
    g_q = nc.dram_tensor("g_q", [S], F32, kind="ExternalInput").ap()
    be_q = nc.dram_tensor("be_q", [S], F32, kind="ExternalInput").ap()
    g_k = nc.dram_tensor("g_k", [S], F32, kind="ExternalInput").ap()
    be_k = nc.dram_tensor("be_k", [S], F32, kind="ExternalInput").ap()
    wg_sh = nc.dram_tensor("wg_sh", [384, D], F32, kind="ExternalInput").ap()
    wu_sh = nc.dram_tensor("wu_sh", [384, D], F32, kind="ExternalInput").ap()
    wd_sh = nc.dram_tensor("wd_sh", [D, 384], F32, kind="ExternalInput").ap()
    y_out = nc.dram_tensor("y", [T, D], F32, kind="ExternalOutput").ap()

    u_dram = nc.dram_tensor("u_sc", [T, E], BF16, **dkind).ap()
    v_dram = nc.dram_tensor("v_sc", [L, E], BF16, **dkind).ap()
    xin_dram = nc.dram_tensor("xin_sc", [T, D], F32, **dkind).ap()
    y1_dram = nc.dram_tensor("y1_sc", [T, D], F32, **dkind).ap()
    y1n_dram = nc.dram_tensor("y1n_sc", [T, D], F32, **dkind).ap()
    wout_dram = nc.dram_tensor("woutbf_sc", [E, D], BF16).ap()
    SHN = 3 * 131072 * 2 + 8 * 49152
    wsh_in = nc.dram_tensor("wsh_in", [1, SHN], BF16).ap()
    wsh_out = nc.dram_tensor("wsh_out", [8, SHN], BF16).ap()
    VKG = 4 * P * E
    VKT = VKG + P * 512
    vkg_in = [nc.dram_tensor(f"vkg_in{g}", [1, VKT], BF16).ap() for g in range(4)]
    vkg_out = [nc.dram_tensor(f"vkg_out{g}", [2, VKT], BF16).ap() for g in range(4)]
    graw_dram = nc.dram_tensor("graw_sc", [I, T], F32, **dkind).ap()
    uraw_dram = nc.dram_tensor("uraw_sc", [I, T], F32, **dkind).ap()
    gpint_dram = nc.dram_tensor("gpint_sc", [I, T], F32, **dkind).ap()
    oint_dram = nc.dram_tensor("oint_sc", [I, T], F32, **dkind).ap()
    cc_in = {}
    cc_out = {}

    d_u = mk_bufs(NT)
    d_v = mk_bufs(NKT)
    d_xin = mk_bufs(NT)
    d_y1 = mk_bufs(NT)
    d_y1n = mk_bufs(NT)
    d_wout = mk_bufs(16)
    d_wsh = TBuf()
    b_ag = TBuf()
    d_vk = [TBuf() for _ in range(4)]
    b_vkag = [TBuf() for _ in range(4)]
    d_graw = mk_bufs(NIC)
    d_uraw = mk_bufs(NIC)
    d_gpint = mk_bufs(NIC)
    d_oint = mk_bufs(NIC)
    d_cc = mk_bufs(4)

    from contextlib import ExitStack
    st = ExitStack()
    b_y1nmx = TBuf()

    def sb(name, shape, dt):
        return st.enter_context(nc.sbuf_tensor(name, shape, dt))

    consts = sb("consts", [P, 4], F32)
    ident = sb("ident", [P, P], BF16)
    ones1 = sb("ones1", [1, P], BF16)
    gb_cols = sb("gb_cols", [P, 5], F32)
    lnT = sb("lnT", [P, 16], F32)
    mxsl = sb("mxsl", [P, 320], F32)
    gsc = sb("gsc", [P, 16], F32)
    bstat = sb("bstat", [P, 16], F32)
    bout_hl = sb("bout_hl", [1, 2 * D], BF16)
    atn = ExitStack()
    k_fm = atn.enter_context(nc.sbuf_tensor("k_fm", [P, L], BF16))
    q_fm = atn.enter_context(nc.sbuf_tensor("q_fm", [P, T], BF16))

    CONST_EPS = consts[:, 0:1]

    nc.const_aps.register(F32, LN_EPS, consts[:, 0:1])
    nc.const_aps.register(F32, MAGIC, consts[:, 1:2])
    nc.const_aps.register(F32, 65536.0, consts[:, 2:3])
    nc.const_aps.register(F32, I4096, consts[:, 3:4])

    b_consts = TBuf()
    b_ident = TBuf()
    b_ones = TBuf()
    b_gb = TBuf()
    b_lnT = TBuf()
    b_bout = TBuf()
    b_bstage = TBuf()
    b_bstat = TBuf()
    b_kfm = mk_bufs(NKT)
    b_qfm = mk_bufs(NT)

    em.op("gp", lambda g: g.memset(consts[:, 0:1], LN_EPS), wr=[b_consts], accum=True, tag="c0")
    em.op("gp", lambda g: g.memset(consts[:, 1:2], MAGIC), wr=[b_consts], accum=True, tag="c1")
    em.op("gp", lambda g: g.memset(consts[:, 2:3], 65536.0), wr=[b_consts], accum=True, tag="c2")
    em.op("gp", lambda g: g.memset(consts[:, 3:4], I4096), wr=[b_consts], accum=True, tag="c3")
    em.op("gp", lambda g: g.memset(ident[:], 0.0), wr=[b_ident], tag="id0")
    em.op("gp", lambda g: g.affine_select(out=ident[:], in_=ident[:], compare_op=ALU.not_equal,
                                          fill=1.0, base=0, pattern=[[-1, P]], channel_multiplier=1),
          rd=[b_ident], wr=[b_ident], tag="id1")
    em.op("gp", lambda g: g.memset(ones1[:], 1.0), wr=[b_ones], tag="ones")

    for j, src in enumerate([g_q, be_q, g_k, be_k]):
        em.op("sy", lambda s, j=j, src=src: s.dma_start(out=gb_cols[:, j:j + 1], in_=src[:, None]),
              wr=[b_gb], dma=True, accum=True, tag=f"gb{j}")
    em.op("sy", lambda s: s.dma_start(out=gb_cols[:, 4:5], in_=b_in_d[2 * E:2 * E + S, None]),
          wr=[b_gb], dma=True, accum=True, tag="bz")
    em.op("sy", lambda s: s.dma_start(out=lnT[:, 0:8], in_=ln_g.rearrange("(c p) -> p c", p=P),
                                        allow_slow_non_contiguous=True),
          wr=[b_lnT], dma=True, accum=True, tag="lng")
    em.op("sy", lambda s: s.dma_start(out=lnT[:, 8:16], in_=ln_b.rearrange("(c p) -> p c", p=P),
                                        allow_slow_non_contiguous=True),
          wr=[b_lnT], dma=True, accum=True, tag="lnb")
    sc_slot = [0]
    stats_holder = [None]

    def layernorm(x_ap, b_x, out_ap, b_out_t, tag):
        stats = stats_holder[0]
        s0 = sc_slot[0]
        sc_slot[0] += 18
        assert sc_slot[0] <= stats.shape[1]
        bns = stats[:, s0:s0 + 12]
        mv = stats[:, s0 + 12:s0 + 14]
        sd = stats[:, s0 + 14:s0 + 15]
        rstd = stats[:, s0 + 15:s0 + 16]
        nmr = stats[:, s0 + 16:s0 + 17]
        b_bns, b_mv, b_sd, b_rstd, b_nmr = TBuf(), TBuf(), TBuf(), TBuf(), TBuf()
        em.op("ve", lambda v: v.bn_stats(bns[:, 0:6], x_ap[:, 0:512]),
              rd=[b_x], wr=[b_bns], accum=True, tag=tag + "b0")
        em.op("ve", lambda v: v.bn_stats(bns[:, 6:12], x_ap[:, 512:1024]),
              rd=[b_x], wr=[b_bns], accum=True, tag=tag + "b1")
        em.op("ve", lambda v: v.bn_aggr(mv, bns), rd=[b_bns], wr=[b_mv], tag=tag + "ag")
        em.op("sc", lambda s: s.activation(sd, mv[:, 1:2], AF.Sqrt, bias=LN_EPS),
              rd=[b_mv], wr=[b_sd], deps=b_consts.rdeps(), tag=tag + "sd")
        em.op("ve", lambda v: v.reciprocal(rstd, sd), rd=[b_sd], wr=[b_rstd], tag=tag + "rs")
        em.op("ve", lambda v: v.scalar_tensor_tensor(nmr, mv[:, 0:1], -1.0, rstd,
                                                     op0=ALU.mult, op1=ALU.mult),
              rd=[b_mv, b_rstd], wr=[b_nmr], tag=tag + "nm")
        em.op("sc", lambda s: s.activation(out_ap, x_ap, AF.Identity, bias=nmr, scale=rstd),
              rd=[b_x, b_nmr, b_rstd], wr=[b_out_t], tag=tag + "ap")

    def layernorm12(x_ap, b_x, xin_ap, b_xin, na_ap, b_na, own, tag):
        stats = stats_holder[0]
        s0 = sc_slot[0]
        sc_slot[0] += 28
        assert sc_slot[0] <= stats.shape[1]
        bns = stats[:, s0:s0 + 12]
        mv = stats[:, s0 + 12:s0 + 14]
        ap2 = stats[:, s0 + 14:s0 + 16]
        sd2 = stats[:, s0 + 16:s0 + 18]
        rr2 = stats[:, s0 + 18:s0 + 20]
        nbs = stats[:, s0 + 20:s0 + 21]
        nb1 = stats[:, s0 + 21:s0 + 22]
        rv = stats[:, s0 + 22:s0 + 23]
        v2 = stats[:, s0 + 23:s0 + 24]
        b_st, b_mv, b_ap2, b_sd2, b_rr2, b_nb = TBuf(), TBuf(), TBuf(), TBuf(), TBuf(), TBuf()
        b_nb1 = TBuf()
        em.op("ve", lambda v: v.bn_stats(bns[:, 0:6], x_ap[:, 0:512]),
              rd=[b_x], wr=[b_st], accum=True, tag=tag + "b0")
        em.op("ve", lambda v: v.bn_stats(bns[:, 6:12], x_ap[:, 512:1024]),
              rd=[b_x], wr=[b_st], accum=True, tag=tag + "b1")
        em.op("ve", lambda v: v.bn_aggr(mv, bns), rd=[b_st], wr=[b_mv], tag=tag + "ag")
        em.op("ve", lambda v: v.tensor_scalar(ap2[:, 0:1], mv[:, 1:2], 1.0, LN_EPS,
                                              op0=ALU.mult, op1=ALU.add),
              rd=[b_mv], wr=[b_ap2], accum=True, tag=tag + "a1")
        em.op("ve", lambda v: v.reciprocal(rv, ap2[:, 0:1]), rd=[b_ap2], wr=[b_nb],
              accum=True, tag=tag + "rv")
        em.op("ve", lambda v: v.tensor_scalar(v2, mv[:, 1:2], rv, None, op0=ALU.mult),
              rd=[b_mv, b_nb], wr=[b_nb], accum=True, tag=tag + "v2")
        em.op("ve", lambda v: v.scalar_tensor_tensor(ap2[:, 1:2], v2, LN_EPS, ap2[:, 0:1],
                                                     op0=ALU.add, op1=ALU.mult),
              rd=[b_nb, b_ap2], wr=[b_ap2], accum=True, tag=tag + "pr")
        em.op("sc", lambda s: s.activation(sd2, ap2, AF.Sqrt),
              rd=[b_ap2], wr=[b_sd2], tag=tag + "sd")
        em.op("ve", lambda v: v.reciprocal(rr2, sd2), rd=[b_sd2], wr=[b_rr2], tag=tag + "rr")
        em.op("ve", lambda v: v.scalar_tensor_tensor(nbs, mv[:, 0:1], -1.0, rr2[:, 1:2],
                                                     op0=ALU.mult, op1=ALU.mult),
              rd=[b_mv, b_rr2], wr=[b_nb1], accum=True, tag=tag + "nb")
        em.op("sc", lambda s: s.activation(na_ap, x_ap, AF.Identity, bias=nbs,
                                           scale=rr2[:, 1:2]),
              rd=[b_x, b_nb1, b_rr2], wr=[b_na], tag=tag + "na")
        if own:
            em.op("ve", lambda v: v.scalar_tensor_tensor(nb1, mv[:, 0:1], -1.0, rr2[:, 0:1],
                                                         op0=ALU.mult, op1=ALU.mult),
                  rd=[b_mv, b_rr2], wr=[b_nb1], accum=True, tag=tag + "n1")
            em.op("sc", lambda s: s.activation(xin_ap, x_ap, AF.Identity, bias=nb1,
                                               scale=rr2[:, 0:1]),
                  rd=[b_x, b_nb1, b_rr2], wr=[b_xin], tag=tag + "x1")

    def phase_a1():
        ph = ExitStack()
        def _A(n, s, d):
            return ph.enter_context(nc.sbuf_tensor(n, s, d))
        def _Q(n, s, d):
            return ph.enter_context(nc.psum_tensor(n, s, d))
        tr_ps = _Q("tr_ps", [P, 1024], BF16)
        mm_ps0 = _Q("mm_ps0", [P, 512], F32)
        mm_ps1 = _Q("mm_ps1", [P, 512], F32)
        mm_ps2 = _Q("mm_ps2", [P, 512], F32)
        mm_ps3 = _Q("mm_ps3", [P, 512], F32)
        mm_ps4 = _Q("mm_ps4", [P, 512], F32)
        Wi_bf = _A("Wi_bf", [P, 8 * 4224], BF16)
        Wi_st = _A("Wi_st", [P, 2 * 4224], F32)
        binrow = _A("binrow", [1, 2 * 4224], BF16)
        xtile = _A("xtile", [P, 3 * D], F32)
        xintile = _A("xintile", [P, 3 * D], F32)
        na2 = _A("na2", [P, 4 * D], BF16)
        xnT = _A("xnT", [P, 4 * D], BF16)
        ztile = _A("ztile", [P, 2 * S], F32)
        uvout = _A("uvout", [P, 2 * E], BF16)
        w_st = _A("w_st", [P, D], F32)
        w_bfs = _A("w_bfs", [P, D], BF16)
        lp_in = _A("lp_in", [P, D], F32)
        lp_t = _A("lp_t", [P, D], F32)
        lp_t2 = _A("lp_t2", [P, D], F32)
        lp_q = _A("lp_q", [P, D], BF16)
        lp_st = _A("lp_st", [P, 96], F32)
        lnTb = _A("lnTb", [P, 16], BF16)
        stats_a1 = _A("stats_a1", [P, 600], F32)

        stats_holder[0] = stats_a1
        sc_slot[0] = 0
        mm_ps = [mm_ps0, mm_ps1, mm_ps2, mm_ps3, mm_ps4]
        b_trps = TBuf()
        b_mmps = mk_bufs(5)
        b_Wi = mk_bufs(8)
        b_Wist = TBuf()
        b_binrow = TBuf()
        b_xt = mk_bufs(4)
        b_xint = mk_bufs(4)
        b_na2 = mk_bufs(4)
        b_xnT = mk_bufs(4)
        b_z = TBuf()
        b_z2 = TBuf()
        b_uv = mk_bufs(4)
        b_wst = mk_bufs(2)
        b_wbf = TBuf()
        b_lpin = mk_bufs(2)
        b_lpt = TBuf()
        b_lpt2 = TBuf()
        b_lpq = mk_bufs(2)
        b_lpst = TBuf()

        b_Wist2 = [b_Wist, TBuf()]
        for d in range(8):
            wsl = Wi_st[:, (d % 2) * 4224:(d % 2 + 1) * 4224]
            em.op("sy", lambda s, d=d, wsl=wsl: s.dma_start(
                out=wsl, in_=W_in[d * P:(d + 1) * P, :]),
                wr=[b_Wist2[d % 2]], dma=True, tag=f"wi_ld{d}")
            em.op("ve", lambda v, d=d, wsl=wsl: v.tensor_copy(
                Wi_bf[:, d * 4224:(d + 1) * 4224], wsl),
                rd=[b_Wist2[d % 2]], wr=[b_Wi[d]], tag=f"wi_c{d}")
        b_lnTb = TBuf()
        for d in range(8):
            em.op("ve", lambda v, d=d: v.tensor_copy(lnTb[:, 2 * d:2 * d + 1],
                                                     lnT[:, 8 + d:9 + d]),
                  rd=[b_lnT], wr=[b_lnTb], accum=True, tag=f"bh{d}")
            em.op("ve", lambda v, d=d: v.tensor_tensor(lnTb[:, 2 * d + 1:2 * d + 2],
                                                       lnT[:, 8 + d:9 + d],
                                                       lnTb[:, 2 * d:2 * d + 1], ALU.subtract),
                  rd=[b_lnT, b_lnTb], wr=[b_lnTb], accum=True, tag=f"bl{d}")
        bstage = Wi_st[0:1, :]
        em.op("sy", lambda s: s.dma_start(out=bstage[0:1, 0:4224], in_=b_in_d[None, :]),
              wr=[b_Wist2[0], b_Wist2[1], b_bstage], dma=True, tag="bi_ld")
        b_bsum = b_lpt
        for j in range(8):
            bank, bbank = mm_ps[j % 2], b_mmps[j % 2]
            for d in range(8):
                for hl in range(2):
                    em.op("te", lambda t, d=d, j=j, hl=hl, bank=bank: t.matmul(
                        bank[0:1, 0:512], lnTb[:, 2 * d + hl:2 * d + hl + 1],
                        Wi_bf[:, d * 4224 + j * 512: d * 4224 + j * 512 + 512],
                        start=(d == 0 and hl == 0), stop=(d == 7 and hl == 1)),
                        rd=[b_lnTb, b_Wi[d]], wr=[bbank], accum=not (d == 0 and hl == 0),
                        tag=f"bw{j}_{d}_{hl}")
            em.op("ve", lambda v, j=j, bank=bank: v.tensor_tensor(
                lp_t[0:1, 0:512], bank[0:1, 0:512],
                bstage[0:1, j * 512:(j + 1) * 512], ALU.add),
                rd=[bbank, b_bstage], wr=[b_bsum], tag=f"bws{j}")
            em.op("ve", lambda v, j=j: v.tensor_copy(
                binrow[0:1, j * 512:(j + 1) * 512], lp_t[0:1, 0:512]),
                rd=[b_bsum], wr=[b_binrow], accum=True, tag=f"bwh{j}")
            em.op("ve", lambda v, j=j: v.tensor_tensor(
                binrow[0:1, 4224 + j * 512: 4224 + (j + 1) * 512], lp_t[0:1, 0:512],
                binrow[0:1, j * 512:(j + 1) * 512], ALU.subtract),
                rd=[b_bsum, b_binrow], wr=[b_binrow], accum=True, tag=f"bwl{j}")
        zbank, b_zbank = mm_ps[2], b_mmps[2]
        for d in range(8):
            for hl in range(2):
                em.op("te", lambda t, d=d, hl=hl: t.matmul(
                    zbank[:, 0:1], Wi_bf[:, d * 4224 + 2 * E: d * 4224 + 2 * E + S],
                    lnTb[:, 2 * d + hl:2 * d + hl + 1],
                    start=(d == 0 and hl == 0), stop=(d == 7 and hl == 1)),
                    rd=[b_lnTb, b_Wi[d]], wr=[b_zbank], accum=not (d == 0 and hl == 0),
                    tag=f"bwz{d}_{hl}")
        em.op("ve", lambda v: v.tensor_tensor(gb_cols[:, 4:5], gb_cols[:, 4:5],
                                              zbank[:, 0:1], ALU.add),
              rd=[b_gb, b_zbank], wr=[b_gb], accum=True, tag="bwzadd")
        for d in range(8):
            em.op("ve", lambda v, d=d: v.tensor_scalar(
                Wi_bf[:, d * 4224:(d + 1) * 4224], Wi_bf[:, d * 4224:(d + 1) * 4224],
                lnT[:, d:d + 1], None, op0=ALU.mult),
                rd=[b_lnT], wr=[b_Wi[d]], tag=f"wg{d}")

        em.op("sy", lambda s: s.dma_start(out=bstage[0:1, 0:D], in_=b_out_d[None, :]),
              wr=[b_Wist, b_bstage], dma=True, tag="bo_ld")
        em.op("ve", lambda v: v.tensor_copy(bout_hl[0:1, 0:D], bstage[0:1, 0:D]),
              rd=[b_bstage], wr=[b_bout], accum=True, tag="bo_hi")
        em.op("ve", lambda v: v.tensor_tensor(bout_hl[0:1, D:2 * D], bstage[0:1, 0:D],
                                              bout_hl[0:1, 0:D], ALU.subtract),
              rd=[b_bstage, b_bout], wr=[b_bout], accum=True, tag="bo_lo")

        lp_units = ([("g", k) for k in range(3)] + [("u", k) for k in range(3)]
                    + [("d", k) for k in range(8)])
        lp_pos = [0]

        def emit_lpbq_unit():
            if lp_pos[0] >= len(lp_units):
                return
            kind, key = lp_units[lp_pos[0]]
            pi = 0
            lp_pos[0] += 1
            if kind == "g":
                src_ap = wg_sh[key * P:(key + 1) * P, :]
                off, W = key * 131072, D
            elif kind == "u":
                src_ap = wu_sh[key * P:(key + 1) * P, :]
                off, W = 3 * 131072 + key * 131072, D
            else:
                src_ap = wd_sh[key * P:(key + 1) * P, :]
                off, W = 6 * 131072 + key * 49152, 384
            nb = W // 32
            dst_ap = wsh_in[0, off:off + P * W].rearrange("(p c) -> p c", p=P)
            inb = lp_in[:, pi * D:pi * D + W]
            qb = lp_q[:, pi * D:pi * D + W]
            bin_, bq_ = b_lpin[pi], b_lpq[pi]
            amax, ssc, rsc = lp_st[:, 0:nb], lp_st[:, 32:32 + nb], lp_st[:, 64:64 + nb]
            em.op("sy", lambda s: s.dma_start(out=inb, in_=src_ap), wr=[bin_], dma=True, tag="lpl")
            inv = inb.rearrange("p (b k) -> p b k", k=32)
            em.op("ve", lambda v: v.tensor_reduce(amax, inv, axis=AX.X, op=ALU.max,
                                                  apply_absolute_value=True),
                  rd=[bin_], wr=[b_lpst], tag="lpm")
            em.op("ve", lambda v: v.tensor_scalar(ssc, amax, C7, 1e-12, op0=ALU.mult, op1=ALU.max),
                  rd=[b_lpst], wr=[b_lpst], accum=True, tag="lps")
            em.op("ve", lambda v: v.reciprocal(rsc, ssc), rd=[b_lpst], wr=[b_lpst],
                  accum=True, tag="lpr")
            sb3 = ssc.unsqueeze(2).broadcast_to([P, nb, 32])
            rb3 = rsc.unsqueeze(2).broadcast_to([P, nb, 32])
            tv = lp_t[:, 0:W].rearrange("p (b k) -> p b k", k=32)
            t2v = lp_t2[:, 0:W].rearrange("p (b k) -> p b k", k=32)
            qv = qb.rearrange("p (b k) -> p b k", k=32)
            em.op("ve", lambda v: v.tensor_tensor(tv, inv, rb3, ALU.mult),
                  rd=[bin_, b_lpst], wr=[b_lpt], tag="lpt")
            em.op("ve", lambda v: v.tensor_scalar(lp_t2[:, 0:W], lp_t[:, 0:W], MAGIC, MAGIC,
                                                  op0=ALU.add, op1=ALU.subtract),
                  rd=[b_lpt], wr=[b_lpt2], tag="lpn")
            em.op("ve", lambda v: v.tensor_tensor(qv, t2v, sb3, ALU.mult),
                  rd=[b_lpt2, b_lpst], wr=[bq_], tag="lpq")
            em.op("gp", lambda g: g.dma_start(out=dst_ap, in_=qb), rd=[bq_], wr=[d_wsh],
                  dma=True, accum=True, tag="lpw")


        wo_pos = [0]

        def emit_wout_unit():
            if wo_pos[0] >= 16:
                return
            e = wo_pos[0]
            pi = 0
            wo_pos[0] += 1
            stg = w_st[:, 0:D]
            em.op("sy", lambda s, e=e, stg=stg: s.dma_start(out=stg, in_=W_out[e * P:(e + 1) * P, :]),
                  wr=[b_wst[pi], b_wbf], dma=True, tag="wol")
            em.op("ve", lambda v, stg=stg: v.tensor_copy(w_bfs[:], stg), rd=[b_wst[pi]],
                  wr=[b_wbf], tag="woc")
            em.op("gp", lambda g, e=e: g.dma_start(out=wout_dram[e * P:(e + 1) * P, :],
                                                   in_=w_bfs[:]),
                  rd=[b_wbf], wr=[d_wout[e]], dma=True, tag="wos")

        def a1_front(abs_t, own):
            p3 = abs_t % 3
            pi = abs_t % 4
            xa = xtile[:, p3 * D:(p3 + 1) * D]
            xia = xintile[:, p3 * D:(p3 + 1) * D]
            naa = na2[:, pi * D:(pi + 1) * D]
            xnTa = xnT[:, pi * D:(pi + 1) * D]
            bx, bxi, bna, bxnT = b_xt[p3], b_xint[p3], b_na2[pi], b_xnT[pi]

            em.op("sy", lambda s: s.dma_start(out=xa, in_=x_in[abs_t * P:(abs_t + 1) * P, :]),
                  wr=[bx], dma=True, tag=f"xl{abs_t}")
            layernorm12(xa, bx, xia, bxi, naa, bna, own, f"l12_{abs_t}")
            if own:
                em.op("gp", lambda g: g.dma_start(out=xin_dram[abs_t * P:(abs_t + 1) * P, :],
                                                  in_=xia),
                      rd=[bxi], wr=[d_xin[abs_t]], dma=True, tag=f"xis{abs_t}")
            for dd in range(8):
                em.op("te", lambda t, dd=dd, naa=naa: t.transpose(
                    tr_ps[:, dd * P:(dd + 1) * P], naa[:, dd * P:(dd + 1) * P], ident[:]),
                    rd=[bna, b_ident], wr=[b_trps], accum=(dd > 0), tag=f"tr{abs_t}_{dd}")
            em.op("ve", lambda v, xnTa=xnTa: v.tensor_copy(xnTa, tr_ps[:]),
                  rd=[b_trps], wr=[bxnT], tag=f"te{abs_t}")

        def a1_back(abs_t, own):
            pi = abs_t % 4
            pu = abs_t % 2
            xnTa = xnT[:, pi * D:(pi + 1) * D]
            bxnT = b_xnT[pi]

            zp, bzp = mm_ps[4], b_mmps[4]
            for dd in range(8):
                em.op("te", lambda t, dd=dd: t.matmul(
                    zp[:, 0:S], Wi_bf[:, dd * 4224 + 2 * E: dd * 4224 + 2 * E + S],
                    xnTa[:, dd * P:(dd + 1) * P], start=(dd == 0), stop=(dd == 7)),
                    rd=[b_Wi[dd], bxnT], wr=[bzp], accum=(dd > 0), tag=f"zm{abs_t}_{dd}")
            zt = ztile[:, (abs_t % 2) * S:(abs_t % 2 + 1) * S]
            bz2 = b_z if abs_t % 2 == 0 else b_z2
            em.op("sc", lambda s, zt=zt: s.activation(zt, zp[:, 0:S], AF.Silu,
                                                      bias=gb_cols[:, 4:5]),
                  rd=[bzp, b_gb], wr=[bz2], tag=f"ze{abs_t}")
            em.op("sc", lambda s, zt=zt: s.activation(k_fm[:, abs_t * P:(abs_t + 1) * P], zt,
                                                      AF.Identity, bias=gb_cols[:, 3:4],
                                                      scale=gb_cols[:, 2:3]),
                  rd=[bz2, b_gb], wr=[b_kfm[abs_t]], tag=f"kv{abs_t}")
            if own:
                em.op("sc", lambda s, zt=zt: s.activation(q_fm[:, abs_t * P:(abs_t + 1) * P], zt,
                                                          AF.Identity, bias=gb_cols[:, 1:2],
                                                          scale=gb_cols[:, 0:1]),
                      rd=[bz2, b_gb], wr=[b_qfm[abs_t]], tag=f"qv{abs_t}")

            vdst = vkg_in[abs_t // 4][0, (abs_t % 4) * P * E:(abs_t % 4 + 1) * P * E]\
                .rearrange("(p c) -> p c", p=P)
            specs = [("v", E, vdst, d_vk[abs_t // 4], abs_t * P)]
            if own:
                specs.append(("u", 0, u_dram, d_u[abs_t], abs_t * P))
            for which, col0, dstd, dbuf, row in specs:
                uvb = b_uv[pu]
                for dd in range(8):
                    for eb in range(4):
                        bank, bbank = mm_ps[eb], b_mmps[eb]
                        c0 = col0 + eb * 512
                        em.op("te", lambda t, dd=dd, c0=c0, bank=bank: t.matmul(
                            bank[:], xnTa[:, dd * P:(dd + 1) * P],
                            Wi_bf[:, dd * 4224 + c0: dd * 4224 + c0 + 512],
                            start=(dd == 0), stop=False),
                            rd=[b_Wi[dd], bxnT], wr=[bbank], accum=(dd > 0),
                            tag=f"{which}m{abs_t}_{eb}_{dd}")
                for eb in range(4):
                    bank, bbank = mm_ps[eb], b_mmps[eb]
                    c0 = col0 + eb * 512
                    em.op("te", lambda t, c0=c0, bank=bank: t.matmul(
                        bank[:], ones1[:], binrow[0:1, c0:c0 + 512], start=False, stop=False),
                        rd=[b_ones, b_binrow], wr=[bbank], accum=True, tag=f"{which}h{abs_t}_{eb}")
                    em.op("te", lambda t, c0=c0, bank=bank: t.matmul(
                        bank[:], ones1[:], binrow[0:1, 4224 + c0:4224 + c0 + 512],
                        start=False, stop=True),
                        rd=[b_ones, b_binrow], wr=[bbank], accum=True, tag=f"{which}l{abs_t}_{eb}")
                    em.op("sc", lambda s, eb=eb, bank=bank: s.activation(
                        uvout[:, pu * E + eb * 512: pu * E + eb * 512 + 512], bank[:], AF.Silu),
                        rd=[bbank], wr=[uvb], accum=(eb > 0), tag=f"{which}e{abs_t}_{eb}")
                em.op("gp", lambda g, dstd=dstd, row=row, which=which: g.dma_start(
                    out=(dstd[row:row + P, :] if which == "u" else dstd),
                    in_=uvout[:, pu * E:(pu + 1) * E]),
                    rd=[uvb], wr=[dbuf], dma=True, accum=(which == "v"),
                    tag=f"{which}s{abs_t}")

        a1_front(0, True)
        a1_front(1, True)
        for t_ in range(NT):
            if t_ + 2 < NT:
                a1_front(t_ + 2, True)
            a1_back(t_, True)
            emit_lpbq_unit()
            emit_wout_unit()
            if t_ % 4 == 3:
                gidx = t_ // 4
                em.op("gp", lambda g, gidx=gidx: g.dma_start(
                    out=vkg_in[gidx][0, VKG:VKT].rearrange("(p c) -> p c", p=P),
                    in_=k_fm[:, gidx * 512:(gidx + 1) * 512]),
                    rd=b_kfm[4 * gidx:4 * gidx + 4], wr=[d_vk[gidx]], dma=True,
                    accum=True, tag=f"kship{gidx}")
                em.op("gp", lambda g, gidx=gidx: g.collective_compute(
                    "AllGather", ALU.bypass,
                    replica_groups=[[0, 1], [2, 3], [4, 5], [6, 7]],
                    ins=[vkg_in[gidx].opt()], outs=[vkg_out[gidx].opt()]),
                    rd=[d_vk[gidx]], wr=[b_vkag[gidx]], cc=True, tag=f"vk_ag{gidx}")
        while lp_pos[0] < len(lp_units):
            emit_lpbq_unit()
        while wo_pos[0] < 16:
            emit_wout_unit()
        em.op("gp", lambda g: g.collective_compute(
            "AllGather", ALU.bypass, replica_groups=[list(range(8))],
            ins=[wsh_in.opt()], outs=[wsh_out.opt()]),
            rd=[d_wsh], wr=[b_ag], cc=True, tag="wsh_ag")
        ph.close()

    y1n_mx = mxsl[:, 0:16]
    phase_a1()
    em.barrier()

    def phase_a2():
        ph = ExitStack()
        def _A(n, s, d):
            return ph.enter_context(nc.sbuf_tensor(n, s, d))
        def _Q(n, s, d):
            return ph.enter_context(nc.psum_tensor(n, s, d))
        sc_ps0 = _Q("sc_ps0", [P, 512], F32)
        sc_ps1 = _Q("sc_ps1", [P, 512], F32)
        av_ps0 = _Q("av_ps0", [P, 512], F32)
        av_ps1 = _Q("av_ps1", [P, 512], F32)
        av_ps2 = _Q("av_ps2", [P, 512], F32)
        av_ps3 = _Q("av_ps3", [P, 512], F32)
        pj_ps0 = _Q("pj_ps0", [P, 512], F32)
        pj_ps1 = _Q("pj_ps1", [P, 512], F32)
        v_sb = _A("v_sb", [P, NKT * E], BF16)
        scT = _A("scT", [P, L], BF16)
        scf = _A("scf", [P, 1024], F32)
        u_sb = _A("u_sb", [P, 2 * E], BF16)
        m_sb = _A("m_sb", [P, E], BF16)
        mT_sb = _A("mT_sb", [P, E], BF16)
        wo_sb = _A("wo_sb", [P, 8 * D], BF16)
        ep1 = _A("ep1", [P, D], F32)
        ep2 = _A("ep2", [P, D], F32)
        ep3 = _A("ep3", [P, D], F32)
        stats_a2 = _A("stats_a2", [P, 600], F32)

        stats_holder[0] = stats_a2
        sc_slot[0] = 0
        av_ps = [av_ps0, av_ps1, av_ps2, av_ps3]
        sc_ps = [sc_ps0, sc_ps1]
        pj_ps = [pj_ps0, pj_ps1]
        mt_ps = [av_ps0[:].bitcast(BF16), av_ps2[:].bitcast(BF16)]
        b_scps = mk_bufs(2)
        b_scf2 = mk_bufs(2)
        b_avps = mk_bufs(4)
        b_pjps = mk_bufs(2)
        b_vsb = mk_bufs(NKT)
        b_scT = mk_bufs(8)
        b_scf = TBuf()
        b_usb = mk_bufs(2)
        b_msb = mk_bufs(4)
        b_mTsb = mk_bufs(2)
        b_wosb = mk_bufs(2)
        b_ep1, b_ep2, b_ep3 = TBuf(), TBuf(), TBuf()

        for half in range(2):
            for g_ in range(4):
                em.op("sy", lambda s, half=half, g_=g_: s.dma_start(
                    out=k_fm[:, half * T + g_ * 512: half * T + (g_ + 1) * 512],
                    in_=vkg_out[g_][half, VKG:VKT].rearrange("(p c) -> p c", p=P)),
                    rd=[b_vkag[g_]], wr=b_kfm[half * NT + 4 * g_: half * NT + 4 * g_ + 4],
                    dma=True, tag=f"krl{half}_{g_}")
        for g_ in range(4):
            for half in range(2):
                for i_ in range(4):
                    kt = half * NT + g_ * 4 + i_
                    em.op("sy", lambda s, kt=kt, g_=g_, i_=i_, half=half: s.dma_start(
                        out=v_sb[:, kt * E:(kt + 1) * E],
                        in_=vkg_out[g_][half, i_ * P * E:(i_ + 1) * P * E]
                        .rearrange("(p c) -> p c", p=P)),
                        rd=[b_vkag[g_]], wr=[b_vsb[kt]], dma=True, tag=f"vr{kt}")

        def emit_scores(q):
            for g4 in range(8):
                bank, bbank = sc_ps[g4 % 2], b_scps[g4 % 2]
                for j in range(4):
                    kt = g4 * 4 + j
                    em.op("te", lambda t, kt=kt, j=j, q=q, bank=bank: t.matmul(
                        bank[:, j * P:(j + 1) * P], k_fm[:, kt * P:(kt + 1) * P],
                        q_fm[:, q * P:(q + 1) * P], start=True, stop=True),
                        rd=[b_kfm[kt], b_qfm[q]], wr=[bbank], accum=(j > 0), tag=f"sc{q}_{kt}")
                em.op("sc", lambda s, bank=bank, g4=g4, q=q: s.activation(
                    scf[:, (g4 % 2) * 512:(g4 % 2 + 1) * 512], bank[:], AF.Relu, scale=I4096),
                    rd=[bbank], wr=[b_scf2[g4 % 2]], deps=b_consts.rdeps(),
                    tag=f"se{q}_{g4}")
                em.op("ve", lambda v, g4=g4, q=q: v.tensor_tensor(
                    scT[:, g4 * 512:(g4 + 1) * 512], scf[:, (g4 % 2) * 512:(g4 % 2 + 1) * 512],
                    scf[:, (g4 % 2) * 512:(g4 % 2 + 1) * 512], ALU.mult),
                    rd=[b_scf2[g4 % 2]], wr=[b_scT[g4]], tag=f"sq{q}_{g4}")

        emit_scores(0)

        for q in range(NT):
            for g in range(4):
                em.op("sy", lambda s, g=g: s.dma_start(
                    out=wo_sb[:, (g % 2) * 4 * D:((g % 2) + 1) * 4 * D].rearrange(
                        "p (c h) -> p c h", c=4),
                    in_=wout_dram[g * 512:(g + 1) * 512, :].rearrange("(c p) h -> p c h", p=P)),
                    rd=[d_wout[4 * g], d_wout[4 * g + 1], d_wout[4 * g + 2], d_wout[4 * g + 3]],
                    wr=[b_wosb[g % 2]], dma=True, tag=f"wr{q}_{g}")
            upi = q % 2
            em.op("sy", lambda s, q=q, upi=upi: s.dma_start(
                out=u_sb[:, upi * E:(upi + 1) * E], in_=u_dram[q * P:(q + 1) * P, :]),
                rd=[d_u[q]], wr=[b_usb[upi]], dma=True, tag=f"ur{q}")
            for kt in range(NKT):
                for eb in range(4):
                    em.op("te", lambda t, kt=kt, eb=eb: t.matmul(
                        av_ps[eb][:], scT[:, kt * P:(kt + 1) * P],
                        v_sb[:, kt * E + eb * 512: kt * E + eb * 512 + 512],
                        start=(kt == 0), stop=(kt == NKT - 1)),
                        rd=[b_scT[kt // 4], b_vsb[kt]], wr=[b_avps[eb]], accum=(kt > 0),
                        tag=f"av{q}_{eb}_{kt}")
            if q + 1 < NT:
                emit_scores(q + 1)
            for eb in range(4):
                em.op("ve", lambda v, eb=eb, upi=upi: v.tensor_tensor(
                    m_sb[:, eb * 512:(eb + 1) * 512], av_ps[eb][:],
                    u_sb[:, upi * E + eb * 512: upi * E + eb * 512 + 512], ALU.mult),
                    rd=[b_avps[eb], b_usb[upi]], wr=[b_msb[eb]], tag=f"m{q}_{eb}")
            for g in range(2):
                mtb = mt_ps[g]
                wrb = [b_avps[2 * g], b_avps[2 * g + 1]]
                for j in range(8):
                    e = g * 8 + j
                    em.op("te", lambda t, e=e, j=j, mtb=mtb: t.transpose(
                        mtb[:, j * P:(j + 1) * P], m_sb[:, e * P:(e + 1) * P], ident[:]),
                        rd=[b_msb[e // 4], b_ident], wr=wrb, accum=(j > 0), tag=f"mt{q}_{e}")
                em.op("sc", lambda s, g=g, mtb=mtb: s.copy(mT_sb[:, g * 1024:(g + 1) * 1024],
                                                           mtb[:]),
                      rd=wrb, wr=[b_mTsb[g]], tag=f"me{q}_{g}")
            for half in range(2):
                em.op("te", lambda t, half=half: t.matmul(
                    pj_ps[half][:], ones1[:], bout_hl[0:1, half * 512:(half + 1) * 512],
                    start=True, stop=False),
                    rd=[b_ones, b_bout], wr=[b_pjps[half]], tag=f"ph{q}_{half}")
                em.op("te", lambda t, half=half: t.matmul(
                    pj_ps[half][:], ones1[:], bout_hl[0:1, D + half * 512: D + half * 512 + 512],
                    start=False, stop=False),
                    rd=[b_ones, b_bout], wr=[b_pjps[half]], accum=True, tag=f"pl{q}_{half}")
            for e in range(16):
                for half in range(2):
                    em.op("te", lambda t, e=e, half=half: t.matmul(
                        pj_ps[half][:], mT_sb[:, e * P:(e + 1) * P],
                        wo_sb[:, (e % 8) * D + half * 512: (e % 8) * D + half * 512 + 512],
                        start=False, stop=(e == 15)),
                        rd=[b_mTsb[e // 8], b_wosb[(e // 4) % 2]], wr=[b_pjps[half]], accum=True,
                        tag=f"pj{q}_{e}_{half}")
            em.op("sy", lambda s, q=q: s.dma_start(out=ep1[:], in_=xin_dram[q * P:(q + 1) * P, :]),
                  rd=[d_xin[q]], wr=[b_ep1], dma=True, tag=f"xir{q}")
            em.op("sy", lambda s, q=q: s.dma_start(out=ep2[:], in_=x_in[q * P:(q + 1) * P, :]),
                  wr=[b_ep2], dma=True, tag=f"xr{q}")
            for half in range(2):
                em.op("ve", lambda v, half=half: v.tensor_tensor(
                    ep1[:, half * 512:(half + 1) * 512], pj_ps[half][:],
                    ep1[:, half * 512:(half + 1) * 512], ALU.add),
                    rd=[b_pjps[half], b_ep1], wr=[b_ep1], accum=True, tag=f"ao{q}_{half}")
            layernorm(ep1[:], b_ep1, ep3[:], b_ep3, f"l3_{q}")
            em.op("ve", lambda v: v.tensor_tensor(ep2[:], ep2[:], ep3[:], ALU.add),
                  rd=[b_ep2, b_ep3], wr=[b_ep2], tag=f"y1_{q}")
            em.op("gp", lambda g, q=q: g.dma_start(out=y1_dram[q * P:(q + 1) * P, :], in_=ep2[:]),
                  rd=[b_ep2], wr=[d_y1[q]], dma=True, tag=f"y1s{q}")
            layernorm(ep2[:], b_ep2, ep3[:], b_ep3, f"l4_{q}")
            em.op("gp", lambda g, q=q: g.dma_start(out=y1n_dram[q * P:(q + 1) * P, :], in_=ep3[:]),
                  rd=[b_ep3], wr=[d_y1n[q]], dma=True, tag=f"y1ns{q}")
            em.op("ve", lambda v, q=q: v.tensor_reduce(y1n_mx[:, q:q + 1], ep3[:], axis=AX.X,
                                                       op=ALU.max, apply_absolute_value=True),
                  rd=[b_ep3], wr=[b_y1nmx], accum=True, tag=f"ymx{q}")
        ph.close()

    phase_a2()
    atn.close()
    em.barrier()

    def dump_y1():
        for t_ in range(NT):
            tok = em.op("gp", lambda g, t_=t_: g.dma_start(
                out=y_out[t_ * P:(t_ + 1) * P, :], in_=y1_dram[t_ * P:(t_ + 1) * P, :]),
                rd=[d_y1[t_]], dma=True, tag=f"dump_y{t_}")
            em.force_inc(tok)

    if STAGE == "a":
        dump_y1()

    barrier_n = [0]
    b_pmst = TBuf()

    def all_reduce_max(col_ap, b_col, nvals, tag):
        bi = barrier_n[0]
        barrier_n[0] += 1
        cc_in[bi] = nc.dram_tensor(f"cc_in{bi}", [1, nvals], F32)
        cc_out[bi] = nc.dram_tensor(f"cc_out{bi}", [1, nvals], F32)
        pm = mxsl[:, 300 + 2 * bi:302 + 2 * bi]
        em.op("gp", lambda g: g.tensor_reduce(pm[0:1, 0:nvals], col_ap, axis=AX.C, op=ALU.max),
              rd=[b_col], wr=[b_pmst], accum=True, tag=tag + "p")
        em.op("gp", lambda g: g.dma_start(out=cc_in[bi].ap()[0:1, 0:nvals], in_=pm[0:1, 0:nvals]),
              rd=[b_pmst], wr=[d_cc[bi]], dma=True, tag=tag + "w")
        em.op("gp", lambda g: g.collective_compute(
            "AllReduce", ALU.max, replica_groups=[list(range(8))],
            ins=[cc_in[bi].ap().opt()], outs=[cc_out[bi].ap().opt()]),
            rd=[d_cc[bi]], wr=[d_cc[bi]], accum=True, cc=True, tag=tag + "c")
        gcol = gsc[:, 4 * bi:4 * bi + nvals]
        b_g = TBuf()
        em.op("sy", lambda s: s.dma_start(out=gcol,
                                          in_=cc_out[bi].ap()[0, 0:nvals].partition_broadcast(P)),
              rd=[d_cc[bi]], wr=[b_g], dma=True, tag=tag + "b")
        return gcol, b_g

    def phase_b():
        ph = ExitStack()
        def _A(n, s, d):
            return ph.enter_context(nc.sbuf_tensor(n, s, d))
        def _Q(n, s, d):
            return ph.enter_context(nc.psum_tensor(n, s, d))
        xq_ps = _Q("xq_ps", [P, 1024], BF16)
        fm_ps = [_Q(f"fm_ps{i}", [P, 512], F32) for i in range(6)]
        b_xqps = TBuf()
        b_fmps = mk_bufs(6)

        ph_x = ExitStack()
        if True:
            xqT = ph_x.enter_context(nc.sbuf_tensor("xqT", [P, 8 * T], BF16))
            b_xqT = mk_bufs(8)

            wstg = ph_x.enter_context(nc.sbuf_tensor("wstg", [P, 2 * D], BF16))
            mrow = ph_x.enter_context(nc.sbuf_tensor("mrow", [P, 2 * E], F32))
            ytile = ph_x.enter_context(nc.sbuf_tensor("ytile", [P, 2 * D], F32))
            yq = ph_x.enter_context(nc.sbuf_tensor("yq", [P, D], BF16))
            ph2 = ExitStack()
            if True:
                wgT = ph2.enter_context(nc.sbuf_tensor("wgT", [P, 8 * I], BF16))
                b_wgT = mk_bufs(8)
                b_wstg2 = mk_bufs(2)
                b_mrow = mk_bufs(2)

                def wprep_chunk(base, wT_sb, b_wT, ci, name):
                    wp = ci % 2
                    wsa = wstg[:, wp * D:(wp + 1) * D]
                    off = base + (ci // 3) * 0
                    j, k = ci // 3, ci % 3
                    src_ap = wsh_out[j, base + k * 131072: base + (k + 1) * 131072]\
                        .rearrange("(p c) -> p c", p=P)
                    em.op("sy", lambda s, wsa=wsa, src_ap=src_ap: s.dma_start(
                        out=wsa, in_=src_ap),
                        rd=[b_ag], wr=[b_wstg2[wp]], dma=True, tag=f"{name}wl{ci}")
                    for dd in range(8):
                        em.op("te", lambda t, dd=dd, wsa=wsa: t.transpose(
                            xq_ps[:, dd * P:(dd + 1) * P], wsa[:, dd * P:(dd + 1) * P], ident[:]),
                            rd=[b_wstg2[wp], b_ident], wr=[b_xqps], accum=(dd > 0),
                            tag=f"{name}wt{ci}_{dd}")
                    wv3 = wT_sb.rearrange("p (d i) -> p d i", d=8)[:, :, ci * P:(ci + 1) * P]
                    em.op("ve", lambda v, wv3=wv3: v.tensor_copy(
                        wv3, xq_ps[:].rearrange("p (d i) -> p d i", d=8)),
                        rd=[b_xqps], wr=b_wT, accum=True, tag=f"{name}we{ci}")

                def mm1_ic(wT_sb, b_wT, dst_dram, d_dst, mx0, b_mx, bank_off, name, ic):
                    pi = ic % 2
                    rowa = mrow[:, pi * E:(pi + 1) * E]
                    bk0 = (ic % 3) * 2
                    for dd in range(8):
                        for tb in range(4):
                            bank = fm_ps[(bk0 + tb) % 6]
                            bbank = b_fmps[(bk0 + tb) % 6]
                            em.op("te", lambda t, dd=dd, ic=ic, tb=tb, bank=bank: t.matmul(
                                bank[:], wT_sb[:, dd * I + ic * P: dd * I + (ic + 1) * P],
                                xqT[:, dd * T + tb * 512: dd * T + tb * 512 + 512],
                                start=(dd == 0), stop=(dd == 7)),
                                rd=[b_wT[dd], b_xqT[dd]], wr=[bbank], accum=(dd > 0),
                                tag=f"{name}m{ic}_{tb}_{dd}")
                    for tb in range(4):
                        bank = fm_ps[(bk0 + tb) % 6]
                        bbank = b_fmps[(bk0 + tb) % 6]
                        em.op("sc", lambda s, tb=tb, bank=bank, rowa=rowa: s.copy(
                            rowa[:, tb * 512:(tb + 1) * 512], bank[:]),
                            rd=[bbank], wr=[b_mrow[pi]], accum=(tb > 0), tag=f"{name}e{ic}_{tb}")
                        em.op("ve", lambda v, tb=tb, ic=ic, rowa=rowa: v.tensor_reduce(
                            mx0[:, ic * 4 + tb:ic * 4 + tb + 1],
                            rowa[:, tb * 512:(tb + 1) * 512], axis=AX.X,
                            op=ALU.max, apply_absolute_value=True),
                            rd=[b_mrow[pi]], wr=[b_mx], accum=True, tag=f"{name}x{ic}_{tb}")
                    em.op("gp", lambda g, ic=ic, rowa=rowa: g.dma_start(
                        out=dst_dram[ic * P:(ic + 1) * P, :], in_=rowa),
                        rd=[b_mrow[pi]], wr=[d_dst[ic]], dma=True, tag=f"{name}s{ic}")

                for ci in range(8):
                    wprep_chunk(0, wgT, b_wgT, ci, "g")
                ymx_col = mxsl[:, 290:291]
                b_ymxc = TBuf()
                em.op("ve", lambda v: v.tensor_reduce(ymx_col, y1n_mx, axis=AX.X, op=ALU.max),
                      rd=[b_y1nmx], wr=[b_ymxc], tag="ymxr")
                g1col, b_g1 = all_reduce_max(ymx_col, b_ymxc, 1, "b1")
                s1, r1 = bstat[:, 0:1], bstat[:, 1:2]
                em.op("ve", lambda v: v.tensor_scalar(s1, g1col, C32767, 1e-12,
                                                      op0=ALU.mult, op1=ALU.max),
                      rd=[b_g1], wr=[b_bstat], accum=True, tag="s1")
                em.op("ve", lambda v: v.reciprocal(r1, s1), rd=[b_bstat], wr=[b_bstat],
                      accum=True, tag="r1")

                if True:
                    b_yt = mk_bufs(2)
                    b_yq = TBuf()
                    for t_ in range(NT):
                        if t_ + 8 < NIC:
                            wprep_chunk(0, wgT, b_wgT, t_ + 8, "g")
                        pi = t_ % 2
                        ya = ytile[:, pi * D:(pi + 1) * D]
                        em.op("sy", lambda s, t_=t_, ya=ya: s.dma_start(
                            out=ya, in_=y1n_dram[t_ * P:(t_ + 1) * P, :]),
                            rd=[d_y1n[t_]], wr=[b_yt[pi]], dma=True, tag=f"ynr{t_}")
                        em.op("ve", lambda v, ya=ya: v.tensor_scalar(ya, ya, r1, MAGIC,
                                                                     op0=ALU.mult, op1=ALU.add),
                              rd=[b_yt[pi], b_bstat], wr=[b_yt[pi]], tag=f"xq1_{t_}")
                        em.op("ve", lambda v, ya=ya: v.tensor_scalar(yq[:], ya, MAGIC, s1,
                                                                     op0=ALU.subtract, op1=ALU.mult),
                              rd=[b_yt[pi], b_bstat], wr=[b_yq], tag=f"xq2_{t_}")
                        for dd in range(8):
                            em.op("te", lambda t, dd=dd: t.transpose(
                                xq_ps[:, dd * P:(dd + 1) * P], yq[:, dd * P:(dd + 1) * P], ident[:]),
                                rd=[b_yq, b_ident], wr=[b_xqps], accum=(dd > 0), tag=f"xt{t_}_{dd}")
                        for dd in range(8):
                            em.op("sc", lambda s, dd=dd, t_=t_: s.copy(
                                xqT[:, dd * T + t_ * P: dd * T + (t_ + 1) * P],
                                xq_ps[:, dd * P:(dd + 1) * P]),
                                rd=[b_xqps], wr=[b_xqT[dd]], accum=True, tag=f"xe{t_}_{dd}")

                if _LVL[STAGE] <= 1:
                    dump_y1()
                    ph_x.close()
                    ph.close()
                    return
                if _LVL[STAGE] <= 1.3:
                    dump_y1()
                    ph2.close()
                    ph_x.close()
                    ph.close()
                    return
                gate_mx = mxsl[:, 16:112]
                b_gmx = TBuf()
                for ic in range(NIC):
                    mm1_ic(wgT, b_wgT, graw_dram, d_graw, gate_mx, b_gmx, 0, "g", ic)
                if _LVL[STAGE] <= 1.6:
                    dump_y1()
                    ph2.close()
                    ph_x.close()
                    ph.close()
                    return
                gmax_l = mxsl[:, 291:292]
                b_gmaxl = TBuf()
                em.op("ve", lambda v: v.tensor_reduce(gmax_l, gate_mx, axis=AX.X, op=ALU.max),
                      rd=[b_gmx], wr=[b_gmaxl], tag="gmr")
                ggcol, b_gg = all_reduce_max(gmax_l, b_gmaxl, 1, "b2")
                s_g, r_g = bstat[:, 2:3], bstat[:, 3:4]
                em.op("ve", lambda v: v.tensor_scalar(s_g, ggcol, C32767, 1e-12,
                                                      op0=ALU.mult, op1=ALU.max),
                      rd=[b_gg], wr=[b_bstat], accum=True, tag="sg")
                em.op("ve", lambda v: v.reciprocal(r_g, s_g), rd=[b_bstat], wr=[b_bstat],
                      accum=True, tag="rg")
                nmsg = bstat[:, 11:12]
                em.op("ve", lambda v: v.tensor_scalar(nmsg, s_g, -MAGIC, None, op0=ALU.mult),
                      rd=[b_bstat], wr=[b_bstat], accum=True, tag="nmsg")
                ph2.close()
                em.barrier()
                if _LVL[STAGE] <= 2:
                    dump_y1()

                if _LVL[STAGE] <= 2:
                    ph_x.close()
                    ph.close()
                    return
                ph2b = ExitStack()
                wuT = ph2b.enter_context(nc.sbuf_tensor("wuT", [P, 8 * I], BF16))
                cg_in = ph2b.enter_context(nc.sbuf_tensor("cg_in", [P, 2 * E], F32))
                cg_t = ph2b.enter_context(nc.sbuf_tensor("cg_t", [P, E], F32))
                cg_g1 = ph2b.enter_context(nc.sbuf_tensor("cg_g1", [P, E], F32))
                cg_sq = ph2b.enter_context(nc.sbuf_tensor("cg_sq", [P, E], F32))
                cg_out = ph2b.enter_context(nc.sbuf_tensor("cg_out", [P, 2 * E], F32))
                cg_sig = cg_t
                b_wuT = mk_bufs(8)
                b_cgin = mk_bufs(2)
                b_cgt = TBuf()
                b_cgg1, b_cgsq = TBuf(), TBuf()
                b_cgsig = b_cgt
                b_cgout = mk_bufs(2)

                for ci in range(NIC):
                    wprep_chunk(3 * 131072, wuT, b_wuT, ci, "u")

                up_mx = mxsl[:, 112:208]
                b_umx = TBuf()
                gpmx_slots = mxsl[:, 208:232]
                b_gpsl = TBuf()

                def chain_ic(ic):
                    pi = ic % 2
                    ina = cg_in[:, pi * E:(pi + 1) * E]
                    outa = cg_out[:, pi * E:(pi + 1) * E]
                    em.op("sy", lambda s, ic=ic, ina=ina: s.dma_start(
                        out=ina, in_=graw_dram[ic * P:(ic + 1) * P, :]),
                        rd=[d_graw[ic]], wr=[b_cgin[pi]], dma=True, tag=f"cl{ic}")
                    em.op("sc", lambda s, ina=ina: s.activation(cg_t[:], ina, AF.Identity,
                                                                scale=r_g, bias=MAGIC),
                          rd=[b_cgin[pi], b_bstat], wr=[b_cgt],
                          deps=b_consts.rdeps(), tag=f"ct{ic}")
                    em.op("sc", lambda s: s.activation(cg_g1[:], cg_t[:], AF.Identity,
                                                       scale=s_g, bias=nmsg),
                          rd=[b_cgt, b_bstat], wr=[b_cgg1], tag=f"cg{ic}")
                    em.op("sc", lambda s: s.activation(cg_sig[:], cg_g1[:], AF.Sigmoid),
                          rd=[b_cgg1], wr=[b_cgsig], tag=f"cs{ic}")
                    em.op("sc", lambda s: s.activation(cg_sq[:], cg_sig[:], AF.Identity,
                                                       bias=MAGIC, scale=65536.0),
                          rd=[b_cgsig], wr=[b_cgsq], deps=b_consts.rdeps(), tag=f"c2{ic}")
                    em.op("ve", lambda v: v.tensor_scalar(cg_sq[:], cg_sq[:], MAGIC, 65535.0,
                                                          op0=ALU.subtract, op1=ALU.min),
                          rd=[b_cgsq], wr=[b_cgsq], tag=f"cm{ic}")
                    em.op("ve", lambda v, outa=outa: v.tensor_tensor(outa, cg_g1[:], cg_sq[:],
                                                                     ALU.mult),
                          rd=[b_cgg1, b_cgsq], wr=[b_cgout[pi]], tag=f"cp{ic}")
                    em.op("ve", lambda v, ic=ic, outa=outa: v.tensor_reduce(
                        gpmx_slots[:, ic:ic + 1], outa, axis=AX.X, op=ALU.max,
                        apply_absolute_value=True),
                        rd=[b_cgout[pi]], wr=[b_gpsl], accum=True, tag=f"cx{ic}")
                    em.op("gp", lambda g, ic=ic, outa=outa: g.dma_start(
                        out=gpint_dram[ic * P:(ic + 1) * P, :], in_=outa),
                        rd=[b_cgout[pi]], wr=[d_gpint[ic]], dma=True, tag=f"cw{ic}")

                for ic in range(NIC):
                    mm1_ic(wuT, b_wuT, uraw_dram, d_uraw, up_mx, b_umx, 2, "u", ic)
                    chain_ic(ic)

                gp_mx = mxsl[:, 292:293]
                b_gpmx = TBuf()
                em.op("ve", lambda v: v.tensor_reduce(gp_mx, gpmx_slots, axis=AX.X, op=ALU.max),
                      rd=[b_gpsl], wr=[b_gpmx], tag="gpr")
                umax_l = mxsl[:, 293:294]
                b_umaxl = TBuf()
                em.op("ve", lambda v: v.tensor_reduce(umax_l, up_mx, axis=AX.X, op=ALU.max),
                      rd=[b_umx], wr=[b_umaxl], tag="umr")
                both = mxsl[:, 294:296]
                b_both = TBuf()
                em.op("ve", lambda v: v.tensor_copy(both[:, 0:1], gp_mx), rd=[b_gpmx],
                      wr=[b_both], accum=True, tag="bb0")
                em.op("ve", lambda v: v.tensor_copy(both[:, 1:2], umax_l), rd=[b_umaxl],
                      wr=[b_both], accum=True, tag="bb1")
                g2col, b_g2 = all_reduce_max(both, b_both, 2, "b3")
                s_gp, r_gp, rg2 = bstat[:, 4:5], bstat[:, 5:6], bstat[:, 6:7]
                s_up, r_up = bstat[:, 7:8], bstat[:, 8:9]
                em.op("ve", lambda v: v.tensor_scalar(s_gp, g2col[:, 0:1], C32767 * 1.0, 0.0,
                                                      op0=ALU.mult, op1=ALU.max),
                      rd=[b_g2], wr=[b_bstat], accum=True, tag="sgp0")
                em.op("ve", lambda v: v.tensor_scalar(s_gp, s_gp, 2.0 ** -16, 1e-12,
                                                      op0=ALU.mult, op1=ALU.max),
                      rd=[b_bstat], wr=[b_bstat], accum=True, tag="sgp1")
                em.op("ve", lambda v: v.reciprocal(r_gp, s_gp), rd=[b_bstat], wr=[b_bstat],
                      accum=True, tag="rgp")
                em.op("ve", lambda v: v.tensor_scalar(rg2, r_gp, 2.0 ** -16, None, op0=ALU.mult),
                      rd=[b_bstat], wr=[b_bstat], accum=True, tag="rg2")
                em.op("ve", lambda v: v.tensor_scalar(s_up, g2col[:, 1:2], C32767, 1e-12,
                                                      op0=ALU.mult, op1=ALU.max),
                      rd=[b_g2], wr=[b_bstat], accum=True, tag="su")
                em.op("ve", lambda v: v.reciprocal(r_up, s_up), rd=[b_bstat], wr=[b_bstat],
                      accum=True, tag="ru")
                nm_gp = bstat[:, 12:13]
                nm_up = bstat[:, 13:14]
                em.op("ve", lambda v: v.tensor_scalar(nm_gp, s_gp, -MAGIC, None, op0=ALU.mult),
                      rd=[b_bstat], wr=[b_bstat], accum=True, tag="nmgp")
                em.op("ve", lambda v: v.tensor_scalar(nm_up, s_up, -MAGIC, None, op0=ALU.mult),
                      rd=[b_bstat], wr=[b_bstat], accum=True, tag="nmup")
            ph2b.close()
            ph_x.close()
            em.barrier()
            if _LVL[STAGE] <= 3:
                dump_y1()
                ph.close()
                return

            wdT = _A("wdT", [P, 24 * D], BF16)
            wdstg = _A("wdstg", [P, I], BF16)
            b_wdT = mk_bufs(24)
            b_wdstg = TBuf()
            for hc in range(8):
                src_ap = wsh_out[:, 6 * 131072 + hc * 49152:
                                 6 * 131072 + (hc + 1) * 49152]\
                    .rearrange("r (p c) -> p r c", p=P)
                em.op("sy", lambda s, hc=hc, src_ap=src_ap: s.dma_start(
                    out=wdstg[:].rearrange("p (r c) -> p r c", r=8), in_=src_ap),
                    rd=[b_ag], wr=[b_wdstg], dma=True, tag=f"wdl{hc}")
                for g in range(3):
                    for j in range(8):
                        ic = g * 8 + j
                        em.op("te", lambda t, ic=ic, j=j: t.transpose(
                            xq_ps[:, j * P:(j + 1) * P], wdstg[:, ic * P:(ic + 1) * P],
                            ident[:]),
                            rd=[b_wdstg, b_ident], wr=[b_xqps], accum=(j > 0),
                            tag=f"wdt{hc}_{ic}")
                    for j in range(8):
                        ic = g * 8 + j
                        em.op("sc", lambda s, ic=ic, j=j, hc=hc: s.copy(
                            wdT[:, ic * D + hc * P: ic * D + (hc + 1) * P],
                            xq_ps[:, j * P:(j + 1) * P]),
                            rd=[b_xqps], wr=[b_wdT[ic]], accum=True, tag=f"wde{hc}_{ic}")


            ph3 = ExitStack()
            if True:
                e2_g = ph3.enter_context(nc.sbuf_tensor("e2_g", [P, 2 * E], F32))
                e2_u = ph3.enter_context(nc.sbuf_tensor("e2_u", [P, 2 * E], F32))
                e2_tg = ph3.enter_context(nc.sbuf_tensor("e2_tg", [P, E], F32))
                e2_tu = ph3.enter_context(nc.sbuf_tensor("e2_tu", [P, E], F32))
                e2_o = ph3.enter_context(nc.sbuf_tensor("e2_o", [P, 2 * E], F32))
                b_e2g = mk_bufs(2)
                b_e2u = mk_bufs(2)
                b_e2tg, b_e2tu = TBuf(), TBuf()
                b_e2o = mk_bufs(2)
                oint_slots = mxsl[:, 232:256]
                b_osl = TBuf()
                s_gp, rg2 = bstat[:, 4:5], bstat[:, 6:7]
                s_up, r_up = bstat[:, 7:8], bstat[:, 8:9]
                nm_gp, nm_up = bstat[:, 12:13], bstat[:, 13:14]
                for ic in range(NIC):
                    pi = ic % 2
                    ga = e2_g[:, pi * E:(pi + 1) * E]
                    ua = e2_u[:, pi * E:(pi + 1) * E]
                    oa = e2_o[:, pi * E:(pi + 1) * E]
                    em.op("sy", lambda s, ic=ic, ga=ga: s.dma_start(
                        out=ga, in_=gpint_dram[ic * P:(ic + 1) * P, :]),
                        rd=[d_gpint[ic]], wr=[b_e2g[pi]], dma=True, tag=f"e2g{ic}")
                    em.op("sy", lambda s, ic=ic, ua=ua: s.dma_start(
                        out=ua, in_=uraw_dram[ic * P:(ic + 1) * P, :]),
                        rd=[d_uraw[ic]], wr=[b_e2u[pi]], dma=True, tag=f"e2u{ic}")
                    em.op("sc", lambda s, ga=ga: s.activation(e2_tg[:], ga, AF.Identity,
                                                              scale=rg2, bias=MAGIC),
                          rd=[b_e2g[pi], b_bstat], wr=[b_e2tg],
                          deps=b_consts.rdeps(), tag=f"etg{ic}")
                    em.op("sc", lambda s: s.activation(e2_tg[:], e2_tg[:], AF.Identity,
                                                       scale=s_gp, bias=nm_gp),
                          rd=[b_e2tg, b_bstat], wr=[b_e2tg], tag=f"eg2{ic}")
                    em.op("sc", lambda s, ua=ua: s.activation(e2_tu[:], ua, AF.Identity,
                                                              scale=r_up, bias=MAGIC),
                          rd=[b_e2u[pi], b_bstat], wr=[b_e2tu], tag=f"etu{ic}")
                    em.op("ve", lambda v: v.tensor_scalar(e2_tu[:], e2_tu[:], MAGIC, s_up,
                                                          op0=ALU.subtract, op1=ALU.mult),
                          rd=[b_e2tu, b_bstat], wr=[b_e2tu], tag=f"eu2{ic}")
                    em.op("ve", lambda v, oa=oa: v.tensor_tensor(oa, e2_tg[:], e2_tu[:], ALU.mult),
                          rd=[b_e2tg, b_e2tu], wr=[b_e2o[pi]], tag=f"eo{ic}")
                    em.op("ve", lambda v, ic=ic, oa=oa: v.tensor_reduce(
                        oint_slots[:, ic:ic + 1], oa, axis=AX.X, op=ALU.max,
                        apply_absolute_value=True),
                        rd=[b_e2o[pi]], wr=[b_osl], accum=True, tag=f"ex{ic}")
                    em.op("gp", lambda g, ic=ic, oa=oa: g.dma_start(
                        out=oint_dram[ic * P:(ic + 1) * P, :], in_=oa),
                        rd=[b_e2o[pi]], wr=[d_oint[ic]], dma=True, tag=f"es{ic}")
                omax_l = mxsl[:, 296:297]
                b_omaxl = TBuf()
                em.op("ve", lambda v: v.tensor_reduce(omax_l, oint_slots, axis=AX.X, op=ALU.max),
                      rd=[b_osl], wr=[b_omaxl], tag="omr")
                gocol, b_go = all_reduce_max(omax_l, b_omaxl, 1, "b4")
                s_o, r_o = bstat[:, 9:10], bstat[:, 10:11]
                em.op("ve", lambda v: v.tensor_scalar(s_o, gocol, C32767, 1e-12,
                                                      op0=ALU.mult, op1=ALU.max),
                      rd=[b_go], wr=[b_bstat], accum=True, tag="so")
                em.op("ve", lambda v: v.reciprocal(r_o, s_o), rd=[b_bstat], wr=[b_bstat],
                      accum=True, tag="ro")
            ph3.close()
            em.barrier()
            if _LVL[STAGE] <= 4:
                dump_y1()
                ph.close()
                return

            if True:
                dn_ps0 = fm_ps[4]
                dn_ps1 = fm_ps[5]
                obf = _A("obf", [P, 24 * T], BF16)
                e3_in = _A("e3_in", [P, 2 * E], F32)
                e3_t = _A("e3_t", [P, E], F32)
                f_out = _A("f_out", [P, D], F32)
                f_t = _A("f_t", [P, D], F32)
                f_y1 = _A("f_y1", [P, D], F32)
                stats_b = _A("stats_b", [P, 300], F32)
                stats_holder[0] = stats_b
                sc_slot[0] = 0
                dn_ps = [dn_ps0, dn_ps1]
                b_dnps = mk_bufs(2)
                b_obf = mk_bufs(24)
                b_e3in = mk_bufs(2)
                b_e3t = TBuf()
                b_fout, b_ft, b_fy1 = TBuf(), TBuf(), TBuf()
                s_o, r_o = bstat[:, 9:10], bstat[:, 10:11]

                for ic in range(NIC):
                    pi = ic % 2
                    ina = e3_in[:, pi * E:(pi + 1) * E]
                    em.op("sy", lambda s, ic=ic, ina=ina: s.dma_start(
                        out=ina, in_=oint_dram[ic * P:(ic + 1) * P, :]),
                        rd=[d_oint[ic]], wr=[b_e3in[pi]], dma=True, tag=f"e3l{ic}")
                    em.op("ve", lambda v, ina=ina: v.tensor_scalar(e3_t[:], ina, r_o, MAGIC,
                                                                   op0=ALU.mult, op1=ALU.add),
                          rd=[b_e3in[pi], b_bstat], wr=[b_e3t], tag=f"e3t{ic}")
                    em.op("ve", lambda v, ic=ic: v.tensor_scalar(
                        obf[:, ic * T:(ic + 1) * T], e3_t[:], MAGIC, None, op0=ALU.subtract),
                        rd=[b_e3t], wr=[b_obf[ic]], tag=f"e3q{ic}")

                for t_ in range(NT):
                    for ic in range(24):
                        for half in range(2):
                            em.op("te", lambda t, ic=ic, t_=t_, half=half: t.matmul(
                                dn_ps[half][:], obf[:, ic * T + t_ * P: ic * T + (t_ + 1) * P],
                                wdT[:, ic * D + half * 512: ic * D + half * 512 + 512],
                                start=(ic == 0), stop=(ic == 23)),
                                rd=[b_obf[ic], b_wdT[ic]], wr=[b_dnps[half]], accum=(ic > 0),
                                tag=f"dn{t_}_{half}_{ic}")
                    for half in range(2):
                        em.op("ve", lambda v, half=half: v.tensor_scalar(
                            f_out[:, half * 512:(half + 1) * 512], dn_ps[half][:], s_o, None,
                            op0=ALU.mult),
                            rd=[b_dnps[half], b_bstat], wr=[b_fout], accum=(half > 0),
                            tag=f"fo{t_}_{half}")
                    layernorm(f_out[:], b_fout, f_t[:], b_ft, f"l5_{t_}")
                    em.op("sy", lambda s, t_=t_: s.dma_start(
                        out=f_y1[:], in_=y1_dram[t_ * P:(t_ + 1) * P, :]),
                        rd=[d_y1[t_]], wr=[b_fy1], dma=True, tag=f"fyl{t_}")
                    em.op("ve", lambda v: v.tensor_tensor(f_t[:], f_t[:], f_y1[:], ALU.add),
                          rd=[b_ft, b_fy1], wr=[b_ft], tag=f"y2_{t_}")
                    tok = em.op("gp", lambda g, t_=t_: g.dma_start(
                        out=y_out[t_ * P:(t_ + 1) * P, :], in_=f_t[:]),
                        rd=[b_ft], dma=True, tag=f"ys{t_}")
                    em.force_inc(tok)
        ph.close()

    if STAGE != "a":
        phase_b()

    st.close()

    em.finalize()
    from contextlib import ExitStack as _ES
    semstack = _ES()
    sems = {}
    for name in em.sem_names():
        sems[name] = semstack.enter_context(nc.semaphore("s_" + name))
    with nc.Block() as block:
        @block.sync
        def _(sync):
            em.replay("sy", sync, sems)

        @block.scalar
        def _(scalar):
            em.replay("sc", scalar, sems)

        @block.vector
        def _(vector):
            em.replay("ve", vector, sems)

        @block.tensor
        def _(tensor):
            em.replay("te", tensor, sems)

        @block.gpsimd
        def _(gpsimd):
            em.replay("gp", gpsimd, sems)
            for s in range(Em.NSLOT):
                name = f"gp_d{s}"
                if em.final_vals.get(name):
                    gpsimd.wait_ge(sems[name], em.final_vals[name])
    semstack.close()

    return nc


_NC_CACHE = {}


def kernel(**inputs):
    if "prog" not in _NC_CACHE:
        _NC_CACHE["prog"] = build_program()
    nc = _NC_CACHE["prog"]

    xf = np.ascontiguousarray(np.asarray(inputs["L0_attn_x"], dtype=np.float32))
    shared = {
        "ln_g": np.ascontiguousarray(np.asarray(inputs["L0_attn_ln_g"], np.float32)),
        "ln_b": np.ascontiguousarray(np.asarray(inputs["L0_attn_ln_b"], np.float32)),
        "W_in": np.ascontiguousarray(np.asarray(inputs["L0_attn_W_in"], np.float32)),
        "b_in": np.ascontiguousarray(np.asarray(inputs["L0_attn_b_in"], np.float32)),
        "W_out": np.ascontiguousarray(np.asarray(inputs["L0_attn_W_out"], np.float32)),
        "b_out": np.ascontiguousarray(np.asarray(inputs["L0_attn_b_out"], np.float32)),
        "g_q": np.ascontiguousarray(np.asarray(inputs["L0_attn_gamma_q"], np.float32)),
        "be_q": np.ascontiguousarray(np.asarray(inputs["L0_attn_beta_q"], np.float32)),
        "g_k": np.ascontiguousarray(np.asarray(inputs["L0_attn_gamma_k"], np.float32)),
        "be_k": np.ascontiguousarray(np.asarray(inputs["L0_attn_beta_k"], np.float32)),
    }
    wg = np.asarray(inputs["L0_ffn_w_gate"], np.float32)
    wu = np.asarray(inputs["L0_ffn_w_up"], np.float32)
    wd = np.asarray(inputs["L0_ffn_w_down"], np.float32)
    in_maps = []
    for c in range(8):
        b, h = c // 2, c % 2
        xb = xf[b]
        xr = np.concatenate([xb[h * T:(h + 1) * T], xb[(1 - h) * T:(2 - h) * T]], axis=0)
        m = {"x": np.ascontiguousarray(xr),
             "wg_sh": np.ascontiguousarray(wg[c * 384:(c + 1) * 384]),
             "wu_sh": np.ascontiguousarray(wu[c * 384:(c + 1) * 384]),
             "wd_sh": np.ascontiguousarray(wd[:, c * 384:(c + 1) * 384])}
        m.update(shared)
        in_maps.append(m)

    res = run_bass_kernel_spmd(nc, in_maps, core_ids=list(range(8)))
    out = np.empty((4, L, D), dtype=np.float32)
    for c in range(8):
        b, h = c // 2, c % 2
        out[b, h * T:(h + 1) * T] = res.results[c]["y"]
    return out


# version 21
# speedup vs baseline: 1.0667x; 1.0416x over previous
import numpy as np
import ml_dtypes
import concourse.bass as bass
import concourse.mybir as mybir
import bass_rust
from concourse import library_config
from concourse.bass_utils import run_bass_kernel_spmd

F32 = mybir.dt.float32
BF16 = mybir.dt.bfloat16
AF = mybir.ActivationFunctionType
ALU = mybir.AluOpType
AX = mybir.AxisListType

D, E, S, I, L, T, P = 1024, 2048, 128, 3072, 4096, 2048, 128
NT = T // P
NKT = L // P
NIC = I // P
LN_EPS = 1e-5
MAGIC = 12582912.0
C7 = float(np.float32(1.0 / 7.0))
C32767 = float(np.float32(1.0 / 32767.0))
I4096 = float(np.float32(1.0 / 4096.0))

DEBUG_TAPS = False
STAGE = "full"
_LVL = {"a": 0, "b1": 1, "bw": 1.3, "bg": 1.6, "bmm": 2, "bchain": 3, "be2": 4, "full": 9}


class TBuf:
    __slots__ = ("producers", "readers")

    def __init__(self):
        self.producers = []
        self.readers = []

    def rdeps(self):
        return list(self.producers)

    def wdeps(self):
        return self.producers + self.readers

    def wrote(self, tok, reset=True):
        if reset:
            self.producers = [tok]
            self.readers = []
        else:
            self.producers.append(tok)

    def read(self, tok):
        self.readers.append(tok)


def mk_bufs(n):
    return [TBuf() for _ in range(n)]


class Em:
    ENGS = ("sy", "sc", "ve", "te", "gp")
    NSLOT = 8

    def __init__(self):
        self.ops = {e: [] for e in self.ENGS}
        self.barriers = []

    def barrier(self):
        snap = {}
        for eng in self.ENGS:
            if self.ops[eng]:
                self.ops[eng][-1]["inc"] = True
                snap[eng] = len(self.ops[eng])
        self.barriers.append(snap)

    def emit(self, eng, fn, deps=(), dma=False, cc=False, tag=""):
        idx = len(self.ops[eng])
        self.ops[eng].append({"fn": fn, "deps": [d for d in deps if d is not None],
                              "dma": dma, "cc": cc, "inc": False, "sem": None,
                              "val": 0, "prewait": None, "tag": tag,
                              "bar": len(self.barriers)})
        return (eng, idx)

    def op(self, eng, fn, rd=(), wr=(), deps=(), dma=False, cc=False, tag="", accum=False):
        dd = list(deps)
        for b in rd:
            dd += b.rdeps()
        for b in wr:
            dd += (b.rdeps() if accum else b.wdeps())
        best = {}
        keep = set()
        for d in dd:
            if d is None:
                continue
            pe, pi = d
            o = self.ops[pe][pi]
            if o["dma"] or o["cc"]:
                keep.add(d)
            elif pe == eng == "te":
                continue
            elif pi > best.get(pe, -1):
                best[pe] = pi
        dd = list(keep) + [(pe, pi) for pe, pi in best.items()]
        tok = self.emit(eng, fn, dd, dma=dma, cc=cc, tag=tag)
        for b in rd:
            b.read(tok)
        for b in wr:
            b.wrote(tok, reset=not accum)
        return tok

    def force_inc(self, tok):
        self.ops[tok[0]][tok[1]]["inc"] = True

    def finalize(self):
        for eng in self.ENGS:
            for o in self.ops[eng]:
                if o["dma"] or o["cc"]:
                    o["inc"] = True
                for (pe, pi) in o["deps"]:
                    self.ops[pe][pi]["inc"] = True
        self.final_vals = {}
        cc_n = 0
        for eng in self.ENGS:
            cum = 0
            dk = 0
            for o in self.ops[eng]:
                if o["cc"]:
                    slot = cc_n % 4
                    o["sem"] = f"cc{slot}"
                    o["val"] = cc_n // 4 + 1
                    if cc_n >= 4:
                        o["prewait"] = (o["sem"], o["val"] - 1)
                    cc_n += 1
                elif o["dma"]:
                    slot = dk % self.NSLOT
                    o["sem"] = f"{eng}_d{slot}"
                    o["val"] = 16 * (dk // self.NSLOT + 1)
                    if dk >= self.NSLOT:
                        o["prewait"] = (o["sem"], o["val"] - 16)
                    dk += 1
                elif o["inc"]:
                    cum += 1
                    o["sem"] = eng
                    o["val"] = cum
                if o["sem"] is not None:
                    self.final_vals[o["sem"]] = o["val"]
        self.bar_waits = []
        for snap in self.barriers:
            w = {}
            for eng, bound in snap.items():
                for o in self.ops[eng][:bound]:
                    if o["sem"] is not None and not o["cc"]:
                        w[o["sem"]] = max(w.get(o["sem"], 0), o["val"])
            self.bar_waits.append(w)

    def sem_names(self):
        names = list(self.ENGS) + [f"cc{j}" for j in range(4)]
        for eng in self.ENGS:
            for s in range(self.NSLOT):
                names.append(f"{eng}_d{s}")
        return names

    def replay(self, eng_name, eng, sems):
        watermark = {}
        cur_bar = 0
        for o in self.ops[eng_name]:
            need = {}
            if o["bar"] > cur_bar:
                for b in range(cur_bar, o["bar"]):
                    for sname, v in self.bar_waits[b].items():
                        if v > need.get(sname, 0):
                            need[sname] = v
                cur_bar = o["bar"]
            for (pe, pi) in o["deps"]:
                p = self.ops[pe][pi]
                if p["sem"] is None:
                    continue
                if p["val"] > need.get(p["sem"], 0):
                    need[p["sem"]] = p["val"]
            if o["prewait"] is not None:
                sname, v = o["prewait"]
                if v > need.get(sname, 0):
                    need[sname] = v
            for sname, v in sorted(need.items()):
                if v > watermark.get(sname, 0):
                    eng.wait_ge(sems[sname], v)
                    watermark[sname] = v
            inst = o["fn"](eng)
            if o["inc"]:
                assert inst is not None, f"inc op returned None: {o['tag']}"
                inst.then_inc(sems[o["sem"]], 16 if o["dma"] else 1)


def build_program():
    nc = bass.Bass("TRN2", target_bir_lowering=False)
    em = Em()
    dkind = dict(kind="ExternalOutput") if DEBUG_TAPS else {}

    x_in = nc.dram_tensor("x", [L, D], F32, kind="ExternalInput").ap()
    ln_g = nc.dram_tensor("ln_g", [D], F32, kind="ExternalInput").ap()
    ln_b = nc.dram_tensor("ln_b", [D], F32, kind="ExternalInput").ap()
    W_in = nc.dram_tensor("W_in", [D, 2 * E + S], F32, kind="ExternalInput").ap()
    b_in_d = nc.dram_tensor("b_in", [2 * E + S], F32, kind="ExternalInput").ap()
    W_out = nc.dram_tensor("W_out", [E, D], F32, kind="ExternalInput").ap()
    b_out_d = nc.dram_tensor("b_out", [D], F32, kind="ExternalInput").ap()
    g_q = nc.dram_tensor("g_q", [S], F32, kind="ExternalInput").ap()
    be_q = nc.dram_tensor("be_q", [S], F32, kind="ExternalInput").ap()
    g_k = nc.dram_tensor("g_k", [S], F32, kind="ExternalInput").ap()
    be_k = nc.dram_tensor("be_k", [S], F32, kind="ExternalInput").ap()
    wg_sh = nc.dram_tensor("wg_sh", [384, D], F32, kind="ExternalInput").ap()
    wu_sh = nc.dram_tensor("wu_sh", [384, D], F32, kind="ExternalInput").ap()
    wd_sh = nc.dram_tensor("wd_sh", [D, 384], F32, kind="ExternalInput").ap()
    y_out = nc.dram_tensor("y", [T, D], F32, kind="ExternalOutput").ap()

    u_dram = nc.dram_tensor("u_sc", [T, E], BF16, **dkind).ap()
    v_dram = nc.dram_tensor("v_sc", [L, E], BF16, **dkind).ap()
    xin_dram = nc.dram_tensor("xin_sc", [T, D], F32, **dkind).ap()
    y1_dram = nc.dram_tensor("y1_sc", [T, D], F32, **dkind).ap()
    y1n_dram = nc.dram_tensor("y1n_sc", [T, D], F32, **dkind).ap()
    wout_dram = nc.dram_tensor("woutbf_sc", [E, D], BF16).ap()
    SHN = 3 * 131072 * 2 + 8 * 49152
    wsh_in = nc.dram_tensor("wsh_in", [1, SHN], BF16).ap()
    wsh_out = nc.dram_tensor("wsh_out", [8, SHN], BF16).ap()
    VKG = 4 * P * E
    VKT = VKG + P * 512
    vkg_in = [nc.dram_tensor(f"vkg_in{g}", [1, VKT], BF16).ap() for g in range(4)]
    vkg_out = [nc.dram_tensor(f"vkg_out{g}", [2, VKT], BF16).ap() for g in range(4)]
    graw_dram = nc.dram_tensor("graw_sc", [I, T], F32, **dkind).ap()
    uraw_dram = nc.dram_tensor("uraw_sc", [I, T], F32, **dkind).ap()
    gpint_dram = nc.dram_tensor("gpint_sc", [I, T], F32, **dkind).ap()
    oint_dram = nc.dram_tensor("oint_sc", [I, T], F32, **dkind).ap()
    cc_in = {}
    cc_out = {}

    d_u = mk_bufs(NT)
    d_v = mk_bufs(NKT)
    d_xin = mk_bufs(NT)
    d_y1 = mk_bufs(NT)
    d_y1n = mk_bufs(NT)
    d_wout = mk_bufs(16)
    d_wsh = TBuf()
    b_ag = TBuf()
    d_vk = [TBuf() for _ in range(4)]
    b_vkag = [TBuf() for _ in range(4)]
    d_graw = mk_bufs(NIC)
    d_uraw = mk_bufs(NIC)
    d_gpint = mk_bufs(NIC)
    d_oint = mk_bufs(NIC)
    d_cc = mk_bufs(4)

    from contextlib import ExitStack
    st = ExitStack()
    b_y1nmx = TBuf()

    def sb(name, shape, dt):
        return st.enter_context(nc.sbuf_tensor(name, shape, dt))

    consts = sb("consts", [P, 4], F32)
    ident = sb("ident", [P, P], BF16)
    ones1 = sb("ones1", [1, P], BF16)
    gb_cols = sb("gb_cols", [P, 5], F32)
    lnT = sb("lnT", [P, 16], F32)
    mxsl = sb("mxsl", [P, 320], F32)
    gsc = sb("gsc", [P, 16], F32)
    bstat = sb("bstat", [P, 16], F32)
    bout_hl = sb("bout_hl", [1, 2 * D], BF16)
    atn = ExitStack()
    k_fm = atn.enter_context(nc.sbuf_tensor("k_fm", [P, L], BF16))
    q_fm = atn.enter_context(nc.sbuf_tensor("q_fm", [P, T], BF16))

    CONST_EPS = consts[:, 0:1]

    nc.const_aps.register(F32, LN_EPS, consts[:, 0:1])
    nc.const_aps.register(F32, MAGIC, consts[:, 1:2])
    nc.const_aps.register(F32, 65536.0, consts[:, 2:3])
    nc.const_aps.register(F32, I4096, consts[:, 3:4])

    b_consts = TBuf()
    b_ident = TBuf()
    b_ones = TBuf()
    b_gb = TBuf()
    b_lnT = TBuf()
    b_bout = TBuf()
    b_bstage = TBuf()
    b_bstat = TBuf()
    b_kfm = mk_bufs(NKT)
    b_qfm = mk_bufs(NT)

    em.op("gp", lambda g: g.memset(consts[:, 0:1], LN_EPS), wr=[b_consts], accum=True, tag="c0")
    em.op("gp", lambda g: g.memset(consts[:, 1:2], MAGIC), wr=[b_consts], accum=True, tag="c1")
    em.op("gp", lambda g: g.memset(consts[:, 2:3], 65536.0), wr=[b_consts], accum=True, tag="c2")
    em.op("gp", lambda g: g.memset(consts[:, 3:4], I4096), wr=[b_consts], accum=True, tag="c3")
    em.op("gp", lambda g: g.memset(ident[:], 0.0), wr=[b_ident], tag="id0")
    em.op("gp", lambda g: g.affine_select(out=ident[:], in_=ident[:], compare_op=ALU.not_equal,
                                          fill=1.0, base=0, pattern=[[-1, P]], channel_multiplier=1),
          rd=[b_ident], wr=[b_ident], tag="id1")
    em.op("gp", lambda g: g.memset(ones1[:], 1.0), wr=[b_ones], tag="ones")

    for j, src in enumerate([g_q, be_q, g_k, be_k]):
        em.op("sy", lambda s, j=j, src=src: s.dma_start(out=gb_cols[:, j:j + 1], in_=src[:, None]),
              wr=[b_gb], dma=True, accum=True, tag=f"gb{j}")
    em.op("sy", lambda s: s.dma_start(out=gb_cols[:, 4:5], in_=b_in_d[2 * E:2 * E + S, None]),
          wr=[b_gb], dma=True, accum=True, tag="bz")
    em.op("sy", lambda s: s.dma_start(out=lnT[:, 0:8], in_=ln_g.rearrange("(c p) -> p c", p=P),
                                        allow_slow_non_contiguous=True),
          wr=[b_lnT], dma=True, accum=True, tag="lng")
    em.op("sy", lambda s: s.dma_start(out=lnT[:, 8:16], in_=ln_b.rearrange("(c p) -> p c", p=P),
                                        allow_slow_non_contiguous=True),
          wr=[b_lnT], dma=True, accum=True, tag="lnb")
    sc_slot = [0]
    stats_holder = [None]

    def layernorm(x_ap, b_x, out_ap, b_out_t, tag):
        stats = stats_holder[0]
        s0 = sc_slot[0]
        sc_slot[0] += 18
        assert sc_slot[0] <= stats.shape[1]
        bns = stats[:, s0:s0 + 12]
        mv = stats[:, s0 + 12:s0 + 14]
        sd = stats[:, s0 + 14:s0 + 15]
        rstd = stats[:, s0 + 15:s0 + 16]
        nmr = stats[:, s0 + 16:s0 + 17]
        b_bns, b_mv, b_sd, b_rstd, b_nmr = TBuf(), TBuf(), TBuf(), TBuf(), TBuf()
        em.op("ve", lambda v: v.bn_stats(bns[:, 0:6], x_ap[:, 0:512]),
              rd=[b_x], wr=[b_bns], accum=True, tag=tag + "b0")
        em.op("ve", lambda v: v.bn_stats(bns[:, 6:12], x_ap[:, 512:1024]),
              rd=[b_x], wr=[b_bns], accum=True, tag=tag + "b1")
        em.op("ve", lambda v: v.bn_aggr(mv, bns), rd=[b_bns], wr=[b_mv], tag=tag + "ag")
        em.op("sc", lambda s: s.activation(sd, mv[:, 1:2], AF.Sqrt, bias=LN_EPS),
              rd=[b_mv], wr=[b_sd], deps=b_consts.rdeps(), tag=tag + "sd")
        em.op("ve", lambda v: v.reciprocal(rstd, sd), rd=[b_sd], wr=[b_rstd], tag=tag + "rs")
        em.op("ve", lambda v: v.scalar_tensor_tensor(nmr, mv[:, 0:1], -1.0, rstd,
                                                     op0=ALU.mult, op1=ALU.mult),
              rd=[b_mv, b_rstd], wr=[b_nmr], tag=tag + "nm")
        em.op("sc", lambda s: s.activation(out_ap, x_ap, AF.Identity, bias=nmr, scale=rstd),
              rd=[b_x, b_nmr, b_rstd], wr=[b_out_t], tag=tag + "ap")

    def layernorm12(x_ap, b_x, xin_ap, b_xin, na_ap, b_na, own, tag):
        stats = stats_holder[0]
        s0 = sc_slot[0]
        sc_slot[0] += 28
        assert sc_slot[0] <= stats.shape[1]
        bns = stats[:, s0:s0 + 12]
        mv = stats[:, s0 + 12:s0 + 14]
        ap2 = stats[:, s0 + 14:s0 + 16]
        sd2 = stats[:, s0 + 16:s0 + 18]
        rr2 = stats[:, s0 + 18:s0 + 20]
        nbs = stats[:, s0 + 20:s0 + 21]
        nb1 = stats[:, s0 + 21:s0 + 22]
        rv = stats[:, s0 + 22:s0 + 23]
        v2 = stats[:, s0 + 23:s0 + 24]
        b_st, b_mv, b_ap2, b_sd2, b_rr2, b_nb = TBuf(), TBuf(), TBuf(), TBuf(), TBuf(), TBuf()
        b_nb1 = TBuf()
        em.op("ve", lambda v: v.bn_stats(bns[:, 0:6], x_ap[:, 0:512]),
              rd=[b_x], wr=[b_st], accum=True, tag=tag + "b0")
        em.op("ve", lambda v: v.bn_stats(bns[:, 6:12], x_ap[:, 512:1024]),
              rd=[b_x], wr=[b_st], accum=True, tag=tag + "b1")
        em.op("ve", lambda v: v.bn_aggr(mv, bns), rd=[b_st], wr=[b_mv], tag=tag + "ag")
        em.op("ve", lambda v: v.tensor_scalar(ap2[:, 0:1], mv[:, 1:2], 1.0, LN_EPS,
                                              op0=ALU.mult, op1=ALU.add),
              rd=[b_mv], wr=[b_ap2], accum=True, tag=tag + "a1")
        em.op("ve", lambda v: v.reciprocal(rv, ap2[:, 0:1]), rd=[b_ap2], wr=[b_nb],
              accum=True, tag=tag + "rv")
        em.op("ve", lambda v: v.tensor_scalar(v2, mv[:, 1:2], rv, None, op0=ALU.mult),
              rd=[b_mv, b_nb], wr=[b_nb], accum=True, tag=tag + "v2")
        em.op("ve", lambda v: v.scalar_tensor_tensor(ap2[:, 1:2], v2, LN_EPS, ap2[:, 0:1],
                                                     op0=ALU.add, op1=ALU.mult),
              rd=[b_nb, b_ap2], wr=[b_ap2], accum=True, tag=tag + "pr")
        em.op("sc", lambda s: s.activation(sd2, ap2, AF.Sqrt),
              rd=[b_ap2], wr=[b_sd2], tag=tag + "sd")
        em.op("ve", lambda v: v.reciprocal(rr2, sd2), rd=[b_sd2], wr=[b_rr2], tag=tag + "rr")
        em.op("ve", lambda v: v.scalar_tensor_tensor(nbs, mv[:, 0:1], -1.0, rr2[:, 1:2],
                                                     op0=ALU.mult, op1=ALU.mult),
              rd=[b_mv, b_rr2], wr=[b_nb1], accum=True, tag=tag + "nb")
        em.op("sc", lambda s: s.activation(na_ap, x_ap, AF.Identity, bias=nbs,
                                           scale=rr2[:, 1:2]),
              rd=[b_x, b_nb1, b_rr2], wr=[b_na], tag=tag + "na")
        if own:
            em.op("ve", lambda v: v.scalar_tensor_tensor(nb1, mv[:, 0:1], -1.0, rr2[:, 0:1],
                                                         op0=ALU.mult, op1=ALU.mult),
                  rd=[b_mv, b_rr2], wr=[b_nb1], accum=True, tag=tag + "n1")
            em.op("sc", lambda s: s.activation(xin_ap, x_ap, AF.Identity, bias=nb1,
                                               scale=rr2[:, 0:1]),
                  rd=[b_x, b_nb1, b_rr2], wr=[b_xin], tag=tag + "x1")

    def phase_a1():
        ph = ExitStack()
        def _A(n, s, d):
            return ph.enter_context(nc.sbuf_tensor(n, s, d))
        def _Q(n, s, d):
            return ph.enter_context(nc.psum_tensor(n, s, d))
        tr_ps = _Q("tr_ps", [P, 1024], BF16)
        mm_ps0 = _Q("mm_ps0", [P, 512], F32)
        mm_ps1 = _Q("mm_ps1", [P, 512], F32)
        mm_ps2 = _Q("mm_ps2", [P, 512], F32)
        mm_ps3 = _Q("mm_ps3", [P, 512], F32)
        mm_ps4 = _Q("mm_ps4", [P, 512], F32)
        Wi_bf = _A("Wi_bf", [P, 8 * 4224], BF16)
        Wi_st = _A("Wi_st", [P, 2 * 4224], F32)
        binrow = _A("binrow", [1, 2 * 4224], BF16)
        xtile = _A("xtile", [P, 3 * D], F32)
        xintile = _A("xintile", [P, 3 * D], F32)
        na2 = _A("na2", [P, 4 * D], BF16)
        xnT = _A("xnT", [P, 4 * D], BF16)
        ztile = _A("ztile", [P, 2 * S], F32)
        uvout = _A("uvout", [P, 2 * E], BF16)
        w_st = _A("w_st", [P, D], F32)
        w_bfs = _A("w_bfs", [P, D], BF16)
        lp_in = _A("lp_in", [P, D], F32)
        lp_t = _A("lp_t", [P, D], F32)
        lp_t2 = _A("lp_t2", [P, D], F32)
        lp_q = _A("lp_q", [P, D], BF16)
        lp_st = _A("lp_st", [P, 96], F32)
        lnTb = _A("lnTb", [P, 16], BF16)
        stats_a1 = _A("stats_a1", [P, 600], F32)

        stats_holder[0] = stats_a1
        sc_slot[0] = 0
        mm_ps = [mm_ps0, mm_ps1, mm_ps2, mm_ps3, mm_ps4]
        b_trps = TBuf()
        b_mmps = mk_bufs(5)
        b_Wi = mk_bufs(8)
        b_Wist = TBuf()
        b_binrow = TBuf()
        b_xt = mk_bufs(4)
        b_xint = mk_bufs(4)
        b_na2 = mk_bufs(4)
        b_xnT = mk_bufs(4)
        b_z = TBuf()
        b_z2 = TBuf()
        b_uv = mk_bufs(4)
        b_wst = mk_bufs(2)
        b_wbf = TBuf()
        b_lpin = mk_bufs(2)
        b_lpt = TBuf()
        b_lpt2 = TBuf()
        b_lpq = mk_bufs(2)
        b_lpst = TBuf()

        b_Wist2 = [b_Wist, TBuf()]
        for d in range(8):
            wsl = Wi_st[:, (d % 2) * 4224:(d % 2 + 1) * 4224]
            em.op("sy", lambda s, d=d, wsl=wsl: s.dma_start(
                out=wsl, in_=W_in[d * P:(d + 1) * P, :]),
                wr=[b_Wist2[d % 2]], dma=True, tag=f"wi_ld{d}")
            em.op("ve", lambda v, d=d, wsl=wsl: v.tensor_copy(
                Wi_bf[:, d * 4224:(d + 1) * 4224], wsl),
                rd=[b_Wist2[d % 2]], wr=[b_Wi[d]], tag=f"wi_c{d}")
        b_lnTb = TBuf()
        for d in range(8):
            em.op("ve", lambda v, d=d: v.tensor_copy(lnTb[:, 2 * d:2 * d + 1],
                                                     lnT[:, 8 + d:9 + d]),
                  rd=[b_lnT], wr=[b_lnTb], accum=True, tag=f"bh{d}")
            em.op("ve", lambda v, d=d: v.tensor_tensor(lnTb[:, 2 * d + 1:2 * d + 2],
                                                       lnT[:, 8 + d:9 + d],
                                                       lnTb[:, 2 * d:2 * d + 1], ALU.subtract),
                  rd=[b_lnT, b_lnTb], wr=[b_lnTb], accum=True, tag=f"bl{d}")
        bstage = Wi_st[0:1, :]
        em.op("sy", lambda s: s.dma_start(out=bstage[0:1, 0:4224], in_=b_in_d[None, :]),
              wr=[b_Wist2[0], b_Wist2[1], b_bstage], dma=True, tag="bi_ld")
        b_bsum = b_lpt
        for j in range(8):
            bank, bbank = mm_ps[j % 2], b_mmps[j % 2]
            for d in range(8):
                for hl in range(2):
                    em.op("te", lambda t, d=d, j=j, hl=hl, bank=bank: t.matmul(
                        bank[0:1, 0:512], lnTb[:, 2 * d + hl:2 * d + hl + 1],
                        Wi_bf[:, d * 4224 + j * 512: d * 4224 + j * 512 + 512],
                        start=(d == 0 and hl == 0), stop=(d == 7 and hl == 1)),
                        rd=[b_lnTb, b_Wi[d]], wr=[bbank], accum=not (d == 0 and hl == 0),
                        tag=f"bw{j}_{d}_{hl}")
            em.op("ve", lambda v, j=j, bank=bank: v.tensor_tensor(
                lp_t[0:1, 0:512], bank[0:1, 0:512],
                bstage[0:1, j * 512:(j + 1) * 512], ALU.add),
                rd=[bbank, b_bstage], wr=[b_bsum], tag=f"bws{j}")
            em.op("ve", lambda v, j=j: v.tensor_copy(
                binrow[0:1, j * 512:(j + 1) * 512], lp_t[0:1, 0:512]),
                rd=[b_bsum], wr=[b_binrow], accum=True, tag=f"bwh{j}")
            em.op("ve", lambda v, j=j: v.tensor_tensor(
                binrow[0:1, 4224 + j * 512: 4224 + (j + 1) * 512], lp_t[0:1, 0:512],
                binrow[0:1, j * 512:(j + 1) * 512], ALU.subtract),
                rd=[b_bsum, b_binrow], wr=[b_binrow], accum=True, tag=f"bwl{j}")
        zbank, b_zbank = mm_ps[2], b_mmps[2]
        for d in range(8):
            for hl in range(2):
                em.op("te", lambda t, d=d, hl=hl: t.matmul(
                    zbank[:, 0:1], Wi_bf[:, d * 4224 + 2 * E: d * 4224 + 2 * E + S],
                    lnTb[:, 2 * d + hl:2 * d + hl + 1],
                    start=(d == 0 and hl == 0), stop=(d == 7 and hl == 1)),
                    rd=[b_lnTb, b_Wi[d]], wr=[b_zbank], accum=not (d == 0 and hl == 0),
                    tag=f"bwz{d}_{hl}")
        em.op("ve", lambda v: v.tensor_tensor(gb_cols[:, 4:5], gb_cols[:, 4:5],
                                              zbank[:, 0:1], ALU.add),
              rd=[b_gb, b_zbank], wr=[b_gb], accum=True, tag="bwzadd")
        for d in range(8):
            em.op("ve", lambda v, d=d: v.tensor_scalar(
                Wi_bf[:, d * 4224:(d + 1) * 4224], Wi_bf[:, d * 4224:(d + 1) * 4224],
                lnT[:, d:d + 1], None, op0=ALU.mult),
                rd=[b_lnT], wr=[b_Wi[d]], tag=f"wg{d}")

        em.op("sy", lambda s: s.dma_start(out=bstage[0:1, 0:D], in_=b_out_d[None, :]),
              wr=[b_Wist, b_bstage], dma=True, tag="bo_ld")
        em.op("ve", lambda v: v.tensor_copy(bout_hl[0:1, 0:D], bstage[0:1, 0:D]),
              rd=[b_bstage], wr=[b_bout], accum=True, tag="bo_hi")
        em.op("ve", lambda v: v.tensor_tensor(bout_hl[0:1, D:2 * D], bstage[0:1, 0:D],
                                              bout_hl[0:1, 0:D], ALU.subtract),
              rd=[b_bstage, b_bout], wr=[b_bout], accum=True, tag="bo_lo")

        lp_units = ([("g", k) for k in range(3)] + [("u", k) for k in range(3)]
                    + [("d", k) for k in range(8)])
        lp_pos = [0]

        def emit_lpbq_unit():
            if lp_pos[0] >= len(lp_units):
                return
            kind, key = lp_units[lp_pos[0]]
            pi = 0
            lp_pos[0] += 1
            if kind == "g":
                src_ap = wg_sh[key * P:(key + 1) * P, :]
                off, W = key * 131072, D
            elif kind == "u":
                src_ap = wu_sh[key * P:(key + 1) * P, :]
                off, W = 3 * 131072 + key * 131072, D
            else:
                src_ap = wd_sh[key * P:(key + 1) * P, :]
                off, W = 6 * 131072 + key * 49152, 384
            nb = W // 32
            dst_ap = wsh_in[0, off:off + P * W].rearrange("(p c) -> p c", p=P)
            inb = lp_in[:, pi * D:pi * D + W]
            qb = lp_q[:, pi * D:pi * D + W]
            bin_, bq_ = b_lpin[pi], b_lpq[pi]
            amax, ssc, rsc = lp_st[:, 0:nb], lp_st[:, 32:32 + nb], lp_st[:, 64:64 + nb]
            em.op("sy", lambda s: s.dma_start(out=inb, in_=src_ap), wr=[bin_], dma=True, tag="lpl")
            inv = inb.rearrange("p (b k) -> p b k", k=32)
            em.op("ve", lambda v: v.tensor_reduce(amax, inv, axis=AX.X, op=ALU.max,
                                                  apply_absolute_value=True),
                  rd=[bin_], wr=[b_lpst], tag="lpm")
            em.op("ve", lambda v: v.tensor_scalar(ssc, amax, C7, 1e-12, op0=ALU.mult, op1=ALU.max),
                  rd=[b_lpst], wr=[b_lpst], accum=True, tag="lps")
            em.op("ve", lambda v: v.reciprocal(rsc, ssc), rd=[b_lpst], wr=[b_lpst],
                  accum=True, tag="lpr")
            sb3 = ssc.unsqueeze(2).broadcast_to([P, nb, 32])
            rb3 = rsc.unsqueeze(2).broadcast_to([P, nb, 32])
            tv = lp_t[:, 0:W].rearrange("p (b k) -> p b k", k=32)
            t2v = lp_t2[:, 0:W].rearrange("p (b k) -> p b k", k=32)
            qv = qb.rearrange("p (b k) -> p b k", k=32)
            em.op("ve", lambda v: v.tensor_tensor(tv, inv, rb3, ALU.mult),
                  rd=[bin_, b_lpst], wr=[b_lpt], tag="lpt")
            em.op("ve", lambda v: v.tensor_scalar(lp_t2[:, 0:W], lp_t[:, 0:W], MAGIC, MAGIC,
                                                  op0=ALU.add, op1=ALU.subtract),
                  rd=[b_lpt], wr=[b_lpt2], tag="lpn")
            em.op("ve", lambda v: v.tensor_tensor(qv, t2v, sb3, ALU.mult),
                  rd=[b_lpt2, b_lpst], wr=[bq_], tag="lpq")
            em.op("gp", lambda g: g.dma_start(out=dst_ap, in_=qb), rd=[bq_], wr=[d_wsh],
                  dma=True, accum=True, tag="lpw")


        wo_pos = [0]

        def emit_wout_unit():
            if wo_pos[0] >= 16:
                return
            e = wo_pos[0]
            pi = 0
            wo_pos[0] += 1
            stg = w_st[:, 0:D]
            em.op("sy", lambda s, e=e, stg=stg: s.dma_start(out=stg, in_=W_out[e * P:(e + 1) * P, :]),
                  wr=[b_wst[pi], b_wbf], dma=True, tag="wol")
            em.op("ve", lambda v, stg=stg: v.tensor_copy(w_bfs[:], stg), rd=[b_wst[pi]],
                  wr=[b_wbf], tag="woc")
            em.op("gp", lambda g, e=e: g.dma_start(out=wout_dram[e * P:(e + 1) * P, :],
                                                   in_=w_bfs[:]),
                  rd=[b_wbf], wr=[d_wout[e]], dma=True, tag="wos")

        def a1_front(abs_t, own):
            p3 = abs_t % 3
            pi = abs_t % 4
            xa = xtile[:, p3 * D:(p3 + 1) * D]
            xia = xintile[:, p3 * D:(p3 + 1) * D]
            naa = na2[:, pi * D:(pi + 1) * D]
            xnTa = xnT[:, pi * D:(pi + 1) * D]
            bx, bxi, bna, bxnT = b_xt[p3], b_xint[p3], b_na2[pi], b_xnT[pi]

            em.op("sy", lambda s: s.dma_start(out=xa, in_=x_in[abs_t * P:(abs_t + 1) * P, :]),
                  wr=[bx], dma=True, tag=f"xl{abs_t}")
            layernorm12(xa, bx, xia, bxi, naa, bna, own, f"l12_{abs_t}")
            if own:
                em.op("gp", lambda g: g.dma_start(out=xin_dram[abs_t * P:(abs_t + 1) * P, :],
                                                  in_=xia),
                      rd=[bxi], wr=[d_xin[abs_t]], dma=True, tag=f"xis{abs_t}")
            for dd in range(8):
                em.op("te", lambda t, dd=dd, naa=naa: t.transpose(
                    tr_ps[:, dd * P:(dd + 1) * P], naa[:, dd * P:(dd + 1) * P], ident[:]),
                    rd=[bna, b_ident], wr=[b_trps], accum=(dd > 0), tag=f"tr{abs_t}_{dd}")
            em.op("ve", lambda v, xnTa=xnTa: v.tensor_copy(xnTa, tr_ps[:]),
                  rd=[b_trps], wr=[bxnT], tag=f"te{abs_t}")

        def a1_back(abs_t, own):
            pi = abs_t % 4
            pu = abs_t % 2
            xnTa = xnT[:, pi * D:(pi + 1) * D]
            bxnT = b_xnT[pi]

            zp, bzp = mm_ps[4], b_mmps[4]
            for dd in range(8):
                em.op("te", lambda t, dd=dd: t.matmul(
                    zp[:, 0:S], Wi_bf[:, dd * 4224 + 2 * E: dd * 4224 + 2 * E + S],
                    xnTa[:, dd * P:(dd + 1) * P], start=(dd == 0), stop=(dd == 7)),
                    rd=[b_Wi[dd], bxnT], wr=[bzp], accum=(dd > 0), tag=f"zm{abs_t}_{dd}")
            zt = ztile[:, (abs_t % 2) * S:(abs_t % 2 + 1) * S]
            bz2 = b_z if abs_t % 2 == 0 else b_z2
            em.op("sc", lambda s, zt=zt: s.activation(zt, zp[:, 0:S], AF.Silu,
                                                      bias=gb_cols[:, 4:5]),
                  rd=[bzp, b_gb], wr=[bz2], tag=f"ze{abs_t}")
            em.op("sc", lambda s, zt=zt: s.activation(k_fm[:, abs_t * P:(abs_t + 1) * P], zt,
                                                      AF.Identity, bias=gb_cols[:, 3:4],
                                                      scale=gb_cols[:, 2:3]),
                  rd=[bz2, b_gb], wr=[b_kfm[abs_t]], tag=f"kv{abs_t}")
            if own:
                em.op("sc", lambda s, zt=zt: s.activation(q_fm[:, abs_t * P:(abs_t + 1) * P], zt,
                                                          AF.Identity, bias=gb_cols[:, 1:2],
                                                          scale=gb_cols[:, 0:1]),
                      rd=[bz2, b_gb], wr=[b_qfm[abs_t]], tag=f"qv{abs_t}")

            vdst = vkg_in[abs_t // 4][0, (abs_t % 4) * P * E:(abs_t % 4 + 1) * P * E]\
                .rearrange("(p c) -> p c", p=P)
            specs = [("v", E, vdst, d_vk[abs_t // 4], abs_t * P)]
            if own:
                specs.append(("u", 0, u_dram, d_u[abs_t], abs_t * P))
            for which, col0, dstd, dbuf, row in specs:
                uvb = b_uv[pu]
                for dd in range(8):
                    for eb in range(4):
                        bank, bbank = mm_ps[eb], b_mmps[eb]
                        c0 = col0 + eb * 512
                        em.op("te", lambda t, dd=dd, c0=c0, bank=bank: t.matmul(
                            bank[:], xnTa[:, dd * P:(dd + 1) * P],
                            Wi_bf[:, dd * 4224 + c0: dd * 4224 + c0 + 512],
                            start=(dd == 0), stop=False),
                            rd=[b_Wi[dd], bxnT], wr=[bbank], accum=(dd > 0),
                            tag=f"{which}m{abs_t}_{eb}_{dd}")
                for eb in range(4):
                    bank, bbank = mm_ps[eb], b_mmps[eb]
                    c0 = col0 + eb * 512
                    em.op("te", lambda t, c0=c0, bank=bank: t.matmul(
                        bank[:], ones1[:], binrow[0:1, c0:c0 + 512], start=False, stop=False),
                        rd=[b_ones, b_binrow], wr=[bbank], accum=True, tag=f"{which}h{abs_t}_{eb}")
                    em.op("te", lambda t, c0=c0, bank=bank: t.matmul(
                        bank[:], ones1[:], binrow[0:1, 4224 + c0:4224 + c0 + 512],
                        start=False, stop=True),
                        rd=[b_ones, b_binrow], wr=[bbank], accum=True, tag=f"{which}l{abs_t}_{eb}")
                    em.op("sc", lambda s, eb=eb, bank=bank: s.activation(
                        uvout[:, pu * E + eb * 512: pu * E + eb * 512 + 512], bank[:], AF.Silu),
                        rd=[bbank], wr=[uvb], accum=(eb > 0), tag=f"{which}e{abs_t}_{eb}")
                em.op("gp", lambda g, dstd=dstd, row=row, which=which: g.dma_start(
                    out=(dstd[row:row + P, :] if which == "u" else dstd),
                    in_=uvout[:, pu * E:(pu + 1) * E]),
                    rd=[uvb], wr=[dbuf], dma=True, accum=(which == "v"),
                    tag=f"{which}s{abs_t}")

        a1_front(0, True)
        a1_front(1, True)
        for t_ in range(NT):
            if t_ + 2 < NT:
                a1_front(t_ + 2, True)
            a1_back(t_, True)
            emit_lpbq_unit()
            emit_wout_unit()
            if t_ % 4 == 3:
                gidx = t_ // 4
                em.op("gp", lambda g, gidx=gidx: g.dma_start(
                    out=vkg_in[gidx][0, VKG:VKT].rearrange("(p c) -> p c", p=P),
                    in_=k_fm[:, gidx * 512:(gidx + 1) * 512]),
                    rd=b_kfm[4 * gidx:4 * gidx + 4], wr=[d_vk[gidx]], dma=True,
                    accum=True, tag=f"kship{gidx}")
                em.op("gp", lambda g, gidx=gidx: g.collective_compute(
                    "AllGather", ALU.bypass,
                    replica_groups=[[0, 1], [2, 3], [4, 5], [6, 7]],
                    ins=[vkg_in[gidx].opt()], outs=[vkg_out[gidx].opt()]),
                    rd=[d_vk[gidx]], wr=[b_vkag[gidx]], cc=True, tag=f"vk_ag{gidx}")
        while lp_pos[0] < len(lp_units):
            emit_lpbq_unit()
        while wo_pos[0] < 16:
            emit_wout_unit()
        em.op("gp", lambda g: g.collective_compute(
            "AllGather", ALU.bypass, replica_groups=[list(range(8))],
            ins=[wsh_in.opt()], outs=[wsh_out.opt()]),
            rd=[d_wsh], wr=[b_ag], cc=True, tag="wsh_ag")
        ph.close()

    y1n_mx = mxsl[:, 0:16]
    phase_a1()
    em.barrier()

    def phase_a2():
        ph = ExitStack()
        def _A(n, s, d):
            return ph.enter_context(nc.sbuf_tensor(n, s, d))
        def _Q(n, s, d):
            return ph.enter_context(nc.psum_tensor(n, s, d))
        sc_ps0 = _Q("sc_ps0", [P, 512], F32)
        sc_ps1 = _Q("sc_ps1", [P, 512], F32)
        av_ps0 = _Q("av_ps0", [P, 512], F32)
        av_ps1 = _Q("av_ps1", [P, 512], F32)
        av_ps2 = _Q("av_ps2", [P, 512], F32)
        av_ps3 = _Q("av_ps3", [P, 512], F32)
        pj_ps0 = _Q("pj_ps0", [P, 512], F32)
        pj_ps1 = _Q("pj_ps1", [P, 512], F32)
        v_sb = _A("v_sb", [P, NKT * E], BF16)
        scT = _A("scT", [P, L], BF16)
        scf = _A("scf", [P, 1024], F32)
        u_sb = _A("u_sb", [P, 2 * E], BF16)
        m_sb = _A("m_sb", [P, E], BF16)
        mT_sb = _A("mT_sb", [P, E], BF16)
        wo_sb = _A("wo_sb", [P, 8 * D], BF16)
        ep1 = _A("ep1", [P, D], F32)
        ep2 = _A("ep2", [P, D], F32)
        ep3 = _A("ep3", [P, D], F32)
        stats_a2 = _A("stats_a2", [P, 600], F32)

        stats_holder[0] = stats_a2
        sc_slot[0] = 0
        av_ps = [av_ps0, av_ps1, av_ps2, av_ps3]
        sc_ps = [sc_ps0, sc_ps1]
        pj_ps = [pj_ps0, pj_ps1]
        mt_ps = [av_ps0[:].bitcast(BF16), av_ps2[:].bitcast(BF16)]
        b_scps = mk_bufs(2)
        b_scf2 = mk_bufs(2)
        b_avps = mk_bufs(4)
        b_pjps = mk_bufs(2)
        b_vsb = mk_bufs(NKT)
        b_scT = mk_bufs(8)
        b_scf = TBuf()
        b_usb = mk_bufs(2)
        b_msb = mk_bufs(4)
        b_mTsb = mk_bufs(2)
        b_wosb = mk_bufs(2)
        b_ep1, b_ep2, b_ep3 = TBuf(), TBuf(), TBuf()

        for half in range(2):
            for g_ in range(4):
                em.op("sy", lambda s, half=half, g_=g_: s.dma_start(
                    out=k_fm[:, half * T + g_ * 512: half * T + (g_ + 1) * 512],
                    in_=vkg_out[g_][half, VKG:VKT].rearrange("(p c) -> p c", p=P)),
                    rd=[b_vkag[g_]], wr=b_kfm[half * NT + 4 * g_: half * NT + 4 * g_ + 4],
                    dma=True, tag=f"krl{half}_{g_}")
        for g_ in range(4):
            for half in range(2):
                for i_ in range(4):
                    kt = half * NT + g_ * 4 + i_
                    em.op("sy", lambda s, kt=kt, g_=g_, i_=i_, half=half: s.dma_start(
                        out=v_sb[:, kt * E:(kt + 1) * E],
                        in_=vkg_out[g_][half, i_ * P * E:(i_ + 1) * P * E]
                        .rearrange("(p c) -> p c", p=P)),
                        rd=[b_vkag[g_]], wr=[b_vsb[kt]], dma=True, tag=f"vr{kt}")

        def emit_scores(q):
            for g4 in range(8):
                bank, bbank = sc_ps[g4 % 2], b_scps[g4 % 2]
                for j in range(4):
                    kt = g4 * 4 + j
                    em.op("te", lambda t, kt=kt, j=j, q=q, bank=bank: t.matmul(
                        bank[:, j * P:(j + 1) * P], k_fm[:, kt * P:(kt + 1) * P],
                        q_fm[:, q * P:(q + 1) * P], start=True, stop=True),
                        rd=[b_kfm[kt], b_qfm[q]], wr=[bbank], accum=(j > 0), tag=f"sc{q}_{kt}")
                em.op("sc", lambda s, bank=bank, g4=g4, q=q: s.activation(
                    scf[:, (g4 % 2) * 512:(g4 % 2 + 1) * 512], bank[:], AF.Relu, scale=I4096),
                    rd=[bbank], wr=[b_scf2[g4 % 2]], deps=b_consts.rdeps(),
                    tag=f"se{q}_{g4}")
                em.op("ve", lambda v, g4=g4, q=q: v.tensor_tensor(
                    scT[:, g4 * 512:(g4 + 1) * 512], scf[:, (g4 % 2) * 512:(g4 % 2 + 1) * 512],
                    scf[:, (g4 % 2) * 512:(g4 % 2 + 1) * 512], ALU.mult),
                    rd=[b_scf2[g4 % 2]], wr=[b_scT[g4]], tag=f"sq{q}_{g4}")

        emit_scores(0)

        for q in range(NT):
            for g in range(4):
                em.op("sy", lambda s, g=g: s.dma_start(
                    out=wo_sb[:, (g % 2) * 4 * D:((g % 2) + 1) * 4 * D].rearrange(
                        "p (c h) -> p c h", c=4),
                    in_=wout_dram[g * 512:(g + 1) * 512, :].rearrange("(c p) h -> p c h", p=P)),
                    rd=[d_wout[4 * g], d_wout[4 * g + 1], d_wout[4 * g + 2], d_wout[4 * g + 3]],
                    wr=[b_wosb[g % 2]], dma=True, tag=f"wr{q}_{g}")
            upi = q % 2
            em.op("sy", lambda s, q=q, upi=upi: s.dma_start(
                out=u_sb[:, upi * E:(upi + 1) * E], in_=u_dram[q * P:(q + 1) * P, :]),
                rd=[d_u[q]], wr=[b_usb[upi]], dma=True, tag=f"ur{q}")
            for kt in range(NKT):
                for eb in range(4):
                    em.op("te", lambda t, kt=kt, eb=eb: t.matmul(
                        av_ps[eb][:], scT[:, kt * P:(kt + 1) * P],
                        v_sb[:, kt * E + eb * 512: kt * E + eb * 512 + 512],
                        start=(kt == 0), stop=(kt == NKT - 1)),
                        rd=[b_scT[kt // 4], b_vsb[kt]], wr=[b_avps[eb]], accum=(kt > 0),
                        tag=f"av{q}_{eb}_{kt}")
            if q + 1 < NT:
                emit_scores(q + 1)
            for eb in range(4):
                em.op("ve", lambda v, eb=eb, upi=upi: v.tensor_tensor(
                    m_sb[:, eb * 512:(eb + 1) * 512], av_ps[eb][:],
                    u_sb[:, upi * E + eb * 512: upi * E + eb * 512 + 512], ALU.mult),
                    rd=[b_avps[eb], b_usb[upi]], wr=[b_msb[eb]], tag=f"m{q}_{eb}")
            for g in range(2):
                mtb = mt_ps[g]
                wrb = [b_avps[2 * g], b_avps[2 * g + 1]]
                for j in range(8):
                    e = g * 8 + j
                    em.op("te", lambda t, e=e, j=j, mtb=mtb: t.transpose(
                        mtb[:, j * P:(j + 1) * P], m_sb[:, e * P:(e + 1) * P], ident[:]),
                        rd=[b_msb[e // 4], b_ident], wr=wrb, accum=(j > 0), tag=f"mt{q}_{e}")
                em.op("sc", lambda s, g=g, mtb=mtb: s.copy(mT_sb[:, g * 1024:(g + 1) * 1024],
                                                           mtb[:]),
                      rd=wrb, wr=[b_mTsb[g]], tag=f"me{q}_{g}")
            for half in range(2):
                em.op("te", lambda t, half=half: t.matmul(
                    pj_ps[half][:], ones1[:], bout_hl[0:1, half * 512:(half + 1) * 512],
                    start=True, stop=False),
                    rd=[b_ones, b_bout], wr=[b_pjps[half]], tag=f"ph{q}_{half}")
                em.op("te", lambda t, half=half: t.matmul(
                    pj_ps[half][:], ones1[:], bout_hl[0:1, D + half * 512: D + half * 512 + 512],
                    start=False, stop=False),
                    rd=[b_ones, b_bout], wr=[b_pjps[half]], accum=True, tag=f"pl{q}_{half}")
            for e in range(16):
                for half in range(2):
                    em.op("te", lambda t, e=e, half=half: t.matmul(
                        pj_ps[half][:], mT_sb[:, e * P:(e + 1) * P],
                        wo_sb[:, (e % 8) * D + half * 512: (e % 8) * D + half * 512 + 512],
                        start=False, stop=(e == 15)),
                        rd=[b_mTsb[e // 8], b_wosb[(e // 4) % 2]], wr=[b_pjps[half]], accum=True,
                        tag=f"pj{q}_{e}_{half}")
            em.op("sy", lambda s, q=q: s.dma_start(out=ep1[:], in_=xin_dram[q * P:(q + 1) * P, :]),
                  rd=[d_xin[q]], wr=[b_ep1], dma=True, tag=f"xir{q}")
            em.op("sy", lambda s, q=q: s.dma_start(out=ep2[:], in_=x_in[q * P:(q + 1) * P, :]),
                  wr=[b_ep2], dma=True, tag=f"xr{q}")
            for half in range(2):
                em.op("ve", lambda v, half=half: v.tensor_tensor(
                    ep1[:, half * 512:(half + 1) * 512], pj_ps[half][:],
                    ep1[:, half * 512:(half + 1) * 512], ALU.add),
                    rd=[b_pjps[half], b_ep1], wr=[b_ep1], accum=True, tag=f"ao{q}_{half}")
            layernorm(ep1[:], b_ep1, ep3[:], b_ep3, f"l3_{q}")
            em.op("ve", lambda v: v.tensor_tensor(ep2[:], ep2[:], ep3[:], ALU.add),
                  rd=[b_ep2, b_ep3], wr=[b_ep2], tag=f"y1_{q}")
            em.op("gp", lambda g, q=q: g.dma_start(out=y1_dram[q * P:(q + 1) * P, :], in_=ep2[:]),
                  rd=[b_ep2], wr=[d_y1[q]], dma=True, tag=f"y1s{q}")
            layernorm(ep2[:], b_ep2, ep3[:], b_ep3, f"l4_{q}")
            em.op("gp", lambda g, q=q: g.dma_start(out=y1n_dram[q * P:(q + 1) * P, :], in_=ep3[:]),
                  rd=[b_ep3], wr=[d_y1n[q]], dma=True, tag=f"y1ns{q}")
            em.op("ve", lambda v, q=q: v.tensor_reduce(y1n_mx[:, q:q + 1], ep3[:], axis=AX.X,
                                                       op=ALU.max, apply_absolute_value=True),
                  rd=[b_ep3], wr=[b_y1nmx], accum=True, tag=f"ymx{q}")
        ph.close()

    phase_a2()
    atn.close()
    em.barrier()

    def dump_y1():
        for t_ in range(NT):
            tok = em.op("gp", lambda g, t_=t_: g.dma_start(
                out=y_out[t_ * P:(t_ + 1) * P, :], in_=y1_dram[t_ * P:(t_ + 1) * P, :]),
                rd=[d_y1[t_]], dma=True, tag=f"dump_y{t_}")
            em.force_inc(tok)

    if STAGE == "a":
        dump_y1()

    barrier_n = [0]
    b_pmst = TBuf()

    def all_reduce_max(col_ap, b_col, nvals, tag):
        bi = barrier_n[0]
        barrier_n[0] += 1
        cc_in[bi] = nc.dram_tensor(f"cc_in{bi}", [1, nvals], F32)
        cc_out[bi] = nc.dram_tensor(f"cc_out{bi}", [1, nvals], F32)
        pm = mxsl[:, 300 + 2 * bi:302 + 2 * bi]
        em.op("gp", lambda g: g.tensor_reduce(pm[0:1, 0:nvals], col_ap, axis=AX.C, op=ALU.max),
              rd=[b_col], wr=[b_pmst], accum=True, tag=tag + "p")
        em.op("gp", lambda g: g.dma_start(out=cc_in[bi].ap()[0:1, 0:nvals], in_=pm[0:1, 0:nvals]),
              rd=[b_pmst], wr=[d_cc[bi]], dma=True, tag=tag + "w")
        em.op("gp", lambda g: g.collective_compute(
            "AllReduce", ALU.max, replica_groups=[list(range(8))],
            ins=[cc_in[bi].ap().opt()], outs=[cc_out[bi].ap().opt()]),
            rd=[d_cc[bi]], wr=[d_cc[bi]], accum=True, cc=True, tag=tag + "c")
        gcol = gsc[:, 4 * bi:4 * bi + nvals]
        b_g = TBuf()
        em.op("sy", lambda s: s.dma_start(out=gcol,
                                          in_=cc_out[bi].ap()[0, 0:nvals].partition_broadcast(P)),
              rd=[d_cc[bi]], wr=[b_g], dma=True, tag=tag + "b")
        return gcol, b_g

    def phase_b():
        ph = ExitStack()
        def _A(n, s, d):
            return ph.enter_context(nc.sbuf_tensor(n, s, d))
        def _Q(n, s, d):
            return ph.enter_context(nc.psum_tensor(n, s, d))
        xq_ps = _Q("xq_ps", [P, 1024], BF16)
        fm_ps = [_Q(f"fm_ps{i}", [P, 512], F32) for i in range(6)]
        b_xqps = TBuf()
        b_fmps = mk_bufs(6)

        ph_x = ExitStack()
        if True:
            xqT = ph_x.enter_context(nc.sbuf_tensor("xqT", [P, 8 * T], BF16))
            b_xqT = mk_bufs(8)

            wstg = ph_x.enter_context(nc.sbuf_tensor("wstg", [P, 2 * D], BF16))
            mrow = ph_x.enter_context(nc.sbuf_tensor("mrow", [P, 2 * E], F32))
            ytile = ph_x.enter_context(nc.sbuf_tensor("ytile", [P, 2 * D], F32))
            yq = ph_x.enter_context(nc.sbuf_tensor("yq", [P, D], BF16))
            ph2 = ExitStack()
            if True:
                wgT = ph2.enter_context(nc.sbuf_tensor("wgT", [P, 8 * I], BF16))
                b_wgT = mk_bufs(8)
                b_wstg2 = mk_bufs(2)
                b_mrow = mk_bufs(2)

                def wprep_chunk(base, wT_sb, b_wT, ci, name):
                    wp = ci % 2
                    wsa = wstg[:, wp * D:(wp + 1) * D]
                    off = base + (ci // 3) * 0
                    j, k = ci // 3, ci % 3
                    src_ap = wsh_out[j, base + k * 131072: base + (k + 1) * 131072]\
                        .rearrange("(p c) -> p c", p=P)
                    em.op("sy", lambda s, wsa=wsa, src_ap=src_ap: s.dma_start(
                        out=wsa, in_=src_ap),
                        rd=[b_ag], wr=[b_wstg2[wp]], dma=True, tag=f"{name}wl{ci}")
                    for dd in range(8):
                        em.op("te", lambda t, dd=dd, wsa=wsa: t.transpose(
                            xq_ps[:, dd * P:(dd + 1) * P], wsa[:, dd * P:(dd + 1) * P], ident[:]),
                            rd=[b_wstg2[wp], b_ident], wr=[b_xqps], accum=(dd > 0),
                            tag=f"{name}wt{ci}_{dd}")
                    wv3 = wT_sb.rearrange("p (d i) -> p d i", d=8)[:, :, ci * P:(ci + 1) * P]
                    em.op("ve", lambda v, wv3=wv3: v.tensor_copy(
                        wv3, xq_ps[:].rearrange("p (d i) -> p d i", d=8)),
                        rd=[b_xqps], wr=b_wT, accum=True, tag=f"{name}we{ci}")

                def mm1_ic(wT_sb, b_wT, dst_dram, d_dst, mx0, b_mx, bank_off, name, ic):
                    pi = ic % 2
                    rowa = mrow[:, pi * E:(pi + 1) * E]
                    bk0 = (ic % 3) * 2
                    for dd in range(8):
                        for tb in range(4):
                            bank = fm_ps[(bk0 + tb) % 6]
                            bbank = b_fmps[(bk0 + tb) % 6]
                            em.op("te", lambda t, dd=dd, ic=ic, tb=tb, bank=bank: t.matmul(
                                bank[:], wT_sb[:, dd * I + ic * P: dd * I + (ic + 1) * P],
                                xqT[:, dd * T + tb * 512: dd * T + tb * 512 + 512],
                                start=(dd == 0), stop=(dd == 7)),
                                rd=[b_wT[dd], b_xqT[dd]], wr=[bbank], accum=(dd > 0),
                                tag=f"{name}m{ic}_{tb}_{dd}")
                    for tb in range(4):
                        bank = fm_ps[(bk0 + tb) % 6]
                        bbank = b_fmps[(bk0 + tb) % 6]
                        em.op("sc", lambda s, tb=tb, bank=bank, rowa=rowa: s.copy(
                            rowa[:, tb * 512:(tb + 1) * 512], bank[:]),
                            rd=[bbank], wr=[b_mrow[pi]], accum=(tb > 0), tag=f"{name}e{ic}_{tb}")
                        em.op("ve", lambda v, tb=tb, ic=ic, rowa=rowa: v.tensor_reduce(
                            mx0[:, ic * 4 + tb:ic * 4 + tb + 1],
                            rowa[:, tb * 512:(tb + 1) * 512], axis=AX.X,
                            op=ALU.max, apply_absolute_value=True),
                            rd=[b_mrow[pi]], wr=[b_mx], accum=True, tag=f"{name}x{ic}_{tb}")
                    em.op("gp", lambda g, ic=ic, rowa=rowa: g.dma_start(
                        out=dst_dram[ic * P:(ic + 1) * P, :], in_=rowa),
                        rd=[b_mrow[pi]], wr=[d_dst[ic]], dma=True, tag=f"{name}s{ic}")

                for ci in range(8):
                    wprep_chunk(0, wgT, b_wgT, ci, "g")
                ymx_col = mxsl[:, 290:291]
                b_ymxc = TBuf()
                em.op("ve", lambda v: v.tensor_reduce(ymx_col, y1n_mx, axis=AX.X, op=ALU.max),
                      rd=[b_y1nmx], wr=[b_ymxc], tag="ymxr")
                g1col, b_g1 = all_reduce_max(ymx_col, b_ymxc, 1, "b1")
                s1, r1 = bstat[:, 0:1], bstat[:, 1:2]
                em.op("ve", lambda v: v.tensor_scalar(s1, g1col, C32767, 1e-12,
                                                      op0=ALU.mult, op1=ALU.max),
                      rd=[b_g1], wr=[b_bstat], accum=True, tag="s1")
                em.op("ve", lambda v: v.reciprocal(r1, s1), rd=[b_bstat], wr=[b_bstat],
                      accum=True, tag="r1")

                if True:
                    b_yt = mk_bufs(2)
                    b_yq = TBuf()
                    for t_ in range(NT):
                        if t_ + 8 < NIC:
                            wprep_chunk(0, wgT, b_wgT, t_ + 8, "g")
                        pi = t_ % 2
                        ya = ytile[:, pi * D:(pi + 1) * D]
                        em.op("sy", lambda s, t_=t_, ya=ya: s.dma_start(
                            out=ya, in_=y1n_dram[t_ * P:(t_ + 1) * P, :]),
                            rd=[d_y1n[t_]], wr=[b_yt[pi]], dma=True, tag=f"ynr{t_}")
                        em.op("ve", lambda v, ya=ya: v.tensor_scalar(ya, ya, r1, MAGIC,
                                                                     op0=ALU.mult, op1=ALU.add),
                              rd=[b_yt[pi], b_bstat], wr=[b_yt[pi]], tag=f"xq1_{t_}")
                        em.op("ve", lambda v, ya=ya: v.tensor_scalar(yq[:], ya, MAGIC, s1,
                                                                     op0=ALU.subtract, op1=ALU.mult),
                              rd=[b_yt[pi], b_bstat], wr=[b_yq], tag=f"xq2_{t_}")
                        for dd in range(8):
                            em.op("te", lambda t, dd=dd: t.transpose(
                                xq_ps[:, dd * P:(dd + 1) * P], yq[:, dd * P:(dd + 1) * P], ident[:]),
                                rd=[b_yq, b_ident], wr=[b_xqps], accum=(dd > 0), tag=f"xt{t_}_{dd}")
                        for dd in range(8):
                            em.op("sc", lambda s, dd=dd, t_=t_: s.copy(
                                xqT[:, dd * T + t_ * P: dd * T + (t_ + 1) * P],
                                xq_ps[:, dd * P:(dd + 1) * P]),
                                rd=[b_xqps], wr=[b_xqT[dd]], accum=True, tag=f"xe{t_}_{dd}")

                if _LVL[STAGE] <= 1:
                    dump_y1()
                    ph_x.close()
                    ph.close()
                    return
                if _LVL[STAGE] <= 1.3:
                    dump_y1()
                    ph2.close()
                    ph_x.close()
                    ph.close()
                    return
                gate_mx = mxsl[:, 16:112]
                b_gmx = TBuf()
                for ic in range(NIC):
                    mm1_ic(wgT, b_wgT, graw_dram, d_graw, gate_mx, b_gmx, 0, "g", ic)
                if _LVL[STAGE] <= 1.6:
                    dump_y1()
                    ph2.close()
                    ph_x.close()
                    ph.close()
                    return
                gmax_l = mxsl[:, 291:292]
                b_gmaxl = TBuf()
                em.op("ve", lambda v: v.tensor_reduce(gmax_l, gate_mx, axis=AX.X, op=ALU.max),
                      rd=[b_gmx], wr=[b_gmaxl], tag="gmr")
                ggcol, b_gg = all_reduce_max(gmax_l, b_gmaxl, 1, "b2")
                s_g, r_g = bstat[:, 2:3], bstat[:, 3:4]
                em.op("ve", lambda v: v.tensor_scalar(s_g, ggcol, C32767, 1e-12,
                                                      op0=ALU.mult, op1=ALU.max),
                      rd=[b_gg], wr=[b_bstat], accum=True, tag="sg")
                em.op("ve", lambda v: v.reciprocal(r_g, s_g), rd=[b_bstat], wr=[b_bstat],
                      accum=True, tag="rg")
                nmsg = bstat[:, 11:12]
                em.op("ve", lambda v: v.tensor_scalar(nmsg, s_g, -MAGIC, None, op0=ALU.mult),
                      rd=[b_bstat], wr=[b_bstat], accum=True, tag="nmsg")
                ph2.close()
                em.barrier()
                if _LVL[STAGE] <= 2:
                    dump_y1()

                if _LVL[STAGE] <= 2:
                    ph_x.close()
                    ph.close()
                    return
                ph2b = ExitStack()
                wuT = ph2b.enter_context(nc.sbuf_tensor("wuT", [P, 8 * I], BF16))
                cg_in = ph2b.enter_context(nc.sbuf_tensor("cg_in", [P, 2 * E], F32))
                cg_t = ph2b.enter_context(nc.sbuf_tensor("cg_t", [P, E], F32))
                cg_g1 = ph2b.enter_context(nc.sbuf_tensor("cg_g1", [P, E], F32))
                cg_sq = ph2b.enter_context(nc.sbuf_tensor("cg_sq", [P, E], F32))
                cg_out = ph2b.enter_context(nc.sbuf_tensor("cg_out", [P, 2 * E], F32))
                cg_sig = cg_t
                b_wuT = mk_bufs(8)
                b_cgin = mk_bufs(2)
                b_cgt = TBuf()
                b_cgg1, b_cgsq = TBuf(), TBuf()
                b_cgsig = b_cgt
                b_cgout = mk_bufs(2)

                for ci in range(NIC):
                    wprep_chunk(3 * 131072, wuT, b_wuT, ci, "u")

                up_mx = mxsl[:, 112:208]
                b_umx = TBuf()
                gpmx_slots = mxsl[:, 208:232]
                b_gpsl = TBuf()

                def chain_ic(ic):
                    pi = ic % 2
                    ina = cg_in[:, pi * E:(pi + 1) * E]
                    outa = cg_out[:, pi * E:(pi + 1) * E]
                    em.op("sy", lambda s, ic=ic, ina=ina: s.dma_start(
                        out=ina, in_=graw_dram[ic * P:(ic + 1) * P, :]),
                        rd=[d_graw[ic]], wr=[b_cgin[pi]], dma=True, tag=f"cl{ic}")
                    em.op("sc", lambda s, ina=ina: s.activation(cg_t[:], ina, AF.Identity,
                                                                scale=r_g, bias=MAGIC),
                          rd=[b_cgin[pi], b_bstat], wr=[b_cgt],
                          deps=b_consts.rdeps(), tag=f"ct{ic}")
                    em.op("sc", lambda s: s.activation(cg_g1[:], cg_t[:], AF.Identity,
                                                       scale=s_g, bias=nmsg),
                          rd=[b_cgt, b_bstat], wr=[b_cgg1], tag=f"cg{ic}")
                    em.op("sc", lambda s: s.activation(cg_sig[:], cg_g1[:], AF.Sigmoid),
                          rd=[b_cgg1], wr=[b_cgsig], tag=f"cs{ic}")
                    em.op("sc", lambda s: s.activation(cg_sq[:], cg_sig[:], AF.Identity,
                                                       bias=MAGIC, scale=65536.0),
                          rd=[b_cgsig], wr=[b_cgsq], deps=b_consts.rdeps(), tag=f"c2{ic}")
                    em.op("ve", lambda v: v.tensor_scalar(cg_sq[:], cg_sq[:], MAGIC, 65535.0,
                                                          op0=ALU.subtract, op1=ALU.min),
                          rd=[b_cgsq], wr=[b_cgsq], tag=f"cm{ic}")
                    em.op("ve", lambda v, outa=outa: v.tensor_tensor(outa, cg_g1[:], cg_sq[:],
                                                                     ALU.mult),
                          rd=[b_cgg1, b_cgsq], wr=[b_cgout[pi]], tag=f"cp{ic}")
                    em.op("ve", lambda v, ic=ic, outa=outa: v.tensor_reduce(
                        gpmx_slots[:, ic:ic + 1], outa, axis=AX.X, op=ALU.max,
                        apply_absolute_value=True),
                        rd=[b_cgout[pi]], wr=[b_gpsl], accum=True, tag=f"cx{ic}")
                    em.op("gp", lambda g, ic=ic, outa=outa: g.dma_start(
                        out=gpint_dram[ic * P:(ic + 1) * P, :], in_=outa),
                        rd=[b_cgout[pi]], wr=[d_gpint[ic]], dma=True, tag=f"cw{ic}")

                for ic in range(NIC):
                    mm1_ic(wuT, b_wuT, uraw_dram, d_uraw, up_mx, b_umx, 2, "u", ic)
                    chain_ic(ic)

                gp_mx = mxsl[:, 292:293]
                b_gpmx = TBuf()
                em.op("ve", lambda v: v.tensor_reduce(gp_mx, gpmx_slots, axis=AX.X, op=ALU.max),
                      rd=[b_gpsl], wr=[b_gpmx], tag="gpr")
                umax_l = mxsl[:, 293:294]
                b_umaxl = TBuf()
                em.op("ve", lambda v: v.tensor_reduce(umax_l, up_mx, axis=AX.X, op=ALU.max),
                      rd=[b_umx], wr=[b_umaxl], tag="umr")
                both = mxsl[:, 294:296]
                b_both = TBuf()
                em.op("ve", lambda v: v.tensor_copy(both[:, 0:1], gp_mx), rd=[b_gpmx],
                      wr=[b_both], accum=True, tag="bb0")
                em.op("ve", lambda v: v.tensor_copy(both[:, 1:2], umax_l), rd=[b_umaxl],
                      wr=[b_both], accum=True, tag="bb1")
                g2col, b_g2 = all_reduce_max(both, b_both, 2, "b3")
                s_gp, r_gp, rg2 = bstat[:, 4:5], bstat[:, 5:6], bstat[:, 6:7]
                s_up, r_up = bstat[:, 7:8], bstat[:, 8:9]
                em.op("ve", lambda v: v.tensor_scalar(s_gp, g2col[:, 0:1], C32767 * 1.0, 0.0,
                                                      op0=ALU.mult, op1=ALU.max),
                      rd=[b_g2], wr=[b_bstat], accum=True, tag="sgp0")
                em.op("ve", lambda v: v.tensor_scalar(s_gp, s_gp, 2.0 ** -16, 1e-12,
                                                      op0=ALU.mult, op1=ALU.max),
                      rd=[b_bstat], wr=[b_bstat], accum=True, tag="sgp1")
                em.op("ve", lambda v: v.reciprocal(r_gp, s_gp), rd=[b_bstat], wr=[b_bstat],
                      accum=True, tag="rgp")
                em.op("ve", lambda v: v.tensor_scalar(rg2, r_gp, 2.0 ** -16, None, op0=ALU.mult),
                      rd=[b_bstat], wr=[b_bstat], accum=True, tag="rg2")
                em.op("ve", lambda v: v.tensor_scalar(s_up, g2col[:, 1:2], C32767, 1e-12,
                                                      op0=ALU.mult, op1=ALU.max),
                      rd=[b_g2], wr=[b_bstat], accum=True, tag="su")
                em.op("ve", lambda v: v.reciprocal(r_up, s_up), rd=[b_bstat], wr=[b_bstat],
                      accum=True, tag="ru")
                nm_gp = bstat[:, 12:13]
                nm_up = bstat[:, 13:14]
                em.op("ve", lambda v: v.tensor_scalar(nm_gp, s_gp, -MAGIC, None, op0=ALU.mult),
                      rd=[b_bstat], wr=[b_bstat], accum=True, tag="nmgp")
                em.op("ve", lambda v: v.tensor_scalar(nm_up, s_up, -MAGIC, None, op0=ALU.mult),
                      rd=[b_bstat], wr=[b_bstat], accum=True, tag="nmup")
            ph2b.close()
            ph_x.close()
            em.barrier()
            if _LVL[STAGE] <= 3:
                dump_y1()
                ph.close()
                return

            wdT = _A("wdT", [P, 24 * D], BF16)
            wdstg = _A("wdstg", [P, I], BF16)
            b_wdT = mk_bufs(24)
            b_wdstg = TBuf()
            for hc in range(8):
                src_ap = wsh_out[:, 6 * 131072 + hc * 49152:
                                 6 * 131072 + (hc + 1) * 49152]\
                    .rearrange("r (p c) -> p r c", p=P)
                em.op("sy", lambda s, hc=hc, src_ap=src_ap: s.dma_start(
                    out=wdstg[:].rearrange("p (r c) -> p r c", r=8), in_=src_ap),
                    rd=[b_ag], wr=[b_wdstg], dma=True, tag=f"wdl{hc}")
                for g in range(3):
                    for j in range(8):
                        ic = g * 8 + j
                        em.op("te", lambda t, ic=ic, j=j: t.transpose(
                            xq_ps[:, j * P:(j + 1) * P], wdstg[:, ic * P:(ic + 1) * P],
                            ident[:]),
                            rd=[b_wdstg, b_ident], wr=[b_xqps], accum=(j > 0),
                            tag=f"wdt{hc}_{ic}")
                    for j in range(8):
                        ic = g * 8 + j
                        em.op("sc", lambda s, ic=ic, j=j, hc=hc: s.copy(
                            wdT[:, ic * D + hc * P: ic * D + (hc + 1) * P],
                            xq_ps[:, j * P:(j + 1) * P]),
                            rd=[b_xqps], wr=[b_wdT[ic]], accum=True, tag=f"wde{hc}_{ic}")


            ph3 = ExitStack()
            if True:
                e2_g = ph3.enter_context(nc.sbuf_tensor("e2_g", [P, 2 * E], F32))
                e2_u = ph3.enter_context(nc.sbuf_tensor("e2_u", [P, 2 * E], F32))
                e2_tg = ph3.enter_context(nc.sbuf_tensor("e2_tg", [P, E], F32))
                e2_tu = ph3.enter_context(nc.sbuf_tensor("e2_tu", [P, E], F32))
                e2_o = ph3.enter_context(nc.sbuf_tensor("e2_o", [P, 2 * E], F32))
                b_e2g = mk_bufs(2)
                b_e2u = mk_bufs(2)
                b_e2tg, b_e2tu = TBuf(), TBuf()
                b_e2o = mk_bufs(2)
                oint_slots = mxsl[:, 232:256]
                b_osl = TBuf()
                s_gp, rg2 = bstat[:, 4:5], bstat[:, 6:7]
                s_up, r_up = bstat[:, 7:8], bstat[:, 8:9]
                nm_gp, nm_up = bstat[:, 12:13], bstat[:, 13:14]
                for ic in range(NIC):
                    pi = ic % 2
                    ga = e2_g[:, pi * E:(pi + 1) * E]
                    ua = e2_u[:, pi * E:(pi + 1) * E]
                    oa = e2_o[:, pi * E:(pi + 1) * E]
                    em.op("sy", lambda s, ic=ic, ga=ga: s.dma_start(
                        out=ga, in_=gpint_dram[ic * P:(ic + 1) * P, :]),
                        rd=[d_gpint[ic]], wr=[b_e2g[pi]], dma=True, tag=f"e2g{ic}")
                    em.op("sy", lambda s, ic=ic, ua=ua: s.dma_start(
                        out=ua, in_=uraw_dram[ic * P:(ic + 1) * P, :]),
                        rd=[d_uraw[ic]], wr=[b_e2u[pi]], dma=True, tag=f"e2u{ic}")
                    em.op("sc", lambda s, ga=ga: s.activation(e2_tg[:], ga, AF.Identity,
                                                              scale=rg2, bias=MAGIC),
                          rd=[b_e2g[pi], b_bstat], wr=[b_e2tg],
                          deps=b_consts.rdeps(), tag=f"etg{ic}")
                    em.op("sc", lambda s: s.activation(e2_tg[:], e2_tg[:], AF.Identity,
                                                       scale=s_gp, bias=nm_gp),
                          rd=[b_e2tg, b_bstat], wr=[b_e2tg], tag=f"eg2{ic}")
                    em.op("sc", lambda s, ua=ua: s.activation(e2_tu[:], ua, AF.Identity,
                                                              scale=r_up, bias=MAGIC),
                          rd=[b_e2u[pi], b_bstat], wr=[b_e2tu], tag=f"etu{ic}")
                    em.op("ve", lambda v: v.tensor_scalar(e2_tu[:], e2_tu[:], MAGIC, s_up,
                                                          op0=ALU.subtract, op1=ALU.mult),
                          rd=[b_e2tu, b_bstat], wr=[b_e2tu], tag=f"eu2{ic}")
                    em.op("ve", lambda v, oa=oa: v.tensor_tensor(oa, e2_tg[:], e2_tu[:], ALU.mult),
                          rd=[b_e2tg, b_e2tu], wr=[b_e2o[pi]], tag=f"eo{ic}")
                    em.op("ve", lambda v, ic=ic, oa=oa: v.tensor_reduce(
                        oint_slots[:, ic:ic + 1], oa, axis=AX.X, op=ALU.max,
                        apply_absolute_value=True),
                        rd=[b_e2o[pi]], wr=[b_osl], accum=True, tag=f"ex{ic}")
                    em.op("gp", lambda g, ic=ic, oa=oa: g.dma_start(
                        out=oint_dram[ic * P:(ic + 1) * P, :], in_=oa),
                        rd=[b_e2o[pi]], wr=[d_oint[ic]], dma=True, tag=f"es{ic}")
                omax_l = mxsl[:, 296:297]
                b_omaxl = TBuf()
                em.op("ve", lambda v: v.tensor_reduce(omax_l, oint_slots, axis=AX.X, op=ALU.max),
                      rd=[b_osl], wr=[b_omaxl], tag="omr")
                gocol, b_go = all_reduce_max(omax_l, b_omaxl, 1, "b4")
                s_o, r_o = bstat[:, 9:10], bstat[:, 10:11]
                em.op("ve", lambda v: v.tensor_scalar(s_o, gocol, C32767, 1e-12,
                                                      op0=ALU.mult, op1=ALU.max),
                      rd=[b_go], wr=[b_bstat], accum=True, tag="so")
                em.op("ve", lambda v: v.reciprocal(r_o, s_o), rd=[b_bstat], wr=[b_bstat],
                      accum=True, tag="ro")
            ph3.close()
            em.barrier()
            if _LVL[STAGE] <= 4:
                dump_y1()
                ph.close()
                return

            if True:
                dn_ps0 = fm_ps[4]
                dn_ps1 = fm_ps[5]
                obf = _A("obf", [P, 24 * T], BF16)
                e3_in = _A("e3_in", [P, 2 * E], F32)
                e3_t = _A("e3_t", [P, E], F32)
                f_out = _A("f_out", [P, D], F32)
                f_t = _A("f_t", [P, D], F32)
                f_y1 = _A("f_y1", [P, D], F32)
                stats_b = _A("stats_b", [P, 300], F32)
                stats_holder[0] = stats_b
                sc_slot[0] = 0
                dn_ps = [dn_ps0, dn_ps1]
                b_dnps = mk_bufs(2)
                b_obf = mk_bufs(24)
                b_e3in = mk_bufs(2)
                b_e3t = TBuf()
                b_fout, b_ft, b_fy1 = TBuf(), TBuf(), TBuf()
                s_o, r_o = bstat[:, 9:10], bstat[:, 10:11]

                for ic in range(NIC):
                    pi = ic % 2
                    ina = e3_in[:, pi * E:(pi + 1) * E]
                    em.op("sy", lambda s, ic=ic, ina=ina: s.dma_start(
                        out=ina, in_=oint_dram[ic * P:(ic + 1) * P, :]),
                        rd=[d_oint[ic]], wr=[b_e3in[pi]], dma=True, tag=f"e3l{ic}")
                    em.op("ve", lambda v, ina=ina: v.tensor_scalar(e3_t[:], ina, r_o, MAGIC,
                                                                   op0=ALU.mult, op1=ALU.add),
                          rd=[b_e3in[pi], b_bstat], wr=[b_e3t], tag=f"e3t{ic}")
                    em.op("ve", lambda v, ic=ic: v.tensor_scalar(
                        obf[:, ic * T:(ic + 1) * T], e3_t[:], MAGIC, None, op0=ALU.subtract),
                        rd=[b_e3t], wr=[b_obf[ic]], tag=f"e3q{ic}")

                for t_ in range(NT):
                    for ic in range(24):
                        for half in range(2):
                            em.op("te", lambda t, ic=ic, t_=t_, half=half: t.matmul(
                                dn_ps[half][:], obf[:, ic * T + t_ * P: ic * T + (t_ + 1) * P],
                                wdT[:, ic * D + half * 512: ic * D + half * 512 + 512],
                                start=(ic == 0), stop=(ic == 23)),
                                rd=[b_obf[ic], b_wdT[ic]], wr=[b_dnps[half]], accum=(ic > 0),
                                tag=f"dn{t_}_{half}_{ic}")
                    for half in range(2):
                        em.op("ve", lambda v, half=half: v.tensor_scalar(
                            f_out[:, half * 512:(half + 1) * 512], dn_ps[half][:], s_o, None,
                            op0=ALU.mult),
                            rd=[b_dnps[half], b_bstat], wr=[b_fout], accum=(half > 0),
                            tag=f"fo{t_}_{half}")
                    layernorm(f_out[:], b_fout, f_t[:], b_ft, f"l5_{t_}")
                    em.op("sy", lambda s, t_=t_: s.dma_start(
                        out=f_y1[:], in_=y1_dram[t_ * P:(t_ + 1) * P, :]),
                        rd=[d_y1[t_]], wr=[b_fy1], dma=True, tag=f"fyl{t_}")
                    em.op("ve", lambda v: v.tensor_tensor(f_t[:], f_t[:], f_y1[:], ALU.add),
                          rd=[b_ft, b_fy1], wr=[b_ft], tag=f"y2_{t_}")
                    tok = em.op("gp", lambda g, t_=t_: g.dma_start(
                        out=y_out[t_ * P:(t_ + 1) * P, :], in_=f_t[:]),
                        rd=[b_ft], dma=True, tag=f"ys{t_}")
                    em.force_inc(tok)
        ph.close()

    if STAGE != "a":
        phase_b()

    st.close()

    em.finalize()
    from contextlib import ExitStack as _ES
    semstack = _ES()
    sems = {}
    for name in em.sem_names():
        sems[name] = semstack.enter_context(nc.semaphore("s_" + name))
    with nc.Block() as block:
        @block.sync
        def _(sync):
            em.replay("sy", sync, sems)

        @block.scalar
        def _(scalar):
            em.replay("sc", scalar, sems)

        @block.vector
        def _(vector):
            em.replay("ve", vector, sems)

        @block.tensor
        def _(tensor):
            em.replay("te", tensor, sems)

        @block.gpsimd
        def _(gpsimd):
            em.replay("gp", gpsimd, sems)
            for s in range(Em.NSLOT):
                name = f"gp_d{s}"
                if em.final_vals.get(name):
                    gpsimd.wait_ge(sems[name], em.final_vals[name])
    semstack.close()

    return nc


_NC_CACHE = {}


def kernel(**inputs):
    if "prog" not in _NC_CACHE:
        _NC_CACHE["prog"] = build_program()
    nc = _NC_CACHE["prog"]

    xf = np.ascontiguousarray(np.asarray(inputs["L0_attn_x"], dtype=np.float32))
    shared = {
        "ln_g": np.ascontiguousarray(np.asarray(inputs["L0_attn_ln_g"], np.float32)),
        "ln_b": np.ascontiguousarray(np.asarray(inputs["L0_attn_ln_b"], np.float32)),
        "W_in": np.ascontiguousarray(np.asarray(inputs["L0_attn_W_in"], np.float32)),
        "b_in": np.ascontiguousarray(np.asarray(inputs["L0_attn_b_in"], np.float32)),
        "W_out": np.ascontiguousarray(np.asarray(inputs["L0_attn_W_out"], np.float32)),
        "b_out": np.ascontiguousarray(np.asarray(inputs["L0_attn_b_out"], np.float32)),
        "g_q": np.ascontiguousarray(np.asarray(inputs["L0_attn_gamma_q"], np.float32)),
        "be_q": np.ascontiguousarray(np.asarray(inputs["L0_attn_beta_q"], np.float32)),
        "g_k": np.ascontiguousarray(np.asarray(inputs["L0_attn_gamma_k"], np.float32)),
        "be_k": np.ascontiguousarray(np.asarray(inputs["L0_attn_beta_k"], np.float32)),
    }
    wg = np.asarray(inputs["L0_ffn_w_gate"], np.float32)
    wu = np.asarray(inputs["L0_ffn_w_up"], np.float32)
    wd = np.asarray(inputs["L0_ffn_w_down"], np.float32)
    in_maps = []
    for c in range(8):
        b, h = c // 2, c % 2
        xb = xf[b]
        xr = np.concatenate([xb[h * T:(h + 1) * T], xb[(1 - h) * T:(2 - h) * T]], axis=0)
        m = {"x": np.ascontiguousarray(xr),
             "wg_sh": np.ascontiguousarray(wg[c * 384:(c + 1) * 384]),
             "wu_sh": np.ascontiguousarray(wu[c * 384:(c + 1) * 384]),
             "wd_sh": np.ascontiguousarray(wd[:, c * 384:(c + 1) * 384])}
        m.update(shared)
        in_maps.append(m)

    res = run_bass_kernel_spmd(nc, in_maps, core_ids=list(range(8)))
    out = np.empty((4, L, D), dtype=np.float32)
    for c in range(8):
        b, h = c // 2, c % 2
        out[b, h * T:(h + 1) * T] = res.results[c]["y"]
    return out
